# version 20
# baseline (speedup 1.0000x reference)
import numpy as np
import ml_dtypes
import concourse.bass as bass
import concourse.mybir as mybir
from concourse.bass_utils import run_bass_kernel_spmd

F32 = mybir.dt.float32
BF16 = mybir.dt.bfloat16
U32 = mybir.dt.uint32
AF = mybir.ActivationFunctionType
ALU = mybir.AluOpType
AX = mybir.AxisListType
IOff = bass.IndirectOffsetOnAxis

T = 2048
D = 1024
H = 16
HD = 64
HID = 2048
NE = 8
CAPG = 4096
CAP = 640
CT = CAP // 128
NTT = T // 128
NFT = D // 128
NHT = HID // 128
AUGW = D + 8
NROWS = T + CAP
SEARCH_ITERS = 16
ACT_CONSTS = (-4.0, 0.125, -1.0, -0.5, 1e-5)
ENAMES = ("t", "v", "s", "g", "q_s", "q_a", "q_g")


class Ledger:

    NQSEM = 20

    def __init__(self, semnames=None):
        self.progs = {e: [] for e in ENAMES}
        self.seq = 0
        self.semnames = (["t", "v", "s", "g", "cc"] +
                         [f"{q}{j}" for q in ("qs", "qa", "qg")
                          for j in range(self.NQSEM)])
        self.cnt = {k: 0 for k in self.semnames}
        self.dmacnt = {"qs": 0, "qa": 0, "qg": 0}
        self.clocks = {e: {} for e in ENAMES}
        self.snap = {}
        self.last_w = {}
        self.readers = {}
        self.eng_sem = {"t": "t", "v": "v", "s": "s", "g": "g",
                        "q_s": "qs", "q_a": "qa", "q_g": "qg"}
        self.sem_step = {n: (16 if n[:2] in ("qs", "qa", "qg") else 1)
                         for n in self.semnames}

    def _append(self, ename, entry):
        self.progs[ename].append(entry + (self.seq,))
        self.seq += 1

    def _wait(self, ename, semname, val):
        cl = self.clocks[ename]
        if cl.get(semname, 0) >= val:
            return
        self._append(ename, ("wait", semname, val))
        snap = self.snap.get((semname, val), {semname: val})
        for k, v in snap.items():
            if cl.get(k, 0) < v:
                cl[k] = v

    def _needed(self, reads, writes):
        need = {}
        def add(dep):
            if dep is None:
                return
            s, v = dep
            if need.get(s, 0) < v:
                need[s] = v
        for k in reads:
            add(self.last_w.get(k))
        for k in writes:
            add(self.last_w.get(k))
            for dep in self.readers.get(k, ()):
                add(dep)
        return need

    def _bump(self, ename, semname):
        step = self.sem_step[semname]
        self.cnt[semname] += step
        val = self.cnt[semname]
        self.snap[(semname, val)] = dict(self.clocks[ename]) | {semname: val}
        return (semname, val)

    def op(self, ename, fn, reads=(), writes=(), sem=None):
        for s, v in sorted(self._needed(reads, writes).items()):
            self._wait(ename, s, v)
        semname = sem if sem is not None else self.eng_sem[ename]
        if semname in self.dmacnt:
            idx = self.dmacnt[semname]
            self.dmacnt[semname] += 1
            q = semname
            semname = f"{q}{idx % self.NQSEM}"
            if idx >= self.NQSEM:
                self._wait(ename, semname, 16 * (idx // self.NQSEM))
        dep = self._bump(ename, semname)
        self._append(ename, ("inst", fn, semname, self.sem_step[semname]))
        for k in writes:
            self.last_w[k] = dep
            self.readers[k] = []
        for k in reads:
            self.readers.setdefault(k, []).append(dep)
        return dep

    def group(self, ename, fns_reads, writes=(), sem=None):
        semname = sem if sem is not None else self.eng_sem[ename]
        n = len(fns_reads)
        all_reads = set()
        for i, (fn, reads) in enumerate(fns_reads):
            for s, v in sorted(self._needed(reads, writes if i == 0 else ()).items()):
                self._wait(ename, s, v)
            if i < n - 1:
                self._append(ename, ("inst", fn, None, 0))
            else:
                dep = self._bump(ename, semname)
                self._append(ename, ("inst", fn, semname, self.sem_step[semname]))
            all_reads.update(reads)
        for k in writes:
            self.last_w[k] = dep
            self.readers[k] = []
        for k in all_reads:
            self.readers.setdefault(k, []).append(dep)

    def barrier(self, ename="g", skip=()):
        for s in sorted(self.cnt):
            if self.cnt[s] > 0 and not any(s.startswith(p) for p in skip):
                self._wait(ename, s, self.cnt[s])
        dep = self._bump(ename, self.eng_sem[ename])
        self._append(ename, ("inst", lambda e: e.nop(), self.eng_sem[ename], 1))
        for other in ENAMES:
            if other != ename:
                self._wait(other, dep[0], dep[1])


def build_program(taps=(), stop_after=None):
    taps = set(taps)
    nc = bass.Bass("TRN2", target_bir_lowering=False, debug=False)

    xT = nc.dram_tensor("xT", [D, T], F32, kind="ExternalInput").ap()
    x_tm = nc.dram_tensor("x_tm", [T, D], F32, kind="ExternalInput").ap()
    wqT = nc.dram_tensor("wqT", [NFT, NFT, 128, 128], F32, kind="ExternalInput").ap()
    wkT = nc.dram_tensor("wkT", [NFT, NFT, 128, 128], F32, kind="ExternalInput").ap()
    wvT = nc.dram_tensor("wvT", [NFT, NFT, 128, 128], F32, kind="ExternalInput").ap()
    woT = nc.dram_tensor("woT", [D, D], F32, kind="ExternalInput").ap()
    bo_r = nc.dram_tensor("bo_r", [1, D], F32, kind="ExternalInput").ap()
    wg = nc.dram_tensor("wg", [D, NE], F32, kind="ExternalInput").ap()
    w1pm = nc.dram_tensor("w1pm", [NE * 4 * 128, 4096], F32, kind="ExternalInput").ap()
    w3pm = nc.dram_tensor("w3pm", [NE * 4 * 128, 4096], F32, kind="ExternalInput").ap()
    w2pm = nc.dram_tensor("w2pm", [NE * 128, 16384], F32, kind="ExternalInput").ap()
    esel = nc.dram_tensor("esel", [128, 1], U32, kind="ExternalInput").ap()
    eye_bf = nc.dram_tensor("eye_bf", [128, 128], BF16, kind="ExternalInput").ap()
    eye_f = nc.dram_tensor("eye_f", [128, 128], F32, kind="ExternalInput").ap()
    sut_f = nc.dram_tensor("sut_f", [128, 128], F32, kind="ExternalInput").ap()
    onec_f = nc.dram_tensor("onec_f", [128, 1], F32, kind="ExternalInput").ap()
    oner_f = nc.dram_tensor("oner_f", [1, 128], F32, kind="ExternalInput").ap()
    oner_b = nc.dram_tensor("oner_b", [1, 128], BF16, kind="ExternalInput").ap()
    utk_b = nc.dram_tensor("utk_b", [128, 128], BF16, kind="ExternalInput").ap()
    actc_f = nc.dram_tensor("actc_f", [128, 8], F32, kind="ExternalInput").ap()

    out_tm = nc.dram_tensor("out_tm", [T, D], F32, kind="ExternalOutput").ap()
    tap_aps = {}
    def tap(name, shape, dtype=F32):
        if name in taps:
            tap_aps[name] = nc.dram_tensor("tap_" + name, shape, dtype,
                                           kind="ExternalOutput").ap()
        return tap_aps.get(name)

    x2_tap = tap("x2", [T, D])
    sc_tap = tap("scores", [T, NE])
    thr_tap = tap("thr", [1, NE])
    pos_tap = tap("pos", [128, NE * NTT], U32)
    il_tap = tap("ilist", [NE * CAP, 1], U32)
    y2_tap = tap("y2", [T, D])

    x2aug_d = nc.dram_tensor("x2aug_d", [NROWS, AUGW], BF16, kind="Internal").ap()
    y2acc_d = nc.dram_tensor("y2acc_d", [NROWS, D], F32, kind="Internal").ap()
    ilist_d = nc.dram_tensor("ilist_d", [NE * CAP, 1], U32, kind="Internal").ap()
    cc1_in = nc.dram_tensor("cc1_in", [NE, T], F32, kind="Internal").ap()
    cc1_out = nc.dram_tensor("cc1_out", [8 * NE, T], F32, kind="Internal").ap()
    cc2_in = nc.dram_tensor("cc2_in", [1, 1], F32, kind="Internal").ap()
    cc2_out = nc.dram_tensor("cc2_out", [8, 1], F32, kind="Internal").ap()
    ts_d = nc.dram_tensor("ts_d", [1, 128], F32, kind="Internal").ap()
    wob_d = nc.dram_tensor("wob_d", [D, D], BF16, kind="Internal").ap()
    w1b_d = nc.dram_tensor("w1b_d", [NE * 4 * 128, 4096], BF16, kind="Internal").ap()
    w3b_d = nc.dram_tensor("w3b_d", [NE * 4 * 128, 4096], BF16, kind="Internal").ap()
    w2b_d = nc.dram_tensor("w2b_d", [NE * 128, 16384], BF16, kind="Internal").ap()
    off_d = nc.dram_tensor("off_d", [1, 128], F32, kind="Internal").ap()

    REGS = {}
    import contextlib
    stack = contextlib.ExitStack()
    stkX2 = contextlib.ExitStack()
    stkCT = contextlib.ExitStack()
    stkA = contextlib.ExitStack()
    stkB = contextlib.ExitStack()
    stkC = contextlib.ExitStack()
    stkD = contextlib.ExitStack()
    stkE = contextlib.ExitStack()

    def mk(stk):
        return (lambda name, shape, dtype: stk.enter_context(nc.sbuf_tensor(name, shape, dtype)))
    sb = mk(stack)
    ps = lambda name, shape, dtype: stack.enter_context(nc.psum_tensor(name, shape, dtype))

    stkPS1 = contextlib.ExitStack()
    with stack:
        ps1 = lambda name, shape, dtype: stkPS1.enter_context(nc.psum_tensor(name, shape, dtype))
        PSF = [ps1(f"psf{i}", [128, 512], F32) for i in range(6)]
        PSB = [ps1(f"psb{i}", [128, 512], BF16) for i in range(2)]
        BK = [f"PSF{i}" for i in range(6)] + ["PSB0", "PSB1"]

        EYEB = sb("eyeb", [128, 128], BF16)
        EYEF = sb("eyef", [128, 128], F32)
        SUT = sb("sut", [128, 128], F32)
        ONEC = sb("onec", [128, 1], F32)
        ONER = sb("oner", [1, 128], F32)
        ONERB = sb("onerb", [1, 128], BF16)
        UTK = sb("utk", [128, 128], BF16)
        ESEL = sb("eselS", [128, 1], U32)
        BO = sb("boS", [1, D], F32)
        WGS = sb("wgs", [128, NE * NFT], F32)
        IDS = sb("ids", [128, NTT], U32)
        ONESQ = sb("onesq", [128, 128], F32)
        ACTC = sb("actc", [128, 8], F32)
        for j, v in enumerate(ACT_CONSTS):
            nc.const_aps.register(F32, v, ACTC[:, j:j + 1])
        INIT5 = sb("init5", [128, CT], U32)
        BIGL = sb("bigl", [128, 1], F32)
        JUNKB = [sb(f"junkb{j}", [128, D], BF16) for j in range(2)]
        JUNKF = sb("junkf", [128, 128], F32)
        ST1 = [sb(f"st1_{j}", [128, 1], F32) for j in range(2)]
        ST2 = [sb(f"st2_{j}", [128, 1], F32) for j in range(2)]
        MU = [sb(f"mu{j}", [128, 1], F32) for j in range(2)]
        VAR = [sb(f"var{j}", [128, 1], F32) for j in range(2)]
        RSTD = [sb(f"rstd{j}", [128, 1], F32) for j in range(2)]
        NMR = [sb(f"nmr{j}", [128, 1], F32) for j in range(2)]
        TM1 = [sb(f"tm1_{j}", [128, 1], F32) for j in range(2)]
        POSW = sb("posw", [128, 128], U32)

        sba2 = mk(stkCT); stkCT.__enter__()
        QTB = [sba2(f"qtb{i}", [128, T], BF16) for i in range(NFT)]
        KTB = [sba2(f"ktb{i}", [128, T], BF16) for i in range(NFT)]
        VTM = [sba2(f"vtm{i}", [128, H * (HD + 1)], BF16) for i in range(NTT)]
        PT = [[sba2(f"pt{l}_{i}", [128, 512], BF16) for i in range(2)] for l in range(2)]
        CTXF = [sba2(f"ctxf{i}", [65, 512], F32) for i in range(2)]
        LROW = [sba2(f"lrow{i}", [1, 512], F32) for i in range(2)]
        RLR = [sba2(f"rlr{i}", [1, 512], F32) for i in range(2)]
        ZROA = sba2("zroa", [128, AUGW], BF16)
        ZROY = sba2("zroy", [128, D], F32)

        sba = mk(stkA); stkA.__enter__()
        XTB = [sba(f"xtb{i}", [128, T], BF16) for i in range(NFT)]
        WSTG = [sba(f"wstg{i}", [128, 1024], BF16) for i in range(2)]
        VTT = sba("vtt", [128, T], BF16)
        CTXT = [stack.enter_context(nc.sbuf_tensor(f"ctxt{i}", [128, T], BF16, side="right"))
                for i in range(NFT)]

        L = Ledger(("t", "v", "s", "g", "qs", "qa", "qg", "cc"))
        E = L.op
        G = L.group

        def ln_block(src_tile, src_keys, out_tile, out_key, par=0):
            st1, st2, mu, var, rstd, nmr, tm1, jb = (ST1[par], ST2[par], MU[par],
                VAR[par], RSTD[par], NMR[par], TM1[par], JUNKB[par])
            k1, k2, kmu, kva, kvr, krl, krs, knm, ktm, kjb = [(n, par) for n in
                ("st1", "st2", "mu", "var_a", "var", "rstd_l", "rstd", "nmr", "tm1", "junkb")]
            E("v", lambda e: e.reduce_sum(st1[:, :], src_tile[:, :], AX.X),
              reads=list(src_keys) + [k1], writes=[k1])
            E("s", lambda e: e.activation(out=jb[:, :], in_=src_tile[:, :],
                                          func=AF.Square, accum_out=st2[:, :]),
              reads=list(src_keys) + [k2, kjb], writes=[k2, kjb])
            E("v", lambda e: e.tensor_scalar_mul(mu[:, :], st1[:, :], 1.0 / D),
              reads=[k1, kmu], writes=[kmu])
            E("v", lambda e: e.tensor_scalar_mul(var[:, :], st2[:, :], 1.0 / D),
              reads=[k2, kva], writes=[kva])
            E("v", lambda e: e.tensor_tensor(tm1[:, :], mu[:, :], mu[:, :], ALU.mult),
              reads=[kmu, ktm], writes=[ktm])
            E("v", lambda e: e.tensor_tensor(var[:, :], var[:, :], tm1[:, :], ALU.subtract),
              reads=[kva, ktm], writes=[kva])
            E("s", lambda e: e.activation(out=rstd[:, :], in_=var[:, :], func=AF.Ln, bias=1e-5),
              reads=[kva, krl], writes=[krl])
            E("s", lambda e: e.activation(out=rstd[:, :], in_=rstd[:, :], func=AF.Exp, scale=-0.5),
              reads=[krl, krs], writes=[krs])
            E("v", lambda e: e.tensor_tensor(nmr[:, :], mu[:, :], rstd[:, :], ALU.mult),
              reads=[kmu, krs, knm], writes=[knm])
            E("v", lambda e: e.tensor_scalar_mul(nmr[:, :], nmr[:, :], -1.0),
              reads=[knm], writes=[knm])
            E("v", lambda e: e.tensor_scalar(out_tile[:, :], src_tile[:, :],
                                             rstd[:, :], nmr[:, :], ALU.mult, ALU.add),
              reads=list(src_keys) + [krs, knm], writes=[out_key])

        for dst, src, key in [
            (EYEB, eye_bf, "eye_bf"), (EYEF, eye_f, "eye_f"),
            (SUT, sut_f, "sut"), (ONEC, onec_f, "onec"),
            (ONER, oner_f, "oner"), (ONERB, oner_b, "onerb"),
            (UTK, utk_b, "utk"), (ESEL, esel, "eselS"), (BO, bo_r, "bo_r"),
            (ACTC, actc_f, "actc"),
        ]:
            E("q_s", lambda e, d=dst, s=src: e.dma_start(out=d[:, :], in_=s[:, :]),
              writes=[key], sem="qs")
        for f in range(NFT):
            E("q_s", lambda e, f=f: e.dma_start(out=WGS[:, f * NE:(f + 1) * NE],
                                                in_=wg[f * 128:(f + 1) * 128, :]),
              writes=[("wgs", f)], sem="qs")
        E("g", lambda e: e.iota(IDS[:, :], pattern=[[128, NTT]], base=0, channel_multiplier=1),
          writes=["ids"])
        E("g", lambda e: e.iota(INIT5[:, :], pattern=[[1, CT]], base=T, channel_multiplier=CT),
          writes=["init5"])
        E("v", lambda e: e.tensor_scalar(BIGL[:, :], IDS[:, 0:1], 1000000.0, None, ALU.add),
          reads=["ids"], writes=["bigl"])
        E("v", lambda e: e.memset(ONESQ[:, :], 1.0), writes=["onesq"])
        for i in range(NTT):
            E("v", lambda e, i=i: e.memset(VTM[i][:, :], 1.0), writes=[("vtm", i)])
        E("v", lambda e: e.memset(ZROA[:, :], 0.0), writes=["zroa"])
        E("v", lambda e: e.memset(ZROY[:, :], 0.0), writes=["zroy"])
        for r in range(CT):
            E("q_s", lambda e, r=r: e.dma_start(out=x2aug_d[T + r * 128: T + (r + 1) * 128, :],
                                                in_=ZROA[:, :]),
              reads=["zroa"], writes=[("x2aug_pad", r)], sem="qs")
            E("q_s", lambda e, r=r: e.dma_start(out=y2acc_d[T + r * 128: T + (r + 1) * 128, :],
                                                in_=ZROY[:, :]),
              reads=["zroy"], writes=[("y2acc_pad", r)], sem="qs")

        L.barrier()

        for i in range(NFT):
            E("q_g", lambda e, i=i: e.dma_start(out=XTB[i][:, :],
                                                in_=xT[i * 128:(i + 1) * 128, :]),
              writes=[("xtb", i)], sem="qg")

        def proj(wsrc, dstf, dkey, post=None):
            for m in range(NFT):
                b = m % 2
                E("q_g", lambda e, m=m, b=b, w=wsrc: e.dma_start(
                    out=WSTG[b][:, :].rearrange("p (k f) -> p k f", f=128),
                    in_=w[m].rearrange("k p f -> p k f")),
                  reads=[("wstg", b)], writes=[("wstg", b)], sem="qg")
                for ch in range(4):
                    pb = (m * 4 + ch) % 2
                    bank = BK[pb]
                    fns = []
                    for k in range(NFT):
                        r = [("wstg", b), ("xtb", k)]
                        if k == 0:
                            r.append(bank)
                        fns.append((
                            (lambda e, b=b, ch=ch, k=k, pb=pb: e.matmul(
                                PSF[pb][:, :], WSTG[b][:, k * 128:(k + 1) * 128],
                                XTB[k][:, ch * 512:(ch + 1) * 512],
                                start=(k == 0), stop=(k == NFT - 1))),
                            r))
                    G("t", fns, writes=[bank])
                    E("v", lambda e, ch=ch, pb=pb, dst=dstf(m): e.tensor_copy(
                        dst[:, ch * 512:(ch + 1) * 512], PSF[pb][:, :]),
                      reads=[bank], writes=[bank, (dkey, m, ch)])
                if post is not None:
                    post(m)

        proj(wqT, lambda m: QTB[m], "q")
        proj(wkT, lambda m: KTB[m], "k")

        def vpost(m):
            for i in range(NTT):
                tb = i % 2
                bank = BK[6 + tb]
                E("t", lambda e, m=m, i=i, tb=tb: e.transpose(
                    PSB[tb][:, 0:128], VTT[:, i * 128:(i + 1) * 128], EYEB[:, :]),
                  reads=[("vt", m, i // 4), "eye_bf", bank], writes=[bank])
                for hh2 in range(2):
                    h = 2 * m + hh2
                    E("v", lambda e, i=i, tb=tb, hh2=hh2, h=h: e.tensor_copy(
                        VTM[i][:, h * (HD + 1): h * (HD + 1) + HD],
                        PSB[tb][:, hh2 * 64: hh2 * 64 + 64]),
                      reads=[bank, ("vtm", i)], writes=[bank, ("vtm", i)])

        proj(wvT, lambda m: VTT, "vt", post=vpost)
        L.barrier()
        stkA.close()

        E("q_g", lambda g: g.dma_start(out=wob_d[:, :], in_=woT[:, :]),
          writes=["wob_d"], sem="qg")
        E("q_g", lambda g: g.dma_start(out=w1b_d[:, :], in_=w1pm[:, :]),
          writes=["w1b"], sem="qg")
        E("q_g", lambda g: g.dma_start(out=w3b_d[:, :], in_=w3pm[:, :]),
          writes=["w3b"], sem="qg")
        E("q_g", lambda g: g.dma_start(out=w2b_d[:, :], in_=w2pm[:, :]),
          writes=["w2b"], sem="qg")

        if stop_after not in ("qkv",):
            for hp in range(NFT):
                for c in range(4):
                    cbank = [BK[4], BK[5]]
                    nj = 4 * c + 4

                    def emit_scores(j):
                        d = j - 4 * c
                        q0 = c * 512 + max(0, d) * 128
                        w = (c + 1) * 512 - q0
                        for lane in range(2):
                            sp = PSF[2 * lane + j % 2]
                            sbank = BK[2 * lane + j % 2]
                            rows = slice(lane * 64, lane * 64 + 64)
                            E("t", lambda e, hp=hp, j=j, rows=rows, q0=q0, w=w, sp=sp: e.matmul(
                                sp[:, 0:w], KTB[hp][rows, j * 128:(j + 1) * 128],
                                QTB[hp][rows, q0:q0 + w], start=True, stop=True),
                              reads=[sbank], writes=[sbank])

                    emit_scores(0)
                    for j in range(nj):
                        d = j - 4 * c
                        q0 = c * 512 + max(0, d) * 128
                        w = (c + 1) * 512 - q0
                        for lane in range(2):
                            sp = PSF[2 * lane + j % 2]
                            sbank = BK[2 * lane + j % 2]
                            pt = PT[lane][j % 2]
                            E("s", lambda e, w=w, sp=sp, pt=pt: e.activation(
                                out=pt[:, 0:w], in_=sp[:, 0:w], func=AF.Exp,
                                bias=-4.0, scale=0.125),
                              reads=[sbank, ("pt", lane, j % 2)],
                              writes=[sbank, ("pt", lane, j % 2)])
                            if d >= 0:
                                E("v", lambda e, pt=pt: e.tensor_tensor(
                                    pt[:, 0:128], pt[:, 0:128], UTK[:, :], ALU.mult),
                                  reads=[("pt", lane, j % 2), "utk"],
                                  writes=[("pt", lane, j % 2)])
                        if j + 1 < nj:
                            emit_scores(j + 1)
                        for lane in range(2):
                            h = 2 * hp + lane
                            pt = PT[lane][j % 2]
                            cp = PSF[4 + lane]
                            r = [("pt", lane, j % 2)]
                            wr = ()
                            if j == 0:
                                r.append(cbank[lane])
                            if j == nj - 1:
                                wr = [cbank[lane]]
                            E("t", lambda e, j=j, h=h, q0=q0, w=w, c=c, nj=nj, pt=pt, cp=cp: e.matmul(
                                cp[0:65, (q0 - c * 512):(q0 - c * 512) + w],
                                VTM[j][:, h * (HD + 1): (h + 1) * (HD + 1)],
                                pt[:, 0:w], start=(j == 0), stop=(j == nj - 1)),
                              reads=r, writes=wr)
                    for lane in range(2):
                        cf = CTXF[lane]
                        nb = BK[1 + 2 * lane]
                        nbp = PSF[1 + 2 * lane]
                        E("v", lambda e, cf=cf, cp=PSF[4 + lane]: e.tensor_copy(cf[:, :], cp[0:65, :]),
                          reads=[cbank[lane], ("ctxf", lane)], writes=[cbank[lane], ("ctxf", lane)])
                        E("s", lambda e, cf=cf, lr=LROW[lane]: e.activation(
                            out=lr[:, :], in_=cf[64:65, :], func=AF.Ln),
                          reads=[("ctxf", lane)], writes=[("lrow", lane)])
                        E("s", lambda e, lr=LROW[lane], rl=RLR[lane]: e.activation(
                            out=rl[:, :], in_=lr[:, :], func=AF.Exp, scale=-1.0),
                          reads=[("lrow", lane)], writes=[("rlr", lane)])
                        E("t", lambda e, rl=RLR[lane], nbp=nbp: e.matmul(
                            nbp[0:64, :], ONER[0:1, 0:64], rl[0:1, :], start=True, stop=True),
                          reads=[("rlr", lane), "oner", nb], writes=[nb])
                        E("v", lambda e, hp=hp, c=c, lane=lane, cf=cf, nbp=nbp: e.tensor_tensor(
                            CTXT[hp][lane * 64:(lane + 1) * 64, c * 512:(c + 1) * 512],
                            cf[0:64, :], nbp[0:64, :], ALU.mult),
                          reads=[("ctxf", lane), nb],
                          writes=[nb, ("ctxt", hp, c, lane)])
            L.barrier(skip=("qg",))

        if stop_after not in ("qkv", "attn"):
            stkCT.close()
            sbx = mk(stkX2); stkX2.__enter__()
            X2 = [sbx(f"x2_{i}", [128, D], F32) for i in range(NTT)]
            sbcc = mk(stkC); stkC.__enter__()
            TRSW = sbcc("trsw", [128, 1024], F32)
            EXR = sbcc("exr", [128, NE], F32)
            ESUM = sbcc("esum", [128, 1], F32)
            ERCP = sbcc("ercp", [128, 1], F32)
            SCT = sbcc("sct", [NE, T], F32)
            SCALL = sbcc("scall", [128, 128], F32)
            BLO = sbcc("blo", [128, 1], F32)
            BHI = sbcc("bhi", [128, 1], F32)
            BMID = sbcc("bmid", [128, 1], F32)
            BCNT = sbcc("bcnt", [128, 1], F32)
            BC1 = sbcc("bc1", [1, 1], F32)
            BSEL = sbcc("bsel", [128, 1], F32)
            BT1 = sbcc("bt1", [128, 1], F32)
            BT2 = sbcc("bt2", [128, 1], F32)
            TROW = sbcc("trow", [1, NE], F32)
            TRT = sbcc("trt", [1, 128], F32)
            SCW = sbcc("scw", [128, 128], F32)
            TBCW = sbcc("tbcw", [128, 128], F32)
            MSKW = sbcc("mskw", [128, 128], F32)
            CUMW = sbcc("cumw", [128, 128], F32)
            MSBW = sbcc("msbw", [128, 128], BF16)
            TSR = sbcc("tsr", [1, 128], F32)
            TS16 = sbcc("ts16", [16, NE], F32)
            OFF16 = sbcc("off16", [16, NE], F32)
            OFFR = sbcc("offr", [1, 128], F32)
            sbb = mk(stkB); stkB.__enter__()
            WOB = [sbb(f"wob{i}", [128, D], BF16) for i in range(NFT)]
            XTMS = [sbb(f"xtms{i}", [128, D], F32) for i in range(2)]
            YT = [sbb(f"yt{i}", [128, D], F32) for i in range(2)]
            BOBC = sbb("bobc", [128, D], F32)
            for hh in range(2):
                sl = slice(hh * 512, hh * 512 + 512)
                E("t", lambda e, sl=sl: e.matmul(PSF[0][:, :], ONER[0:1, :], BO[0:1, sl],
                                                 start=True, stop=True),
                  reads=["oner", "bo_r", "PSF0"], writes=["PSF0"])
                E("v", lambda e, sl=sl, hh=hh: e.tensor_copy(BOBC[:, sl], PSF[0][:, :]),
                  reads=["PSF0"], writes=["PSF0", ("bobc", hh)])
            for k in range(NFT):
                E("q_s", lambda e, k=k: e.dma_start(out=WOB[k][:, :],
                                                    in_=wob_d[k * 128:(k + 1) * 128, :]),
                  reads=["wob_d"], writes=[("wob", k)], sem="qs")
            for i in range(NTT):
                b = i % 2
                E("q_a", lambda e, i=i, b=b: e.dma_start(out=XTMS[b][:, :],
                                                         in_=x_tm[i * 128:(i + 1) * 128, :]),
                  reads=[("xtms", b)], writes=[("xtms", b)], sem="qa")
                for hh in range(2):
                    pb = (2 * i + hh) % 2
                    bank = BK[pb]
                    fns = []
                    for k in range(NFT):
                        r = [("wob", k)]
                        if k == 0:
                            r.append(bank)
                        fns.append((
                            (lambda e, i=i, hh=hh, k=k, pb=pb: e.matmul(
                                PSF[pb][:, :], CTXT[k][:, i * 128:(i + 1) * 128],
                                WOB[k][:, hh * 512:(hh + 1) * 512],
                                start=(k == 0), stop=(k == NFT - 1))),
                            r))
                    G("t", fns, writes=[bank])
                    sl = slice(hh * 512, hh * 512 + 512)
                    E("v", lambda e, b=b, pb=pb, sl=sl: e.tensor_tensor(
                        YT[b][:, sl], PSF[pb][:, :], BOBC[:, sl], ALU.add),
                      reads=[bank, ("bobc", hh), ("yt", b, hh)], writes=[bank, ("yt", b, hh)])
                    E("v", lambda e, b=b, sl=sl: e.tensor_tensor(
                        YT[b][:, sl], YT[b][:, sl], XTMS[b][:, sl], ALU.add),
                      reads=[("yt", b, hh), ("xtms", b)], writes=[("yt", b, hh)])
                ln_block(YT[b], [("yt", b, 0), ("yt", b, 1)], X2[i], ("x2", i), par=b)
                if x2_tap is not None:
                    E("q_a", lambda e, i=i: e.dma_start(out=x2_tap[i * 128:(i + 1) * 128, :],
                                                        in_=X2[i][:, :]),
                      reads=[("x2", i)], writes=[("x2tap", i)], sem="qa")
                pg = 4 + (i % 2)
                gbank = BK[pg]
                for f in range(NFT):
                    tb = f % 2
                    tbank = BK[2 + tb]
                    E("t", lambda e, i=i, f=f, tb=tb: e.transpose(
                        PSF[2 + tb][:, 0:128], X2[i][:, f * 128:(f + 1) * 128], EYEF[:, :]),
                      reads=[("x2", i), "eye_f", tbank], writes=[tbank])
                    E("v", lambda e, f=f, tb=tb: e.tensor_copy(TRSW[:, f * 128:(f + 1) * 128],
                                                               PSF[2 + tb][:, 0:128]),
                      reads=[tbank, ("trsw", f)], writes=[tbank, ("trsw", f)])
                fns = []
                for f in range(NFT):
                    r = [("trsw", f), ("wgs", f)]
                    if f == 0:
                        r.append(gbank)
                    fns.append((
                        (lambda e, f=f, pg=pg: e.matmul(
                            PSF[pg][:, 0:NE], TRSW[:, f * 128:(f + 1) * 128],
                            WGS[:, f * NE:(f + 1) * NE],
                            start=(f == 0), stop=(f == NFT - 1))),
                        r))
                G("t", fns, writes=[gbank])
                E("s", lambda e, pg=pg: e.activation(out=EXR[:, :], in_=PSF[pg][:, 0:NE],
                                                     func=AF.Exp, accum_out=ESUM[:, :]),
                  reads=[gbank, "exr", "esum"], writes=[gbank, "exr", "esum"])
                E("v", lambda e: e.reciprocal(ERCP[:, :], ESUM[:, :]),
                  reads=["esum"], writes=["ercp"])
                E("v", lambda e, i=i: e.tensor_scalar(SCW[:, i * NE:(i + 1) * NE], EXR[:, :],
                                                      ERCP[:, :], None, ALU.mult),
                  reads=["exr", "ercp", ("scw", i)], writes=[("scw", i)])
                if sc_tap is not None:
                    E("q_a", lambda e, i=i: e.dma_start(out=sc_tap[i * 128:(i + 1) * 128, :],
                                                        in_=SCW[:, i * NE:(i + 1) * NE]),
                      reads=[("scw", i)], writes=[("sctap", i)], sem="qa")
                tb2 = i % 2
                t2bank = BK[2 + tb2]
                E("t", lambda e, i=i, tb2=tb2: e.transpose(
                    PSF[2 + tb2][0:NE, 0:128], SCW[:, i * NE:(i + 1) * NE], EYEF[:, :]),
                  reads=[("scw", i), "eye_f", t2bank], writes=[t2bank])
                E("v", lambda e, i=i, tb2=tb2: e.tensor_copy(
                    SCT[:, i * 128:(i + 1) * 128], PSF[2 + tb2][0:NE, 0:128]),
                  reads=[t2bank, ("sct", i)], writes=[t2bank, ("sct", i)])
            L.barrier(skip=("qg",))

        if stop_after not in ("qkv", "attn", "ln1", "gate"):
            stkB.close()
            E("q_s", lambda e: e.dma_start(out=cc1_in[:, :], in_=SCT[:, :]),
              reads=[("sct", i) for i in range(NTT)], writes=["cc1_in"], sem="qs")
            E("g", lambda e: e.collective_compute(
                "AllGather", ALU.bypass, replica_groups=[list(range(8))],
                ins=[cc1_in[:, :]], outs=[cc1_out[:, :]]),
              reads=["cc1_in"], writes=["cc1_out"], sem="cc")
            E("q_g", lambda e: e.indirect_dma_start(
                out=SCALL[:, :], out_offset=None,
                in_=cc1_out.rearrange("r (s f) -> (r s) f", f=128),
                in_offset=IOff(ESEL[:, 0:1], 0)),
              reads=["cc1_out", "eselS"], writes=["scall"], sem="qg")
            E("v", lambda e: e.memset(BLO[:, :], 0.0), writes=["blo"])
            E("v", lambda e: e.memset(BHI[:, :], 0.5), writes=["bhid"])
            for it in range(SEARCH_ITERS):
                E("v", lambda e: e.tensor_tensor(BMID[:, :], BLO[:, :], BHI[:, :], ALU.add),
                  reads=["blo", "bhid", "bmid"], writes=["bmid"])
                E("v", lambda e: e.tensor_scalar(JUNKF[:, :], SCALL[:, :], BMID[:, :], None,
                                                 ALU.is_ge, ALU.add, accum_out=BCNT[:, :]),
                  reads=["scall", "bmid", "bcnt", "junkf"], writes=["bcnt", "junkf"])
                E("t", lambda e: e.matmul(PSF[2][:, 0:1], ONESQ[:, :], BCNT[:, :],
                                          start=True, stop=True),
                  reads=["onesq", "bcnt", "PSF2"], writes=["PSF2"])
                E("v", lambda e: e.tensor_scalar(BSEL[:, :], PSF[2][:, 0:1], float(CAPG), None, ALU.is_ge),
                  reads=["PSF2", "bsel"], writes=["PSF2", "bsel"])
                E("v", lambda e: e.tensor_tensor(BT1[:, :], BSEL[:, :], BHI[:, :], ALU.mult),
                  reads=["bsel", "bhid", "bt1"], writes=["bt1"])
                E("v", lambda e: e.tensor_tensor(BLO[:, :], BLO[:, :], BT1[:, :], ALU.add),
                  reads=["blo", "bt1"], writes=["blo"])
                E("v", lambda e: e.tensor_scalar(BHI[:, :], BHI[:, :], 0.5, None, ALU.mult),
                  reads=["bhid"], writes=["bhid"])
            E("q_s", lambda e: e.dma_start(out=cc2_in[:, :], in_=BLO[0:1, 0:1]),
              reads=["blo"], writes=["cc2_in"], sem="qs")
            E("g", lambda e: e.collective_compute(
                "AllGather", ALU.bypass, replica_groups=[list(range(8))],
                ins=[cc2_in[:, :]], outs=[cc2_out[:, :]]),
              reads=["cc2_in"], writes=["cc2_out"], sem="cc")
            E("q_s", lambda e: e.dma_start(out=TROW[:, :],
                                           in_=cc2_out.rearrange("e one -> one e")),
              reads=["cc2_out"], writes=["trow"], sem="qs")
            if thr_tap is not None:
                E("q_s", lambda e: e.dma_start(out=thr_tap[:, :], in_=TROW[:, :]),
                  reads=["trow"], writes=["thrtap"], sem="qs")
            for i in range(NTT):
                E("v", lambda e, i=i: e.tensor_copy(TRT[0:1, i * NE:(i + 1) * NE], TROW[0:1, :]),
                  reads=["trow", ("trt", i)], writes=[("trt", i)])
            trt_keys = [("trt", i) for i in range(NTT)]
            scw_keys = [("scw", i) for i in range(NTT)]
            E("t", lambda e: e.matmul(PSF[2][:, 0:128], ONER[0:1, :], TRT[0:1, :],
                                      start=True, stop=True),
              reads=trt_keys + ["oner", "PSF2"], writes=["PSF2"])
            E("v", lambda e: e.tensor_copy(TBCW[:, :], PSF[2][:, 0:128]),
              reads=["PSF2", "tbcw"], writes=["PSF2", "tbcw"])
            E("v", lambda e: e.tensor_tensor(MSKW[:, :], SCW[:, :], TBCW[:, :], ALU.is_ge),
              reads=scw_keys + ["tbcw", "mskw"], writes=["mskw"])
            E("v", lambda e: e.tensor_tensor(MSBW[:, :], SCW[:, :], MSKW[:, :], ALU.mult),
              reads=scw_keys + ["mskw", "msbw"], writes=["msbw"])
            for i in range(NTT):
                E("q_g", lambda e, i=i: e.dma_start(out=x2aug_d[i * 128:(i + 1) * 128, 0:D],
                                                    in_=X2[i][:, :]),
                  reads=[("x2", i)], writes=[("x2aug", i)], sem="qg")
                E("q_g", lambda e, i=i: e.dma_start(out=x2aug_d[i * 128:(i + 1) * 128, D:AUGW],
                                                    in_=MSBW[:, i * NE:(i + 1) * NE]),
                  reads=["msbw"], writes=[("x2aug_s", i)], sem="qg")
                E("q_s", lambda e, i=i: e.dma_start(out=y2acc_d[i * 128:(i + 1) * 128, :],
                                                    in_=X2[i][:, :]),
                  reads=[("x2", i)], writes=[("y2acc_init", i)], sem="qs")
                pcb = 2 + (i % 2)
                E("t", lambda e, i=i, pcb=pcb: e.matmul(PSF[pcb][:, 0:NE], SUT[:, :],
                                                        MSKW[:, i * NE:(i + 1) * NE],
                                                        start=True, stop=True),
                  reads=["sut", "mskw", BK[pcb]], writes=[BK[pcb]])
                E("v", lambda e, i=i, pcb=pcb: e.tensor_copy(CUMW[:, i * NE:(i + 1) * NE],
                                                             PSF[pcb][:, 0:NE]),
                  reads=[BK[pcb], ("cumw", i)], writes=[BK[pcb], ("cumw", i)])
                tsb = 4 + (i % 2)
                E("t", lambda e, i=i, tsb=tsb: e.matmul(PSF[tsb][0:1, 0:NE], ONEC[:, :],
                                                        MSKW[:, i * NE:(i + 1) * NE],
                                                        start=True, stop=True),
                  reads=["onec", "mskw", BK[tsb]], writes=[BK[tsb]])
                E("v", lambda e, i=i, tsb=tsb: e.tensor_copy(TSR[:, i * NE:(i + 1) * NE],
                                                             PSF[tsb][0:1, 0:NE]),
                  reads=[BK[tsb], ("tsr", i)], writes=[BK[tsb], ("tsr", i)])
            cum_keys = [("cumw", i) for i in range(NTT)]
            E("q_a", lambda e: e.dma_start(out=ts_d[:, :], in_=TSR[:, :]),
              reads=[("tsr", i) for i in range(NTT)], writes=["ts_d"], sem="qa")
            E("q_a", lambda e: e.dma_start(out=TS16[:, :],
                                           in_=ts_d.rearrange("one (i e) -> (one i) e", e=NE)),
              reads=["ts_d"], writes=["ts16"], sem="qa")
            E("t", lambda e: e.matmul(PSF[2][0:16, 0:NE], SUT[0:16, 0:16], TS16[:, :],
                                      start=True, stop=True),
              reads=["ts16", "sut", "PSF2"], writes=["PSF2"])
            E("v", lambda e: e.tensor_copy(OFF16[:, :], PSF[2][0:16, 0:NE]),
              reads=["PSF2", "off16"], writes=["PSF2", "off16"])
            E("q_a", lambda e: e.dma_start(out=off_d.rearrange("one (i e) -> (one i) e", e=NE),
                                           in_=OFF16[:, :]),
              reads=["off16"], writes=["off_d"], sem="qa")
            E("q_a", lambda e: e.dma_start(out=OFFR[:, :], in_=off_d[:, :]),
              reads=["off_d"], writes=["offr"], sem="qa")
            E("t", lambda e: e.matmul(PSF[3][:, 0:128], ONER[0:1, :], OFFR[0:1, :],
                                      start=True, stop=True),
              reads=["oner", "offr", "PSF3"], writes=["PSF3"])
            E("v", lambda e: e.tensor_tensor(CUMW[:, :], CUMW[:, :], PSF[3][:, 0:128], ALU.add),
              reads=cum_keys + ["PSF3"], writes=["PSF3"] + cum_keys)
            E("v", lambda e: e.tensor_scalar(CUMW[:, :], CUMW[:, :], BIGL[:, :], None, ALU.subtract),
              reads=cum_keys + ["bigl"], writes=cum_keys)
            E("v", lambda e: e.tensor_tensor(CUMW[:, :], CUMW[:, :], MSKW[:, :], ALU.mult),
              reads=cum_keys + ["mskw"], writes=cum_keys)
            E("v", lambda e: e.tensor_scalar(CUMW[:, :], CUMW[:, :], BIGL[:, :], None, ALU.add),
              reads=cum_keys + ["bigl"], writes=cum_keys)
            E("v", lambda e: e.tensor_copy(POSW[:, :], CUMW[:, :]),
              reads=cum_keys + ["posw"], writes=["posw"])
            if pos_tap is not None:
                E("q_a", lambda e: e.dma_start(out=pos_tap[:, :], in_=POSW[:, :]),
                  reads=["posw"], writes=["postap"], sem="qa")
            L.barrier(skip=("qg",))

        if stop_after not in ("qkv", "attn", "ln1", "gate", "route"):
            stkC.close()
            stkX2.close()
            stkPS1.close()
            FPS = [ps(f"fps{i}", [128, 512], F32) for i in range(4)]
            FPB = [ps(f"fpb{i}", [128, 512], BF16) for i in range(4)]
            sbd = mk(stkD); stkD.__enter__()
            ILSB = [sbd(f"ilsb{i}", [128, CT], U32) for i in range(2)]
            RTM = [sbd(f"rtm{i}", [128, CT * AUGW], BF16) for i in range(2)]
            SCOL = sbd("scol", [128, 1], F32)
            RT = [sbd(f"rt{i}", [128, NFT * CAP], BF16) for i in range(2)]
            W1C = [sbd(f"w1c{i}", [128, 4, 1024], BF16) for i in range(2)]
            W3C = [sbd(f"w3c{i}", [128, 4, 1024], BF16) for i in range(2)]
            W2A = sbd("w2a", [128, NHT, D], BF16)
            SIL = [sbd(f"sil{i}", [128, CAP], BF16) for i in range(2)]
            HT = sbd("ht", [128, NHT * CAP], BF16)
            OTM = [sbd("otm0", [128, CT * D], F32)] * 2

            x2aug_keys = ([("x2aug", i) for i in range(NTT)] +
                          [("x2aug_s", i) for i in range(NTT)] +
                          [("x2aug_pad", r) for r in range(CT)])
            y2init_keys = ([("y2acc_init", i) for i in range(NTT)] +
                           [("y2acc_pad", r) for r in range(CT)])

            def route_emit(e_i):
                eb = e_i % 2
                ilrows = slice(e_i * CAP, (e_i + 1) * CAP)
                E("q_g", lambda e, ilrows=ilrows: e.dma_start(
                    out=ilist_d[ilrows, :].rearrange("(p c) one -> p (c one)", c=CT),
                    in_=INIT5[:, :]),
                  reads=["init5", ("il", e_i)], writes=[("il", e_i)], sem="qg")
                for i in range(NTT):
                    E("q_g", lambda e, e_i=e_i, i=i: e.indirect_dma_start(
                        out=ilist_d[:, :],
                        out_offset=IOff(POSW[:, NE * i + e_i: NE * i + e_i + 1], 0),
                        in_=IDS[:, i:i + 1], in_offset=None,
                        element_offset=e_i * CAP,
                        bounds_check=REGS["bc"], oob_is_err=False),
                      reads=["ids", "posw", ("il", e_i)], writes=[("ilsc", e_i, i)], sem="qg")
                E("q_g", lambda e, eb=eb, ilrows=ilrows: e.dma_start(
                    out=ILSB[eb][:, :],
                    in_=ilist_d[ilrows, :].rearrange("(p c) one -> p (c one)", c=CT)),
                  reads=[("il", e_i), ("ilsb", eb)] + [("ilsc", e_i, i) for i in range(NTT)],
                  writes=[("ilsb", eb)], sem="qg")
                if il_tap is not None:
                    E("q_g", lambda e, ilrows=ilrows: e.dma_start(out=il_tap[ilrows, :],
                                                                  in_=ilist_d[ilrows, :]),
                      reads=[("il", e_i)], writes=[("iltap", e_i)], sem="qg")
                for c in range(CT):
                    E("q_g", lambda e, eb=eb, c=c: e.indirect_dma_start(
                        out=RTM[eb][:, c * AUGW:(c + 1) * AUGW], out_offset=None,
                        in_=x2aug_d[:, :], in_offset=IOff(ILSB[eb][:, c:c + 1], 0)),
                      reads=[("ilsb", eb), ("rtm", eb, c)] +
                            (x2aug_keys if e_i == 0 else []),
                      writes=[("rtm", eb, c)], sem="qg")

            def trans_thunks(e_i):
                eb = e_i % 2
                th = []
                for c in range(CT):
                    def t0(eb=eb, c=c, e_i=e_i):
                        E("v", lambda e, eb=eb, c=c, e_i=e_i: e.tensor_copy(
                            SCOL[:, :], RTM[eb][:, c * AUGW + D + e_i: c * AUGW + D + e_i + 1]),
                          reads=[("rtm", eb, c), "scol"], writes=["scol"])
                        E("v", lambda e, eb=eb, c=c: e.tensor_scalar(
                            RTM[eb][:, c * AUGW: c * AUGW + D],
                            RTM[eb][:, c * AUGW: c * AUGW + D],
                            SCOL[:, :], None, ALU.mult),
                          reads=[("rtm", eb, c), "scol"], writes=[("rtm", eb, c)])
                    th.append(t0)
                n = 0
                for c in range(CT):
                    for fb in range(NFT):
                        def t2(eb=eb, c=c, fb=fb, n=n):
                            bk = f"FPB{n % 4}"
                            pb = FPB[n % 4]
                            E("t", lambda e, eb=eb, c=c, fb=fb, pb=pb: e.transpose(
                                pb[:, 0:128],
                                RTM[eb][:, c * AUGW + fb * 128: c * AUGW + (fb + 1) * 128],
                                EYEB[:, :]),
                              reads=[("rtm", eb, c), "eye_bf", bk], writes=[bk])
                            E("v", lambda e, eb=eb, c=c, fb=fb, pb=pb: e.tensor_copy(
                                RT[eb][:, fb * CAP + c * 128: fb * CAP + (c + 1) * 128],
                                pb[:, 0:128]),
                              reads=[bk, ("rt", eb, fb)],
                              writes=[bk, ("rt", eb, fb)])
                        th.append(t2)
                        n += 1
                return th

            def wchunk_emit(e_i, q):
                r = q % 2
                rows = slice((e_i * 4 + q) * 128, (e_i * 4 + q + 1) * 128)
                xr = x2aug_keys if (e_i == 0 and q == 0) else []
                E("q_s", lambda e, rows=rows, r=r: e.dma_start(
                    out=W1C[r][:, :, :],
                    in_=w1b_d[rows, :].rearrange("p (h f) -> p h f", f=1024)),
                  reads=[("w1c", r), "w1b"] + xr, writes=[("w1c", r)], sem="qs")
                E("q_s", lambda e, rows=rows, r=r: e.dma_start(
                    out=W3C[r][:, :, :],
                    in_=w3b_d[rows, :].rearrange("p (h f) -> p h f", f=1024)),
                  reads=[("w3c", r), "w3b"] + xr, writes=[("w3c", r)], sem="qs")

            def w2_emit(e_i):
                xr = x2aug_keys if e_i == 0 else []
                E("q_a", lambda e, e_i=e_i: e.dma_start(
                    out=W2A[:, :, :],
                    in_=w2b_d[e_i * 128:(e_i + 1) * 128, :].rearrange("p (k f) -> p k f", f=1024)),
                  reads=["w2a", "w2b"] + xr, writes=["w2a"], sem="qa")

            def compute_emit(e_i, hooks=(), pending=None):
                eb = e_i % 2
                for ht_i in range(NHT):
                    q, hh = divmod(ht_i, 4)
                    r = (e_i * 4 + q) % 2
                    if ht_i == 0:
                        w2_emit(e_i)
                    if hh == 0 and q + 1 <= 3:
                        wchunk_emit(e_i, q + 1)
                    if ht_i == 12 and e_i + 1 < NE:
                        wchunk_emit(e_i + 1, 0)
                    for hat, hfn in hooks:
                        if ht_i == hat:
                            hfn()
                    for part in range(2):
                        wsb = W1C[r] if part == 0 else W3C[r]
                        wkey = ("w1c", r) if part == 0 else ("w3c", r)
                        for seg, (n0, n1) in enumerate([(0, 512), (512, CAP)]):
                            pi = part * 2 + seg
                            pp = FPS[pi]
                            bank = f"FPS{pi}"
                            fns = []
                            for k in range(NFT):
                                rd = [wkey, ("rt", eb, k)]
                                if k == 0:
                                    rd.append(bank)
                                fns.append((
                                    (lambda e, wsb=wsb, hh=hh, k=k, n0=n0, n1=n1, pp=pp, eb=eb: e.matmul(
                                        pp[:, 0:n1 - n0],
                                        wsb[:, hh, k * 128:(k + 1) * 128],
                                        RT[eb][:, k * CAP + n0: k * CAP + n1],
                                        start=(k == 0), stop=(k == NFT - 1))),
                                    rd))
                            G("t", fns, writes=[bank])
                    sb_i = ht_i % 2
                    for seg, (n0, n1) in enumerate([(0, 512), (512, CAP)]):
                        p1k, p3k = f"FPS{seg}", f"FPS{2 + seg}"
                        p1p, p3p = FPS[seg], FPS[2 + seg]
                        E("s", lambda e, sb_i=sb_i, n0=n0, n1=n1, p1p=p1p: e.activation(
                            out=SIL[sb_i][:, n0:n1], in_=p1p[:, 0:n1 - n0], func=AF.Silu),
                          reads=[p1k, ("sil", sb_i, seg)], writes=[p1k, ("sil", sb_i, seg)])
                        E("v", lambda e, sb_i=sb_i, ht_i=ht_i, n0=n0, n1=n1, p3p=p3p: e.tensor_tensor(
                            HT[:, ht_i * CAP + n0: ht_i * CAP + n1],
                            p3p[:, 0:n1 - n0], SIL[sb_i][:, n0:n1], ALU.mult),
                          reads=[("sil", sb_i, seg), p3k, ("ht", ht_i)],
                          writes=[p3k, ("ht", ht_i)])
                    if pending is not None and ht_i >= 6:
                        slot = ht_i - 6
                        lo = (len(pending) * slot) // 10
                        hi = (len(pending) * (slot + 1)) // 10
                        for th in pending[lo:hi]:
                            th()
                for c in range(CT):
                    for seg, (n0, n1) in enumerate([(0, 512), (512, 1024)]):
                        pp = FPS[seg]
                        bank = f"FPS{seg}"
                        fns = []
                        for k in range(NHT):
                            rd = [("ht", k), "w2a"]
                            if k == 0:
                                rd.append(bank)
                            fns.append((
                                (lambda e, c=c, k=k, n0=n0, n1=n1, pp=pp: e.matmul(
                                    pp[:, 0:n1 - n0],
                                    HT[:, k * CAP + c * 128: k * CAP + (c + 1) * 128],
                                    W2A[:, k, n0:n1],
                                    start=(k == 0), stop=(k == NHT - 1))),
                                rd))
                        G("t", fns, writes=[bank])
                        E("v", lambda e, eb=eb, c=c, n0=n0, n1=n1, pp=pp: e.tensor_copy(
                            OTM[eb][:, c * D + n0: c * D + n1], pp[:, 0:n1 - n0]),
                          reads=[bank, ("otm", c)], writes=[bank, ("otm", c)])

            def scatter_emit(e_i):
                eb = e_i % 2
                prev = ([("y2w", e_i - 1, c) for c in range(CT)] if e_i > 0
                        else y2init_keys)
                for c in range(CT):
                    E("q_g", lambda e, eb=eb, c=c: e.indirect_dma_start(
                        out=y2acc_d[:, :], out_offset=IOff(ILSB[eb][:, c:c + 1], 0),
                        in_=OTM[eb][:, c * D:(c + 1) * D], in_offset=None,
                        compute_op=ALU.add, oob_is_err=True),
                      reads=[("otm", c), ("ilsb", eb)] + prev,
                      writes=[("y2w", e_i, c)], sem="qg")

            route_emit(0)
            wchunk_emit(0, 0)
            for th in trans_thunks(0):
                th()
            for e_i in range(NE):
                hooks = []
                if e_i > 0:
                    hooks.append((1, lambda e_i=e_i: scatter_emit(e_i - 1)))
                if e_i + 1 < NE:
                    hooks.append((2, lambda e_i=e_i: route_emit(e_i + 1)))
                pending = trans_thunks(e_i + 1) if e_i + 1 < NE else None
                compute_emit(e_i, hooks=hooks, pending=pending)
            scatter_emit(NE - 1)
            L.barrier(skip=("qg",))

        if stop_after is None:
            stkD.close()
            sbe = mk(stkE); stkE.__enter__()
            Y2S = [sbe(f"y2s{i}", [128, D], F32) for i in range(2)]
            OUTS = [sbe(f"outs{i}", [128, D], F32) for i in range(2)]
            for i in range(NTT):
                b = i % 2
                E("q_s", lambda e, i=i, b=b: e.dma_start(out=Y2S[b][:, :],
                                                         in_=y2acc_d[i * 128:(i + 1) * 128, :]),
                  reads=[("y2w", NE - 1, c) for c in range(CT)] + [("y2s", b)],
                  writes=[("y2s", b)], sem="qs")
                if y2_tap is not None:
                    E("q_a", lambda e, i=i, b=b: e.dma_start(out=y2_tap[i * 128:(i + 1) * 128, :],
                                                             in_=Y2S[b][:, :]),
                      reads=[("y2s", b)], writes=[("y2tap", i)], sem="qa")
                ln_block(Y2S[b], [("y2s", b)], OUTS[b], ("outs", b), par=b)
                E("q_a", lambda e, i=i, b=b: e.dma_start(out=out_tm[i * 128:(i + 1) * 128, :],
                                                         in_=OUTS[b][:, :]),
                  reads=[("outs", b)], writes=[("out", i), ("y2s", b), ("outs", b)], sem="qa")
        L.barrier()
        for _stk in (stkE, stkD, stkC, stkB, stkX2, stkA, stkCT):
            _stk.close()

        sem_stack = contextlib.ExitStack()
        with sem_stack, nc.Block() as block:
            semmap = {n: sem_stack.enter_context(nc.semaphore("sem_" + n))
                      for n in L.semnames}

            block.tensor(_make_merged_runner(L, ("t",), semmap))
            block.vector(_make_merged_runner(L, ("v",), semmap))
            block.scalar(_make_merged_runner(L, ("s", "q_a"), semmap))
            def _g_pre(eng):
                REGS["bc"] = eng.alloc_register("bc_reg")
                eng.reg_mov(REGS["bc"], CAP - 1)
            block.gpsimd(_make_merged_runner(L, ("g", "q_g"), semmap, pre=_g_pre))
            block.sync(_make_merged_runner(L, ("q_s",), semmap))

    return nc, tap_aps


def _make_merged_runner(L, names, semmap, pre=None):
    def run(eng):
        if pre is not None:
            pre(eng)
        entries = []
        for n in names:
            entries.extend(L.progs[n])
        entries.sort(key=lambda t: t[-1])
        for entry in entries:
            if entry[0] == "wait":
                _, semname, val, _ = entry
                eng.wait_ge(semmap[semname], val)
            else:
                _, fn, semname, step, _ = entry
                inst = fn(eng)
                if semname is not None:
                    if step > 1:
                        inst.then_inc(semmap[semname], step)
                    else:
                        inst.then_inc(semmap[semname])
    return run


def host_inputs(inputs):
    x = np.asarray(inputs["L0_attn_x"])
    Wq = np.asarray(inputs["L0_attn_Wq"]); Wk = np.asarray(inputs["L0_attn_Wk"])
    Wv = np.asarray(inputs["L0_attn_Wv"]); Wo = np.asarray(inputs["L0_attn_Wo"])
    bo = np.asarray(inputs["L0_attn_bo"])
    wg_ = np.asarray(inputs["L0_ffn_wg"])
    w1_ = np.asarray(inputs["L0_ffn_w1"]); w3_ = np.asarray(inputs["L0_ffn_w3"])
    w2_ = np.asarray(inputs["L0_ffn_w2"])

    def tile_T(w):
        wt = np.ascontiguousarray(w.T)
        return np.ascontiguousarray(
            wt.reshape(NFT, 128, NFT, 128).transpose(2, 0, 1, 3))

    wqT = tile_T(Wq); wkT = tile_T(Wk); wvT = tile_T(Wv)
    woT = np.ascontiguousarray(Wo.T)
    def pmaj(w):
        a = w.reshape(NE, NFT, 128, NHT, 128).transpose(0, 2, 3, 1, 4)
        a = a.reshape(NE, 128, 4, 4, NFT, 128).transpose(0, 2, 1, 3, 4, 5)
        return np.ascontiguousarray(a.reshape(NE * 4 * 128, 4096))

    w1pm = pmaj(w1_); w3pm = pmaj(w3_)
    w2pm = np.ascontiguousarray(
        w2_.reshape(NE, NHT, 128, D).transpose(0, 2, 1, 3).reshape(NE * 128, NHT * D))

    eye_bf = np.eye(128, dtype=ml_dtypes.bfloat16)
    eye_f = np.eye(128, dtype=np.float32)
    sut_f = np.triu(np.ones((128, 128), dtype=np.float32), k=1)
    onec_f = np.ones((128, 1), dtype=np.float32)
    oner_f = np.ones((1, 128), dtype=np.float32)
    oner_b = np.ones((1, 128), dtype=ml_dtypes.bfloat16)
    utk_b = (np.arange(128)[:, None] <= np.arange(128)[None, :]).astype(ml_dtypes.bfloat16)

    in_maps = []
    for c in range(8):
        p = np.arange(128)
        esel_c = (((p // 16) * 8 + c) * 16 + (p % 16)).astype(np.uint32).reshape(128, 1)
        in_maps.append({
            "xT": np.ascontiguousarray(x[c].T).astype(np.float32, copy=False),
            "x_tm": np.ascontiguousarray(x[c]).astype(np.float32, copy=False),
            "wqT": wqT, "wkT": wkT, "wvT": wvT, "woT": woT,
            "bo_r": np.ascontiguousarray(bo.reshape(1, D)).astype(np.float32, copy=False),
            "wg": np.ascontiguousarray(wg_).astype(np.float32, copy=False),
            "w1pm": w1pm, "w3pm": w3pm, "w2pm": w2pm,
            "esel": esel_c,
            "actc_f": np.tile(np.array([list(ACT_CONSTS) + [0.0] * (8 - len(ACT_CONSTS))],
                               dtype=np.float32), (128, 1)),
            "eye_bf": eye_bf, "eye_f": eye_f, "sut_f": sut_f,
            "onec_f": onec_f, "oner_f": oner_f, "oner_b": oner_b,
            "utk_b": utk_b,
        })
    return in_maps


def kernel(**inputs):
    in_maps = host_inputs(inputs)
    nc, _ = build_program()
    res = run_bass_kernel_spmd(nc, in_maps, core_ids=list(range(8)))
    out = np.stack([res.results[c]["out_tm"] for c in range(8)], axis=0)
    return out.astype(np.float32)


# version 21
# speedup vs baseline: 1.0754x; 1.0754x over previous
import numpy as np
import ml_dtypes
import concourse.bass as bass
import concourse.mybir as mybir
from concourse.bass_utils import run_bass_kernel_spmd

F32 = mybir.dt.float32
BF16 = mybir.dt.bfloat16
U32 = mybir.dt.uint32
AF = mybir.ActivationFunctionType
ALU = mybir.AluOpType
AX = mybir.AxisListType
IOff = bass.IndirectOffsetOnAxis

T = 2048
D = 1024
H = 16
HD = 64
HID = 2048
NE = 8
CAPG = 4096
CAP = 640
CT = CAP // 128
NTT = T // 128
NFT = D // 128
NHT = HID // 128
AUGW = D + 8
NROWS = T + CAP
SEARCH_ITERS = 16
ACT_CONSTS = (-4.0, 0.125, -1.0, -0.5, 1e-5)
ENAMES = ("t", "v", "s", "g", "q_s", "q_a", "q_g")


class Ledger:

    NQSEM = 20

    def __init__(self, semnames=None):
        self.progs = {e: [] for e in ENAMES}
        self.seq = 0
        self.semnames = (["t", "v", "s", "g", "cc"] +
                         [f"{q}{j}" for q in ("qs", "qa", "qg")
                          for j in range(self.NQSEM)])
        self.cnt = {k: 0 for k in self.semnames}
        self.dmacnt = {"qs": 0, "qa": 0, "qg": 0}
        self.clocks = {e: {} for e in ENAMES}
        self.snap = {}
        self.last_w = {}
        self.readers = {}
        self.eng_sem = {"t": "t", "v": "v", "s": "s", "g": "g",
                        "q_s": "qs", "q_a": "qa", "q_g": "qg"}
        self.sem_step = {n: (16 if n[:2] in ("qs", "qa", "qg") else 1)
                         for n in self.semnames}

    def _append(self, ename, entry):
        self.progs[ename].append(entry + (self.seq,))
        self.seq += 1

    def _wait(self, ename, semname, val):
        cl = self.clocks[ename]
        if cl.get(semname, 0) >= val:
            return
        self._append(ename, ("wait", semname, val))
        snap = self.snap.get((semname, val), {semname: val})
        for k, v in snap.items():
            if cl.get(k, 0) < v:
                cl[k] = v

    def _needed(self, reads, writes):
        need = {}
        def add(dep):
            if dep is None:
                return
            s, v = dep
            if need.get(s, 0) < v:
                need[s] = v
        for k in reads:
            add(self.last_w.get(k))
        for k in writes:
            add(self.last_w.get(k))
            for dep in self.readers.get(k, ()):
                add(dep)
        return need

    def _bump(self, ename, semname):
        step = self.sem_step[semname]
        self.cnt[semname] += step
        val = self.cnt[semname]
        self.snap[(semname, val)] = dict(self.clocks[ename]) | {semname: val}
        return (semname, val)

    def op(self, ename, fn, reads=(), writes=(), sem=None):
        for s, v in sorted(self._needed(reads, writes).items()):
            self._wait(ename, s, v)
        semname = sem if sem is not None else self.eng_sem[ename]
        if semname in self.dmacnt:
            idx = self.dmacnt[semname]
            self.dmacnt[semname] += 1
            q = semname
            semname = f"{q}{idx % self.NQSEM}"
            if idx >= self.NQSEM:
                self._wait(ename, semname, 16 * (idx // self.NQSEM))
        dep = self._bump(ename, semname)
        self._append(ename, ("inst", fn, semname, self.sem_step[semname]))
        for k in writes:
            self.last_w[k] = dep
            self.readers[k] = []
        for k in reads:
            self.readers.setdefault(k, []).append(dep)
        return dep

    def group(self, ename, fns_reads, writes=(), sem=None):
        semname = sem if sem is not None else self.eng_sem[ename]
        n = len(fns_reads)
        all_reads = set()
        for i, (fn, reads) in enumerate(fns_reads):
            for s, v in sorted(self._needed(reads, writes if i == 0 else ()).items()):
                self._wait(ename, s, v)
            if i < n - 1:
                self._append(ename, ("inst", fn, None, 0))
            else:
                dep = self._bump(ename, semname)
                self._append(ename, ("inst", fn, semname, self.sem_step[semname]))
            all_reads.update(reads)
        for k in writes:
            self.last_w[k] = dep
            self.readers[k] = []
        for k in all_reads:
            self.readers.setdefault(k, []).append(dep)

    def barrier(self, ename="g", skip=()):
        for s in sorted(self.cnt):
            if self.cnt[s] > 0 and not any(s.startswith(p) for p in skip):
                self._wait(ename, s, self.cnt[s])
        dep = self._bump(ename, self.eng_sem[ename])
        self._append(ename, ("inst", lambda e: e.nop(), self.eng_sem[ename], 1))
        for other in ENAMES:
            if other != ename:
                self._wait(other, dep[0], dep[1])


def build_program(taps=(), stop_after=None):
    taps = set(taps)
    nc = bass.Bass("TRN2", target_bir_lowering=False, debug=False)

    xT = nc.dram_tensor("xT", [D, T], F32, kind="ExternalInput").ap()
    x_tm = nc.dram_tensor("x_tm", [T, D], F32, kind="ExternalInput").ap()
    wqT = nc.dram_tensor("wqT", [NFT, NFT, 128, 128], F32, kind="ExternalInput").ap()
    wkT = nc.dram_tensor("wkT", [NFT, NFT, 128, 128], F32, kind="ExternalInput").ap()
    wvT = nc.dram_tensor("wvT", [NFT, NFT, 128, 128], F32, kind="ExternalInput").ap()
    woT = nc.dram_tensor("woT", [D, D], F32, kind="ExternalInput").ap()
    bo_r = nc.dram_tensor("bo_r", [1, D], F32, kind="ExternalInput").ap()
    wg = nc.dram_tensor("wg", [D, NE], F32, kind="ExternalInput").ap()
    w1pm = nc.dram_tensor("w1pm", [NE * 4 * 128, 4096], F32, kind="ExternalInput").ap()
    w3pm = nc.dram_tensor("w3pm", [NE * 4 * 128, 4096], F32, kind="ExternalInput").ap()
    w2pm = nc.dram_tensor("w2pm", [NE * 128, 16384], F32, kind="ExternalInput").ap()
    esel = nc.dram_tensor("esel", [128, 1], U32, kind="ExternalInput").ap()
    eye_bf = nc.dram_tensor("eye_bf", [128, 128], BF16, kind="ExternalInput").ap()
    eye_f = nc.dram_tensor("eye_f", [128, 128], F32, kind="ExternalInput").ap()
    sut_f = nc.dram_tensor("sut_f", [128, 128], F32, kind="ExternalInput").ap()
    onec_f = nc.dram_tensor("onec_f", [128, 1], F32, kind="ExternalInput").ap()
    oner_f = nc.dram_tensor("oner_f", [1, 128], F32, kind="ExternalInput").ap()
    oner_b = nc.dram_tensor("oner_b", [1, 128], BF16, kind="ExternalInput").ap()
    utk_b = nc.dram_tensor("utk_b", [128, 128], BF16, kind="ExternalInput").ap()
    actc_f = nc.dram_tensor("actc_f", [128, 8], F32, kind="ExternalInput").ap()

    out_tm = nc.dram_tensor("out_tm", [T, D], F32, kind="ExternalOutput").ap()
    tap_aps = {}
    def tap(name, shape, dtype=F32):
        if name in taps:
            tap_aps[name] = nc.dram_tensor("tap_" + name, shape, dtype,
                                           kind="ExternalOutput").ap()
        return tap_aps.get(name)

    x2_tap = tap("x2", [T, D])
    sc_tap = tap("scores", [T, NE])
    thr_tap = tap("thr", [1, NE])
    pos_tap = tap("pos", [128, NE * NTT], U32)
    il_tap = tap("ilist", [NE * CAP, 1], U32)
    y2_tap = tap("y2", [T, D])

    x2aug_d = nc.dram_tensor("x2aug_d", [NROWS, AUGW], BF16, kind="Internal").ap()
    y2acc_d = nc.dram_tensor("y2acc_d", [NROWS, D], F32, kind="Internal").ap()
    ilist_d = nc.dram_tensor("ilist_d", [NE * CAP, 1], U32, kind="Internal").ap()
    cc1_in = nc.dram_tensor("cc1_in", [NE, T], F32, kind="Internal").ap()
    cc1_out = nc.dram_tensor("cc1_out", [8 * NE, T], F32, kind="Internal").ap()
    cc2_in = nc.dram_tensor("cc2_in", [1, 1], F32, kind="Internal").ap()
    cc2_out = nc.dram_tensor("cc2_out", [8, 1], F32, kind="Internal").ap()
    ts_d = nc.dram_tensor("ts_d", [1, 128], F32, kind="Internal").ap()
    wob_d = nc.dram_tensor("wob_d", [D, D], BF16, kind="Internal").ap()
    w1b_d = nc.dram_tensor("w1b_d", [NE * 4 * 128, 4096], BF16, kind="Internal").ap()
    w3b_d = nc.dram_tensor("w3b_d", [NE * 4 * 128, 4096], BF16, kind="Internal").ap()
    w2b_d = nc.dram_tensor("w2b_d", [NE * 128, 16384], BF16, kind="Internal").ap()
    off_d = nc.dram_tensor("off_d", [1, 128], F32, kind="Internal").ap()

    REGS = {}
    import contextlib
    stack = contextlib.ExitStack()
    stkX2 = contextlib.ExitStack()
    stkCT = contextlib.ExitStack()
    stkA = contextlib.ExitStack()
    stkB = contextlib.ExitStack()
    stkC = contextlib.ExitStack()
    stkD = contextlib.ExitStack()
    stkE = contextlib.ExitStack()

    def mk(stk):
        return (lambda name, shape, dtype: stk.enter_context(nc.sbuf_tensor(name, shape, dtype)))
    sb = mk(stack)
    ps = lambda name, shape, dtype: stack.enter_context(nc.psum_tensor(name, shape, dtype))

    stkPS1 = contextlib.ExitStack()
    with stack:
        ps1 = lambda name, shape, dtype: stkPS1.enter_context(nc.psum_tensor(name, shape, dtype))
        PSF = [ps1(f"psf{i}", [128, 512], F32) for i in range(6)]
        PSB = [ps1(f"psb{i}", [128, 512], BF16) for i in range(2)]
        BK = [f"PSF{i}" for i in range(6)] + ["PSB0", "PSB1"]

        EYEB = sb("eyeb", [128, 128], BF16)
        EYEF = sb("eyef", [128, 128], F32)
        SUT = sb("sut", [128, 128], F32)
        ONEC = sb("onec", [128, 1], F32)
        ONER = sb("oner", [1, 128], F32)
        ONERB = sb("onerb", [1, 128], BF16)
        UTK = sb("utk", [128, 128], BF16)
        ESEL = sb("eselS", [128, 1], U32)
        BO = sb("boS", [1, D], F32)
        WGS = sb("wgs", [128, NE * NFT], F32)
        IDS = sb("ids", [128, NTT], U32)
        ONESQ = sb("onesq", [128, 128], F32)
        ACTC = sb("actc", [128, 8], F32)
        for j, v in enumerate(ACT_CONSTS):
            nc.const_aps.register(F32, v, ACTC[:, j:j + 1])
        INIT5 = sb("init5", [128, CT], U32)
        BIGL = sb("bigl", [128, 1], F32)
        JUNKB = [sb(f"junkb{j}", [128, D], BF16) for j in range(2)]
        JUNKF = sb("junkf", [128, 128], F32)
        ST1 = [sb(f"st1_{j}", [128, 1], F32) for j in range(2)]
        ST2 = [sb(f"st2_{j}", [128, 1], F32) for j in range(2)]
        MU = [sb(f"mu{j}", [128, 1], F32) for j in range(2)]
        VAR = [sb(f"var{j}", [128, 1], F32) for j in range(2)]
        RSTD = [sb(f"rstd{j}", [128, 1], F32) for j in range(2)]
        NMR = [sb(f"nmr{j}", [128, 1], F32) for j in range(2)]
        TM1 = [sb(f"tm1_{j}", [128, 1], F32) for j in range(2)]
        POSW = sb("posw", [128, 128], U32)

        sba2 = mk(stkCT); stkCT.__enter__()
        QTB = [sba2(f"qtb{i}", [128, T], BF16) for i in range(NFT)]
        KTB = [sba2(f"ktb{i}", [128, T], BF16) for i in range(NFT)]
        VTM = [sba2(f"vtm{i}", [128, H * (HD + 1)], BF16) for i in range(NTT)]
        PT = [[sba2(f"pt{l}_{i}", [128, 512], BF16) for i in range(2)] for l in range(2)]
        CTXF = [sba2(f"ctxf{i}", [65, 512], F32) for i in range(2)]
        LROW = [sba2(f"lrow{i}", [1, 512], F32) for i in range(2)]
        RLR = [sba2(f"rlr{i}", [1, 512], F32) for i in range(2)]
        ZROA = sba2("zroa", [128, AUGW], BF16)
        ZROY = sba2("zroy", [128, D], F32)

        sba = mk(stkA); stkA.__enter__()
        XTB = [sba(f"xtb{i}", [128, T], BF16) for i in range(NFT)]
        WSTG = [sba(f"wstg{i}", [128, 1024], BF16) for i in range(2)]
        VTT = sba("vtt", [128, T], BF16)
        CTXT = [stack.enter_context(nc.sbuf_tensor(f"ctxt{i}", [128, T], BF16, side="right"))
                for i in range(NFT)]

        L = Ledger(("t", "v", "s", "g", "qs", "qa", "qg", "cc"))
        E = L.op
        G = L.group

        def ln_block(src_tile, src_keys, out_tile, out_key, par=0):
            st1, st2, mu, var, rstd, nmr, tm1, jb = (ST1[par], ST2[par], MU[par],
                VAR[par], RSTD[par], NMR[par], TM1[par], JUNKB[par])
            k1, k2, kmu, kva, kvr, krl, krs, knm, ktm, kjb = [(n, par) for n in
                ("st1", "st2", "mu", "var_a", "var", "rstd_l", "rstd", "nmr", "tm1", "junkb")]
            E("v", lambda e: e.reduce_sum(st1[:, :], src_tile[:, :], AX.X),
              reads=list(src_keys) + [k1], writes=[k1])
            E("s", lambda e: e.activation(out=jb[:, :], in_=src_tile[:, :],
                                          func=AF.Square, accum_out=st2[:, :]),
              reads=list(src_keys) + [k2, kjb], writes=[k2, kjb])
            E("v", lambda e: e.tensor_scalar_mul(mu[:, :], st1[:, :], 1.0 / D),
              reads=[k1, kmu], writes=[kmu])
            E("v", lambda e: e.tensor_scalar_mul(var[:, :], st2[:, :], 1.0 / D),
              reads=[k2, kva], writes=[kva])
            E("v", lambda e: e.tensor_tensor(tm1[:, :], mu[:, :], mu[:, :], ALU.mult),
              reads=[kmu, ktm], writes=[ktm])
            E("v", lambda e: e.tensor_tensor(var[:, :], var[:, :], tm1[:, :], ALU.subtract),
              reads=[kva, ktm], writes=[kva])
            E("s", lambda e: e.activation(out=rstd[:, :], in_=var[:, :], func=AF.Ln, bias=1e-5),
              reads=[kva, krl], writes=[krl])
            E("s", lambda e: e.activation(out=rstd[:, :], in_=rstd[:, :], func=AF.Exp, scale=-0.5),
              reads=[krl, krs], writes=[krs])
            E("v", lambda e: e.tensor_tensor(nmr[:, :], mu[:, :], rstd[:, :], ALU.mult),
              reads=[kmu, krs, knm], writes=[knm])
            E("v", lambda e: e.tensor_scalar_mul(nmr[:, :], nmr[:, :], -1.0),
              reads=[knm], writes=[knm])
            E("v", lambda e: e.tensor_scalar(out_tile[:, :], src_tile[:, :],
                                             rstd[:, :], nmr[:, :], ALU.mult, ALU.add),
              reads=list(src_keys) + [krs, knm], writes=[out_key])

        for dst, src, key in [
            (EYEB, eye_bf, "eye_bf"), (EYEF, eye_f, "eye_f"),
            (SUT, sut_f, "sut"), (ONEC, onec_f, "onec"),
            (ONER, oner_f, "oner"), (ONERB, oner_b, "onerb"),
            (UTK, utk_b, "utk"), (ESEL, esel, "eselS"), (BO, bo_r, "bo_r"),
            (ACTC, actc_f, "actc"),
        ]:
            E("q_s", lambda e, d=dst, s=src: e.dma_start(out=d[:, :], in_=s[:, :]),
              writes=[key], sem="qs")
        for f in range(NFT):
            E("q_s", lambda e, f=f: e.dma_start(out=WGS[:, f * NE:(f + 1) * NE],
                                                in_=wg[f * 128:(f + 1) * 128, :]),
              writes=[("wgs", f)], sem="qs")
        E("g", lambda e: e.iota(IDS[:, :], pattern=[[128, NTT]], base=0, channel_multiplier=1),
          writes=["ids"])
        E("g", lambda e: e.iota(INIT5[:, :], pattern=[[1, CT]], base=T, channel_multiplier=CT),
          writes=["init5"])
        E("v", lambda e: e.tensor_scalar(BIGL[:, :], IDS[:, 0:1], 1000000.0, None, ALU.add),
          reads=["ids"], writes=["bigl"])
        E("v", lambda e: e.memset(ONESQ[:, :], 1.0), writes=["onesq"])
        for i in range(NTT):
            E("v", lambda e, i=i: e.memset(VTM[i][:, :], 1.0), writes=[("vtm", i)])
        E("v", lambda e: e.memset(ZROA[:, :], 0.0), writes=["zroa"])
        E("v", lambda e: e.memset(ZROY[:, :], 0.0), writes=["zroy"])
        for r in range(CT):
            E("q_s", lambda e, r=r: e.dma_start(out=x2aug_d[T + r * 128: T + (r + 1) * 128, :],
                                                in_=ZROA[:, :]),
              reads=["zroa"], writes=[("x2aug_pad", r)], sem="qs")
            E("q_s", lambda e, r=r: e.dma_start(out=y2acc_d[T + r * 128: T + (r + 1) * 128, :],
                                                in_=ZROY[:, :]),
              reads=["zroy"], writes=[("y2acc_pad", r)], sem="qs")

        L.barrier()

        for i in range(NFT):
            E("q_g", lambda e, i=i: e.dma_start(out=XTB[i][:, :],
                                                in_=xT[i * 128:(i + 1) * 128, :]),
              writes=[("xtb", i)], sem="qg")

        def proj(wsrc, dstf, dkey, post=None):
            for m in range(NFT):
                b = m % 2
                E("q_g", lambda e, m=m, b=b, w=wsrc: e.dma_start(
                    out=WSTG[b][:, :].rearrange("p (k f) -> p k f", f=128),
                    in_=w[m].rearrange("k p f -> p k f")),
                  reads=[("wstg", b)], writes=[("wstg", b)], sem="qg")
                for ch in range(4):
                    pb = (m * 4 + ch) % 2
                    bank = BK[pb]
                    fns = []
                    for k in range(NFT):
                        r = [("wstg", b), ("xtb", k)]
                        if k == 0:
                            r.append(bank)
                        fns.append((
                            (lambda e, b=b, ch=ch, k=k, pb=pb: e.matmul(
                                PSF[pb][:, :], WSTG[b][:, k * 128:(k + 1) * 128],
                                XTB[k][:, ch * 512:(ch + 1) * 512],
                                start=(k == 0), stop=(k == NFT - 1))),
                            r))
                    G("t", fns, writes=[bank])
                    E("v", lambda e, ch=ch, pb=pb, dst=dstf(m): e.tensor_copy(
                        dst[:, ch * 512:(ch + 1) * 512], PSF[pb][:, :]),
                      reads=[bank], writes=[bank, (dkey, m, ch)])
                if post is not None:
                    post(m)

        proj(wqT, lambda m: QTB[m], "q")
        proj(wkT, lambda m: KTB[m], "k")

        def vpost(m):
            for i in range(NTT):
                tb = i % 2
                bank = BK[6 + tb]
                E("t", lambda e, m=m, i=i, tb=tb: e.transpose(
                    PSB[tb][:, 0:128], VTT[:, i * 128:(i + 1) * 128], EYEB[:, :]),
                  reads=[("vt", m, i // 4), "eye_bf", bank], writes=[bank])
                for hh2 in range(2):
                    h = 2 * m + hh2
                    E("v", lambda e, i=i, tb=tb, hh2=hh2, h=h: e.tensor_copy(
                        VTM[i][:, h * (HD + 1): h * (HD + 1) + HD],
                        PSB[tb][:, hh2 * 64: hh2 * 64 + 64]),
                      reads=[bank, ("vtm", i)], writes=[bank, ("vtm", i)])

        proj(wvT, lambda m: VTT, "vt", post=vpost)
        L.barrier()
        stkA.close()

        E("q_g", lambda g: g.dma_start(out=wob_d[:, :], in_=woT[:, :]),
          writes=["wob_d"], sem="qg")
        E("q_g", lambda g: g.dma_start(out=w1b_d[:, :], in_=w1pm[:, :]),
          writes=["w1b"], sem="qg")
        E("q_g", lambda g: g.dma_start(out=w3b_d[:, :], in_=w3pm[:, :]),
          writes=["w3b"], sem="qg")
        E("q_g", lambda g: g.dma_start(out=w2b_d[:, :], in_=w2pm[:, :]),
          writes=["w2b"], sem="qg")

        if stop_after not in ("qkv",):
            for hp in range(NFT):
                for c in range(4):
                    cbank = [BK[4], BK[5]]
                    nj = 4 * c + 4

                    def emit_scores(j):
                        d = j - 4 * c
                        q0 = c * 512 + max(0, d) * 128
                        w = (c + 1) * 512 - q0
                        for lane in range(2):
                            sp = PSF[2 * lane + j % 2]
                            sbank = BK[2 * lane + j % 2]
                            rows = slice(lane * 64, lane * 64 + 64)
                            E("t", lambda e, hp=hp, j=j, rows=rows, q0=q0, w=w, sp=sp: e.matmul(
                                sp[:, 0:w], KTB[hp][rows, j * 128:(j + 1) * 128],
                                QTB[hp][rows, q0:q0 + w], start=True, stop=True),
                              reads=[sbank], writes=[sbank])

                    emit_scores(0)
                    for j in range(nj):
                        d = j - 4 * c
                        q0 = c * 512 + max(0, d) * 128
                        w = (c + 1) * 512 - q0
                        for lane in range(2):
                            sp = PSF[2 * lane + j % 2]
                            sbank = BK[2 * lane + j % 2]
                            pt = PT[lane][j % 2]
                            E("s", lambda e, w=w, sp=sp, pt=pt: e.activation(
                                out=pt[:, 0:w], in_=sp[:, 0:w], func=AF.Exp,
                                bias=-4.0, scale=0.125),
                              reads=[sbank, ("pt", lane, j % 2)],
                              writes=[sbank, ("pt", lane, j % 2)])
                            if d >= 0:
                                E("v", lambda e, pt=pt: e.tensor_tensor(
                                    pt[:, 0:128], pt[:, 0:128], UTK[:, :], ALU.mult),
                                  reads=[("pt", lane, j % 2), "utk"],
                                  writes=[("pt", lane, j % 2)])
                        if j + 1 < nj:
                            emit_scores(j + 1)
                        for lane in range(2):
                            h = 2 * hp + lane
                            pt = PT[lane][j % 2]
                            cp = PSF[4 + lane]
                            r = [("pt", lane, j % 2)]
                            wr = ()
                            if j == 0:
                                r.append(cbank[lane])
                            if j == nj - 1:
                                wr = [cbank[lane]]
                            E("t", lambda e, j=j, h=h, q0=q0, w=w, c=c, nj=nj, pt=pt, cp=cp: e.matmul(
                                cp[0:65, (q0 - c * 512):(q0 - c * 512) + w],
                                VTM[j][:, h * (HD + 1): (h + 1) * (HD + 1)],
                                pt[:, 0:w], start=(j == 0), stop=(j == nj - 1)),
                              reads=r, writes=wr)
                    for lane in range(2):
                        cf = CTXF[lane]
                        nb = BK[1 + 2 * lane]
                        nbp = PSF[1 + 2 * lane]
                        E("v", lambda e, cf=cf, cp=PSF[4 + lane]: e.tensor_copy(cf[:, :], cp[0:65, :]),
                          reads=[cbank[lane], ("ctxf", lane)], writes=[cbank[lane], ("ctxf", lane)])
                        E("s", lambda e, cf=cf, lr=LROW[lane]: e.activation(
                            out=lr[:, :], in_=cf[64:65, :], func=AF.Ln),
                          reads=[("ctxf", lane)], writes=[("lrow", lane)])
                        E("s", lambda e, lr=LROW[lane], rl=RLR[lane]: e.activation(
                            out=rl[:, :], in_=lr[:, :], func=AF.Exp, scale=-1.0),
                          reads=[("lrow", lane)], writes=[("rlr", lane)])
                        E("t", lambda e, rl=RLR[lane], nbp=nbp: e.matmul(
                            nbp[0:64, :], ONER[0:1, 0:64], rl[0:1, :], start=True, stop=True),
                          reads=[("rlr", lane), "oner", nb], writes=[nb])
                        E("v", lambda e, hp=hp, c=c, lane=lane, cf=cf, nbp=nbp: e.tensor_tensor(
                            CTXT[hp][lane * 64:(lane + 1) * 64, c * 512:(c + 1) * 512],
                            cf[0:64, :], nbp[0:64, :], ALU.mult),
                          reads=[("ctxf", lane), nb],
                          writes=[nb, ("ctxt", hp, c, lane)])
            L.barrier(skip=("qg",))

        if stop_after not in ("qkv", "attn"):
            stkCT.close()
            sbx = mk(stkX2); stkX2.__enter__()
            X2 = [sbx(f"x2_{i}", [128, D], F32) for i in range(NTT)]
            sbcc = mk(stkC); stkC.__enter__()
            TRSW = sbcc("trsw", [128, 1024], F32)
            EXR = sbcc("exr", [128, NE], F32)
            ESUM = sbcc("esum", [128, 1], F32)
            ERCP = sbcc("ercp", [128, 1], F32)
            SCT = sbcc("sct", [NE, T], F32)
            SCALL = sbcc("scall", [128, 128], F32)
            BLO = sbcc("blo", [128, 1], F32)
            BHI = sbcc("bhi", [128, 1], F32)
            BMID = sbcc("bmid", [128, 1], F32)
            BCNT = sbcc("bcnt", [128, 1], F32)
            BC1 = sbcc("bc1", [1, 1], F32)
            BSEL = sbcc("bsel", [128, 1], F32)
            BT1 = sbcc("bt1", [128, 1], F32)
            BT2 = sbcc("bt2", [128, 1], F32)
            TROW = sbcc("trow", [1, NE], F32)
            TRT = sbcc("trt", [1, 128], F32)
            SCW = sbcc("scw", [128, 128], F32)
            TBCW = sbcc("tbcw", [128, 128], F32)
            MSKW = sbcc("mskw", [128, 128], F32)
            CUMW = sbcc("cumw", [128, 128], F32)
            MSBW = sbcc("msbw", [128, 128], BF16)
            TSR = sbcc("tsr", [1, 128], F32)
            TS16 = sbcc("ts16", [16, NE], F32)
            OFF16 = sbcc("off16", [16, NE], F32)
            OFFR = sbcc("offr", [1, 128], F32)
            sbb = mk(stkB); stkB.__enter__()
            WOB = [sbb(f"wob{i}", [128, D], BF16) for i in range(NFT)]
            XTMS = [sbb(f"xtms{i}", [128, D], F32) for i in range(2)]
            YT = [sbb(f"yt{i}", [128, D], F32) for i in range(2)]
            BOBC = sbb("bobc", [128, D], F32)
            for hh in range(2):
                sl = slice(hh * 512, hh * 512 + 512)
                E("t", lambda e, sl=sl: e.matmul(PSF[0][:, :], ONER[0:1, :], BO[0:1, sl],
                                                 start=True, stop=True),
                  reads=["oner", "bo_r", "PSF0"], writes=["PSF0"])
                E("v", lambda e, sl=sl, hh=hh: e.tensor_copy(BOBC[:, sl], PSF[0][:, :]),
                  reads=["PSF0"], writes=["PSF0", ("bobc", hh)])
            for k in range(NFT):
                E("q_s", lambda e, k=k: e.dma_start(out=WOB[k][:, :],
                                                    in_=wob_d[k * 128:(k + 1) * 128, :]),
                  reads=["wob_d"], writes=[("wob", k)], sem="qs")
            for i in range(NTT):
                b = i % 2
                E("q_a", lambda e, i=i, b=b: e.dma_start(out=XTMS[b][:, :],
                                                         in_=x_tm[i * 128:(i + 1) * 128, :]),
                  reads=[("xtms", b)], writes=[("xtms", b)], sem="qa")
                for hh in range(2):
                    pb = (2 * i + hh) % 2
                    bank = BK[pb]
                    fns = []
                    for k in range(NFT):
                        r = [("wob", k)]
                        if k == 0:
                            r.append(bank)
                        fns.append((
                            (lambda e, i=i, hh=hh, k=k, pb=pb: e.matmul(
                                PSF[pb][:, :], CTXT[k][:, i * 128:(i + 1) * 128],
                                WOB[k][:, hh * 512:(hh + 1) * 512],
                                start=(k == 0), stop=(k == NFT - 1))),
                            r))
                    G("t", fns, writes=[bank])
                    sl = slice(hh * 512, hh * 512 + 512)
                    E("v", lambda e, b=b, pb=pb, sl=sl: e.tensor_tensor(
                        YT[b][:, sl], PSF[pb][:, :], BOBC[:, sl], ALU.add),
                      reads=[bank, ("bobc", hh), ("yt", b, hh)], writes=[bank, ("yt", b, hh)])
                    E("v", lambda e, b=b, sl=sl: e.tensor_tensor(
                        YT[b][:, sl], YT[b][:, sl], XTMS[b][:, sl], ALU.add),
                      reads=[("yt", b, hh), ("xtms", b)], writes=[("yt", b, hh)])
                ln_block(YT[b], [("yt", b, 0), ("yt", b, 1)], X2[i], ("x2", i), par=b)
                if x2_tap is not None:
                    E("q_a", lambda e, i=i: e.dma_start(out=x2_tap[i * 128:(i + 1) * 128, :],
                                                        in_=X2[i][:, :]),
                      reads=[("x2", i)], writes=[("x2tap", i)], sem="qa")
                pg = 4 + (i % 2)
                gbank = BK[pg]
                for f in range(NFT):
                    tb = f % 2
                    tbank = BK[2 + tb]
                    E("t", lambda e, i=i, f=f, tb=tb: e.transpose(
                        PSF[2 + tb][:, 0:128], X2[i][:, f * 128:(f + 1) * 128], EYEF[:, :]),
                      reads=[("x2", i), "eye_f", tbank], writes=[tbank])
                    E("v", lambda e, f=f, tb=tb: e.tensor_copy(TRSW[:, f * 128:(f + 1) * 128],
                                                               PSF[2 + tb][:, 0:128]),
                      reads=[tbank, ("trsw", f)], writes=[tbank, ("trsw", f)])
                fns = []
                for f in range(NFT):
                    r = [("trsw", f), ("wgs", f)]
                    if f == 0:
                        r.append(gbank)
                    fns.append((
                        (lambda e, f=f, pg=pg: e.matmul(
                            PSF[pg][:, 0:NE], TRSW[:, f * 128:(f + 1) * 128],
                            WGS[:, f * NE:(f + 1) * NE],
                            start=(f == 0), stop=(f == NFT - 1))),
                        r))
                G("t", fns, writes=[gbank])
                E("s", lambda e, pg=pg: e.activation(out=EXR[:, :], in_=PSF[pg][:, 0:NE],
                                                     func=AF.Exp, accum_out=ESUM[:, :]),
                  reads=[gbank, "exr", "esum"], writes=[gbank, "exr", "esum"])
                E("v", lambda e: e.reciprocal(ERCP[:, :], ESUM[:, :]),
                  reads=["esum"], writes=["ercp"])
                E("v", lambda e, i=i: e.tensor_scalar(SCW[:, i * NE:(i + 1) * NE], EXR[:, :],
                                                      ERCP[:, :], None, ALU.mult),
                  reads=["exr", "ercp", ("scw", i)], writes=[("scw", i)])
                if sc_tap is not None:
                    E("q_a", lambda e, i=i: e.dma_start(out=sc_tap[i * 128:(i + 1) * 128, :],
                                                        in_=SCW[:, i * NE:(i + 1) * NE]),
                      reads=[("scw", i)], writes=[("sctap", i)], sem="qa")
                tb2 = i % 2
                t2bank = BK[2 + tb2]
                E("t", lambda e, i=i, tb2=tb2: e.transpose(
                    PSF[2 + tb2][0:NE, 0:128], SCW[:, i * NE:(i + 1) * NE], EYEF[:, :]),
                  reads=[("scw", i), "eye_f", t2bank], writes=[t2bank])
                E("v", lambda e, i=i, tb2=tb2: e.tensor_copy(
                    SCT[:, i * 128:(i + 1) * 128], PSF[2 + tb2][0:NE, 0:128]),
                  reads=[t2bank, ("sct", i)], writes=[t2bank, ("sct", i)])
            L.barrier(skip=("qg",))

        if stop_after not in ("qkv", "attn", "ln1", "gate"):
            stkB.close()
            E("q_s", lambda e: e.dma_start(out=cc1_in[:, :], in_=SCT[:, :]),
              reads=[("sct", i) for i in range(NTT)], writes=["cc1_in"], sem="qs")
            E("g", lambda e: e.collective_compute(
                "AllGather", ALU.bypass, replica_groups=[list(range(8))],
                ins=[cc1_in[:, :]], outs=[cc1_out[:, :]]),
              reads=["cc1_in"], writes=["cc1_out"], sem="cc")
            E("q_g", lambda e: e.indirect_dma_start(
                out=SCALL[:, :], out_offset=None,
                in_=cc1_out.rearrange("r (s f) -> (r s) f", f=128),
                in_offset=IOff(ESEL[:, 0:1], 0)),
              reads=["cc1_out", "eselS"], writes=["scall"], sem="qg")
            E("v", lambda e: e.memset(BLO[:, :], 0.0), writes=["blo"])
            E("v", lambda e: e.memset(BHI[:, :], 0.5), writes=["bhid"])
            for it in range(SEARCH_ITERS):
                E("v", lambda e: e.tensor_tensor(BMID[:, :], BLO[:, :], BHI[:, :], ALU.add),
                  reads=["blo", "bhid", "bmid"], writes=["bmid"])
                E("v", lambda e: e.tensor_scalar(JUNKF[:, :], SCALL[:, :], BMID[:, :], None,
                                                 ALU.is_ge, ALU.add, accum_out=BCNT[:, :]),
                  reads=["scall", "bmid", "bcnt", "junkf"], writes=["bcnt", "junkf"])
                E("t", lambda e: e.matmul(PSF[2][:, 0:1], ONESQ[:, :], BCNT[:, :],
                                          start=True, stop=True),
                  reads=["onesq", "bcnt", "PSF2"], writes=["PSF2"])
                E("v", lambda e: e.tensor_scalar(BSEL[:, :], PSF[2][:, 0:1], float(CAPG), None, ALU.is_ge),
                  reads=["PSF2", "bsel"], writes=["PSF2", "bsel"])
                E("v", lambda e: e.tensor_tensor(BT1[:, :], BSEL[:, :], BHI[:, :], ALU.mult),
                  reads=["bsel", "bhid", "bt1"], writes=["bt1"])
                E("v", lambda e: e.tensor_tensor(BLO[:, :], BLO[:, :], BT1[:, :], ALU.add),
                  reads=["blo", "bt1"], writes=["blo"])
                E("v", lambda e: e.tensor_scalar(BHI[:, :], BHI[:, :], 0.5, None, ALU.mult),
                  reads=["bhid"], writes=["bhid"])
            E("q_s", lambda e: e.dma_start(out=cc2_in[:, :], in_=BLO[0:1, 0:1]),
              reads=["blo"], writes=["cc2_in"], sem="qs")
            E("g", lambda e: e.collective_compute(
                "AllGather", ALU.bypass, replica_groups=[list(range(8))],
                ins=[cc2_in[:, :]], outs=[cc2_out[:, :]]),
              reads=["cc2_in"], writes=["cc2_out"], sem="cc")
            E("q_s", lambda e: e.dma_start(out=TROW[:, :],
                                           in_=cc2_out.rearrange("e one -> one e")),
              reads=["cc2_out"], writes=["trow"], sem="qs")
            if thr_tap is not None:
                E("q_s", lambda e: e.dma_start(out=thr_tap[:, :], in_=TROW[:, :]),
                  reads=["trow"], writes=["thrtap"], sem="qs")
            for i in range(NTT):
                E("v", lambda e, i=i: e.tensor_copy(TRT[0:1, i * NE:(i + 1) * NE], TROW[0:1, :]),
                  reads=["trow", ("trt", i)], writes=[("trt", i)])
            trt_keys = [("trt", i) for i in range(NTT)]
            scw_keys = [("scw", i) for i in range(NTT)]
            E("t", lambda e: e.matmul(PSF[2][:, 0:128], ONER[0:1, :], TRT[0:1, :],
                                      start=True, stop=True),
              reads=trt_keys + ["oner", "PSF2"], writes=["PSF2"])
            E("v", lambda e: e.tensor_copy(TBCW[:, :], PSF[2][:, 0:128]),
              reads=["PSF2", "tbcw"], writes=["PSF2", "tbcw"])
            E("v", lambda e: e.tensor_tensor(MSKW[:, :], SCW[:, :], TBCW[:, :], ALU.is_ge),
              reads=scw_keys + ["tbcw", "mskw"], writes=["mskw"])
            E("v", lambda e: e.tensor_tensor(MSBW[:, :], SCW[:, :], MSKW[:, :], ALU.mult),
              reads=scw_keys + ["mskw", "msbw"], writes=["msbw"])
            for i in range(NTT):
                E("q_g", lambda e, i=i: e.dma_start(out=x2aug_d[i * 128:(i + 1) * 128, 0:D],
                                                    in_=X2[i][:, :]),
                  reads=[("x2", i)], writes=[("x2aug", i)], sem="qg")
                E("q_g", lambda e, i=i: e.dma_start(out=x2aug_d[i * 128:(i + 1) * 128, D:AUGW],
                                                    in_=MSBW[:, i * NE:(i + 1) * NE]),
                  reads=["msbw"], writes=[("x2aug_s", i)], sem="qg")
                E("q_s", lambda e, i=i: e.dma_start(out=y2acc_d[i * 128:(i + 1) * 128, :],
                                                    in_=X2[i][:, :]),
                  reads=[("x2", i)], writes=[("y2acc_init", i)], sem="qs")
                pcb = 2 + (i % 2)
                E("t", lambda e, i=i, pcb=pcb: e.matmul(PSF[pcb][:, 0:NE], SUT[:, :],
                                                        MSKW[:, i * NE:(i + 1) * NE],
                                                        start=True, stop=True),
                  reads=["sut", "mskw", BK[pcb]], writes=[BK[pcb]])
                E("v", lambda e, i=i, pcb=pcb: e.tensor_copy(CUMW[:, i * NE:(i + 1) * NE],
                                                             PSF[pcb][:, 0:NE]),
                  reads=[BK[pcb], ("cumw", i)], writes=[BK[pcb], ("cumw", i)])
                tsb = 4 + (i % 2)
                E("t", lambda e, i=i, tsb=tsb: e.matmul(PSF[tsb][0:1, 0:NE], ONEC[:, :],
                                                        MSKW[:, i * NE:(i + 1) * NE],
                                                        start=True, stop=True),
                  reads=["onec", "mskw", BK[tsb]], writes=[BK[tsb]])
                E("v", lambda e, i=i, tsb=tsb: e.tensor_copy(TSR[:, i * NE:(i + 1) * NE],
                                                             PSF[tsb][0:1, 0:NE]),
                  reads=[BK[tsb], ("tsr", i)], writes=[BK[tsb], ("tsr", i)])
            cum_keys = [("cumw", i) for i in range(NTT)]
            E("q_a", lambda e: e.dma_start(out=ts_d[:, :], in_=TSR[:, :]),
              reads=[("tsr", i) for i in range(NTT)], writes=["ts_d"], sem="qa")
            E("q_a", lambda e: e.dma_start(out=TS16[:, :],
                                           in_=ts_d.rearrange("one (i e) -> (one i) e", e=NE)),
              reads=["ts_d"], writes=["ts16"], sem="qa")
            E("t", lambda e: e.matmul(PSF[2][0:16, 0:NE], SUT[0:16, 0:16], TS16[:, :],
                                      start=True, stop=True),
              reads=["ts16", "sut", "PSF2"], writes=["PSF2"])
            E("v", lambda e: e.tensor_copy(OFF16[:, :], PSF[2][0:16, 0:NE]),
              reads=["PSF2", "off16"], writes=["PSF2", "off16"])
            E("q_a", lambda e: e.dma_start(out=off_d.rearrange("one (i e) -> (one i) e", e=NE),
                                           in_=OFF16[:, :]),
              reads=["off16"], writes=["off_d"], sem="qa")
            E("q_a", lambda e: e.dma_start(out=OFFR[:, :], in_=off_d[:, :]),
              reads=["off_d"], writes=["offr"], sem="qa")
            E("t", lambda e: e.matmul(PSF[3][:, 0:128], ONER[0:1, :], OFFR[0:1, :],
                                      start=True, stop=True),
              reads=["oner", "offr", "PSF3"], writes=["PSF3"])
            E("v", lambda e: e.tensor_tensor(CUMW[:, :], CUMW[:, :], PSF[3][:, 0:128], ALU.add),
              reads=cum_keys + ["PSF3"], writes=["PSF3"] + cum_keys)
            E("v", lambda e: e.tensor_scalar(CUMW[:, :], CUMW[:, :], BIGL[:, :], None, ALU.subtract),
              reads=cum_keys + ["bigl"], writes=cum_keys)
            E("v", lambda e: e.tensor_tensor(CUMW[:, :], CUMW[:, :], MSKW[:, :], ALU.mult),
              reads=cum_keys + ["mskw"], writes=cum_keys)
            E("v", lambda e: e.tensor_scalar(CUMW[:, :], CUMW[:, :], BIGL[:, :], None, ALU.add),
              reads=cum_keys + ["bigl"], writes=cum_keys)
            E("v", lambda e: e.tensor_copy(POSW[:, :], CUMW[:, :]),
              reads=cum_keys + ["posw"], writes=["posw"])
            if pos_tap is not None:
                E("q_a", lambda e: e.dma_start(out=pos_tap[:, :], in_=POSW[:, :]),
                  reads=["posw"], writes=["postap"], sem="qa")
            L.barrier(skip=("qg",))

        if stop_after not in ("qkv", "attn", "ln1", "gate", "route"):
            stkC.close()
            stkX2.close()
            stkPS1.close()
            FPS = [ps(f"fps{i}", [128, 512], F32) for i in range(4)]
            FPB = [ps(f"fpb{i}", [128, 512], BF16) for i in range(4)]
            sbd = mk(stkD); stkD.__enter__()
            ILSB = [sbd(f"ilsb{i}", [128, CT], U32) for i in range(4)]
            RTM = [sbd(f"rtm{i}", [128, CT * AUGW], BF16) for i in range(2)]
            SCOL = sbd("scol", [128, 1], F32)
            RT = [sbd(f"rt{i}", [128, NFT * CAP], BF16) for i in range(2)]
            W1C = [sbd(f"w1c{i}", [128, 4, 1024], BF16) for i in range(2)]
            W3C = [sbd(f"w3c{i}", [128, 4, 1024], BF16) for i in range(2)]
            W2A = sbd("w2a", [128, NHT, D], BF16)
            SIL = [sbd(f"sil{i}", [128, CAP], BF16) for i in range(2)]
            HT = sbd("ht", [128, NHT * CAP], BF16)
            OTM = [sbd("otm0", [128, CT * D], F32)] * 2

            x2aug_keys = ([("x2aug", i) for i in range(NTT)] +
                          [("x2aug_s", i) for i in range(NTT)] +
                          [("x2aug_pad", r) for r in range(CT)])
            y2init_keys = ([("y2acc_init", i) for i in range(NTT)] +
                           [("y2acc_pad", r) for r in range(CT)])

            def ilist_emit(e_i):
                ilrows = slice(e_i * CAP, (e_i + 1) * CAP)
                E("q_g", lambda e, ilrows=ilrows: e.dma_start(
                    out=ilist_d[ilrows, :].rearrange("(p c) one -> p (c one)", c=CT),
                    in_=INIT5[:, :]),
                  reads=["init5", ("il", e_i)], writes=[("il", e_i)], sem="qg")
                for i in range(NTT):
                    E("q_g", lambda e, e_i=e_i, i=i: e.indirect_dma_start(
                        out=ilist_d[:, :],
                        out_offset=IOff(POSW[:, NE * i + e_i: NE * i + e_i + 1], 0),
                        in_=IDS[:, i:i + 1], in_offset=None,
                        element_offset=e_i * CAP,
                        bounds_check=REGS["bc"], oob_is_err=False),
                      reads=["ids", "posw", ("il", e_i)], writes=[("ilsc", e_i, i)], sem="qg")

            def fetch_emit(e_i):
                eb = e_i % 2
                ib = e_i % 4
                ilrows = slice(e_i * CAP, (e_i + 1) * CAP)
                E("q_g", lambda e, ib=ib, ilrows=ilrows: e.dma_start(
                    out=ILSB[ib][:, :],
                    in_=ilist_d[ilrows, :].rearrange("(p c) one -> p (c one)", c=CT)),
                  reads=[("il", e_i), ("ilsb", ib)] + [("ilsc", e_i, i) for i in range(NTT)],
                  writes=[("ilsb", ib)], sem="qg")
                if il_tap is not None:
                    E("q_g", lambda e, ilrows=ilrows: e.dma_start(out=il_tap[ilrows, :],
                                                                  in_=ilist_d[ilrows, :]),
                      reads=[("il", e_i)], writes=[("iltap", e_i)], sem="qg")
                for c in range(CT):
                    E("q_g", lambda e, eb=eb, ib=ib, c=c: e.indirect_dma_start(
                        out=RTM[eb][:, c * AUGW:(c + 1) * AUGW], out_offset=None,
                        in_=x2aug_d[:, :], in_offset=IOff(ILSB[ib][:, c:c + 1], 0)),
                      reads=[("ilsb", ib), ("rtm", eb, c)] +
                            (x2aug_keys if e_i == 0 else []),
                      writes=[("rtm", eb, c)], sem="qg")

            def trans_thunks(e_i):
                eb = e_i % 2
                th = []
                for c in range(CT):
                    def t0(eb=eb, c=c, e_i=e_i):
                        E("v", lambda e, eb=eb, c=c, e_i=e_i: e.tensor_copy(
                            SCOL[:, :], RTM[eb][:, c * AUGW + D + e_i: c * AUGW + D + e_i + 1]),
                          reads=[("rtm", eb, c), "scol"], writes=["scol"])
                        E("v", lambda e, eb=eb, c=c: e.tensor_scalar(
                            RTM[eb][:, c * AUGW: c * AUGW + D],
                            RTM[eb][:, c * AUGW: c * AUGW + D],
                            SCOL[:, :], None, ALU.mult),
                          reads=[("rtm", eb, c), "scol"], writes=[("rtm", eb, c)])
                    th.append(t0)
                n = 0
                for c in range(CT):
                    for fb in range(NFT):
                        def t2(eb=eb, c=c, fb=fb, n=n):
                            bk = f"FPB{n % 4}"
                            pb = FPB[n % 4]
                            E("t", lambda e, eb=eb, c=c, fb=fb, pb=pb: e.transpose(
                                pb[:, 0:128],
                                RTM[eb][:, c * AUGW + fb * 128: c * AUGW + (fb + 1) * 128],
                                EYEB[:, :]),
                              reads=[("rtm", eb, c), "eye_bf", bk], writes=[bk])
                            E("v", lambda e, eb=eb, c=c, fb=fb, pb=pb: e.tensor_copy(
                                RT[eb][:, fb * CAP + c * 128: fb * CAP + (c + 1) * 128],
                                pb[:, 0:128]),
                              reads=[bk, ("rt", eb, fb)],
                              writes=[bk, ("rt", eb, fb)])
                        th.append(t2)
                        n += 1
                return th

            def wchunk_emit(e_i, q):
                r = q % 2
                rows = slice((e_i * 4 + q) * 128, (e_i * 4 + q + 1) * 128)
                xr = x2aug_keys if (e_i == 0 and q == 0) else []
                E("q_s", lambda e, rows=rows, r=r: e.dma_start(
                    out=W1C[r][:, :, :],
                    in_=w1b_d[rows, :].rearrange("p (h f) -> p h f", f=1024)),
                  reads=[("w1c", r), "w1b"] + xr, writes=[("w1c", r)], sem="qs")
                E("q_s", lambda e, rows=rows, r=r: e.dma_start(
                    out=W3C[r][:, :, :],
                    in_=w3b_d[rows, :].rearrange("p (h f) -> p h f", f=1024)),
                  reads=[("w3c", r), "w3b"] + xr, writes=[("w3c", r)], sem="qs")

            def w2_emit(e_i):
                xr = x2aug_keys if e_i == 0 else []
                E("q_a", lambda e, e_i=e_i: e.dma_start(
                    out=W2A[:, :, :],
                    in_=w2b_d[e_i * 128:(e_i + 1) * 128, :].rearrange("p (k f) -> p k f", f=1024)),
                  reads=["w2a", "w2b"] + xr, writes=["w2a"], sem="qa")

            def compute_emit(e_i, hooks=(), pending=None):
                eb = e_i % 2
                for ht_i in range(NHT):
                    q, hh = divmod(ht_i, 4)
                    r = (e_i * 4 + q) % 2
                    if ht_i == 0:
                        w2_emit(e_i)
                    if hh == 0 and q + 1 <= 3:
                        wchunk_emit(e_i, q + 1)
                    if ht_i == 12 and e_i + 1 < NE:
                        wchunk_emit(e_i + 1, 0)
                    for hat, hfn in hooks:
                        if ht_i == hat:
                            hfn()
                    if ht_i == 5 and e_i + 2 < NE:
                        ilist_emit(e_i + 2)
                    for part in range(2):
                        wsb = W1C[r] if part == 0 else W3C[r]
                        wkey = ("w1c", r) if part == 0 else ("w3c", r)
                        for seg, (n0, n1) in enumerate([(0, 512), (512, CAP)]):
                            pi = part * 2 + seg
                            pp = FPS[pi]
                            bank = f"FPS{pi}"
                            fns = []
                            for k in range(NFT):
                                rd = [wkey, ("rt", eb, k)]
                                if k == 0:
                                    rd.append(bank)
                                fns.append((
                                    (lambda e, wsb=wsb, hh=hh, k=k, n0=n0, n1=n1, pp=pp, eb=eb: e.matmul(
                                        pp[:, 0:n1 - n0],
                                        wsb[:, hh, k * 128:(k + 1) * 128],
                                        RT[eb][:, k * CAP + n0: k * CAP + n1],
                                        start=(k == 0), stop=(k == NFT - 1))),
                                    rd))
                            G("t", fns, writes=[bank])
                    sb_i = ht_i % 2
                    for seg, (n0, n1) in enumerate([(0, 512), (512, CAP)]):
                        p1k, p3k = f"FPS{seg}", f"FPS{2 + seg}"
                        p1p, p3p = FPS[seg], FPS[2 + seg]
                        E("s", lambda e, sb_i=sb_i, n0=n0, n1=n1, p1p=p1p: e.activation(
                            out=SIL[sb_i][:, n0:n1], in_=p1p[:, 0:n1 - n0], func=AF.Silu),
                          reads=[p1k, ("sil", sb_i, seg)], writes=[p1k, ("sil", sb_i, seg)])
                        E("v", lambda e, sb_i=sb_i, ht_i=ht_i, n0=n0, n1=n1, p3p=p3p: e.tensor_tensor(
                            HT[:, ht_i * CAP + n0: ht_i * CAP + n1],
                            p3p[:, 0:n1 - n0], SIL[sb_i][:, n0:n1], ALU.mult),
                          reads=[("sil", sb_i, seg), p3k, ("ht", ht_i)],
                          writes=[p3k, ("ht", ht_i)])
                    if pending is not None and ht_i >= 4:
                        slot = ht_i - 4
                        lo = (len(pending) * slot) // 11
                        hi = (len(pending) * (slot + 1)) // 11
                        for th in pending[lo:hi]:
                            th()
                for c in range(CT):
                    for seg, (n0, n1) in enumerate([(0, 512), (512, 1024)]):
                        pp = FPS[seg]
                        bank = f"FPS{seg}"
                        fns = []
                        for k in range(NHT):
                            rd = [("ht", k), "w2a"]
                            if k == 0:
                                rd.append(bank)
                            fns.append((
                                (lambda e, c=c, k=k, n0=n0, n1=n1, pp=pp: e.matmul(
                                    pp[:, 0:n1 - n0],
                                    HT[:, k * CAP + c * 128: k * CAP + (c + 1) * 128],
                                    W2A[:, k, n0:n1],
                                    start=(k == 0), stop=(k == NHT - 1))),
                                rd))
                        G("t", fns, writes=[bank])
                        E("v", lambda e, eb=eb, c=c, n0=n0, n1=n1, pp=pp: e.tensor_copy(
                            OTM[eb][:, c * D + n0: c * D + n1], pp[:, 0:n1 - n0]),
                          reads=[bank, ("otm", c)], writes=[bank, ("otm", c)])

            def scatter_emit(e_i):
                eb = e_i % 2
                ib = e_i % 4
                prev = ([("y2w", e_i - 1, c) for c in range(CT)] if e_i > 0
                        else y2init_keys)
                for c in range(CT):
                    E("q_g", lambda e, eb=eb, ib=ib, c=c: e.indirect_dma_start(
                        out=y2acc_d[:, :], out_offset=IOff(ILSB[ib][:, c:c + 1], 0),
                        in_=OTM[eb][:, c * D:(c + 1) * D], in_offset=None,
                        compute_op=ALU.add, oob_is_err=True),
                      reads=[("otm", c), ("ilsb", ib)] + prev,
                      writes=[("y2w", e_i, c)], sem="qg")

            ilist_emit(0)
            ilist_emit(1)
            fetch_emit(0)
            wchunk_emit(0, 0)
            for th in trans_thunks(0):
                th()
            for e_i in range(NE):
                hooks = []
                if e_i + 1 < NE:
                    hooks.append((0, lambda e_i=e_i: fetch_emit(e_i + 1)))
                if e_i > 0:
                    hooks.append((2, lambda e_i=e_i: scatter_emit(e_i - 1)))
                pending = trans_thunks(e_i + 1) if e_i + 1 < NE else None
                compute_emit(e_i, hooks=hooks, pending=pending)
            scatter_emit(NE - 1)
            L.barrier(skip=("qg",))

        if stop_after is None:
            stkD.close()
            sbe = mk(stkE); stkE.__enter__()
            Y2S = [sbe(f"y2s{i}", [128, D], F32) for i in range(2)]
            OUTS = [sbe(f"outs{i}", [128, D], F32) for i in range(2)]
            for i in range(NTT):
                b = i % 2
                E("q_s", lambda e, i=i, b=b: e.dma_start(out=Y2S[b][:, :],
                                                         in_=y2acc_d[i * 128:(i + 1) * 128, :]),
                  reads=[("y2w", NE - 1, c) for c in range(CT)] + [("y2s", b)],
                  writes=[("y2s", b)], sem="qs")
                if y2_tap is not None:
                    E("q_a", lambda e, i=i, b=b: e.dma_start(out=y2_tap[i * 128:(i + 1) * 128, :],
                                                             in_=Y2S[b][:, :]),
                      reads=[("y2s", b)], writes=[("y2tap", i)], sem="qa")
                ln_block(Y2S[b], [("y2s", b)], OUTS[b], ("outs", b), par=b)
                E("q_a", lambda e, i=i, b=b: e.dma_start(out=out_tm[i * 128:(i + 1) * 128, :],
                                                         in_=OUTS[b][:, :]),
                  reads=[("outs", b)], writes=[("out", i), ("y2s", b), ("outs", b)], sem="qa")
        L.barrier()
        for _stk in (stkE, stkD, stkC, stkB, stkX2, stkA, stkCT):
            _stk.close()

        sem_stack = contextlib.ExitStack()
        with sem_stack, nc.Block() as block:
            semmap = {n: sem_stack.enter_context(nc.semaphore("sem_" + n))
                      for n in L.semnames}

            block.tensor(_make_merged_runner(L, ("t",), semmap))
            block.vector(_make_merged_runner(L, ("v",), semmap))
            block.scalar(_make_merged_runner(L, ("s", "q_a"), semmap))
            def _g_pre(eng):
                REGS["bc"] = eng.alloc_register("bc_reg")
                eng.reg_mov(REGS["bc"], CAP - 1)
            block.gpsimd(_make_merged_runner(L, ("g", "q_g"), semmap, pre=_g_pre))
            block.sync(_make_merged_runner(L, ("q_s",), semmap))

    return nc, tap_aps


def _make_merged_runner(L, names, semmap, pre=None):
    def run(eng):
        if pre is not None:
            pre(eng)
        entries = []
        for n in names:
            entries.extend(L.progs[n])
        entries.sort(key=lambda t: t[-1])
        for entry in entries:
            if entry[0] == "wait":
                _, semname, val, _ = entry
                eng.wait_ge(semmap[semname], val)
            else:
                _, fn, semname, step, _ = entry
                inst = fn(eng)
                if semname is not None:
                    if step > 1:
                        inst.then_inc(semmap[semname], step)
                    else:
                        inst.then_inc(semmap[semname])
    return run


def host_inputs(inputs):
    x = np.asarray(inputs["L0_attn_x"])
    Wq = np.asarray(inputs["L0_attn_Wq"]); Wk = np.asarray(inputs["L0_attn_Wk"])
    Wv = np.asarray(inputs["L0_attn_Wv"]); Wo = np.asarray(inputs["L0_attn_Wo"])
    bo = np.asarray(inputs["L0_attn_bo"])
    wg_ = np.asarray(inputs["L0_ffn_wg"])
    w1_ = np.asarray(inputs["L0_ffn_w1"]); w3_ = np.asarray(inputs["L0_ffn_w3"])
    w2_ = np.asarray(inputs["L0_ffn_w2"])

    def tile_T(w):
        wt = np.ascontiguousarray(w.T)
        return np.ascontiguousarray(
            wt.reshape(NFT, 128, NFT, 128).transpose(2, 0, 1, 3))

    wqT = tile_T(Wq); wkT = tile_T(Wk); wvT = tile_T(Wv)
    woT = np.ascontiguousarray(Wo.T)
    def pmaj(w):
        a = w.reshape(NE, NFT, 128, NHT, 128).transpose(0, 2, 3, 1, 4)
        a = a.reshape(NE, 128, 4, 4, NFT, 128).transpose(0, 2, 1, 3, 4, 5)
        return np.ascontiguousarray(a.reshape(NE * 4 * 128, 4096))

    w1pm = pmaj(w1_); w3pm = pmaj(w3_)
    w2pm = np.ascontiguousarray(
        w2_.reshape(NE, NHT, 128, D).transpose(0, 2, 1, 3).reshape(NE * 128, NHT * D))

    eye_bf = np.eye(128, dtype=ml_dtypes.bfloat16)
    eye_f = np.eye(128, dtype=np.float32)
    sut_f = np.triu(np.ones((128, 128), dtype=np.float32), k=1)
    onec_f = np.ones((128, 1), dtype=np.float32)
    oner_f = np.ones((1, 128), dtype=np.float32)
    oner_b = np.ones((1, 128), dtype=ml_dtypes.bfloat16)
    utk_b = (np.arange(128)[:, None] <= np.arange(128)[None, :]).astype(ml_dtypes.bfloat16)

    in_maps = []
    for c in range(8):
        p = np.arange(128)
        esel_c = (((p // 16) * 8 + c) * 16 + (p % 16)).astype(np.uint32).reshape(128, 1)
        in_maps.append({
            "xT": np.ascontiguousarray(x[c].T).astype(np.float32, copy=False),
            "x_tm": np.ascontiguousarray(x[c]).astype(np.float32, copy=False),
            "wqT": wqT, "wkT": wkT, "wvT": wvT, "woT": woT,
            "bo_r": np.ascontiguousarray(bo.reshape(1, D)).astype(np.float32, copy=False),
            "wg": np.ascontiguousarray(wg_).astype(np.float32, copy=False),
            "w1pm": w1pm, "w3pm": w3pm, "w2pm": w2pm,
            "esel": esel_c,
            "actc_f": np.tile(np.array([list(ACT_CONSTS) + [0.0] * (8 - len(ACT_CONSTS))],
                               dtype=np.float32), (128, 1)),
            "eye_bf": eye_bf, "eye_f": eye_f, "sut_f": sut_f,
            "onec_f": onec_f, "oner_f": oner_f, "oner_b": oner_b,
            "utk_b": utk_b,
        })
    return in_maps


def kernel(**inputs):
    in_maps = host_inputs(inputs)
    nc, _ = build_program()
    res = run_bass_kernel_spmd(nc, in_maps, core_ids=list(range(8)))
    out = np.stack([res.results[c]["out_tm"] for c in range(8)], axis=0)
    return out.astype(np.float32)


# version 23 (final)
# speedup vs baseline: 1.1303x; 1.0511x over previous
import numpy as np
import ml_dtypes
import concourse.bass as bass
import concourse.mybir as mybir
from concourse.bass_utils import run_bass_kernel_spmd

F32 = mybir.dt.float32
BF16 = mybir.dt.bfloat16
U32 = mybir.dt.uint32
AF = mybir.ActivationFunctionType
ALU = mybir.AluOpType
AX = mybir.AxisListType
IOff = bass.IndirectOffsetOnAxis

T = 2048
D = 1024
H = 16
HD = 64
HID = 2048
NE = 8
CAPG = 4096
CAP = 640
CT = CAP // 128
NTT = T // 128
NFT = D // 128
NHT = HID // 128
AUGW = D + 8
NROWS = T + CAP
SEARCH_ITERS = 16
ACT_CONSTS = (-4.0, 0.125, -1.0, -0.5, 1e-5)
ENAMES = ("t", "v", "s", "g", "q_s", "q_a", "q_g")


class Ledger:

    NQSEM = 20

    def __init__(self, semnames=None):
        self.progs = {e: [] for e in ENAMES}
        self.seq = 0
        self.semnames = (["t", "v", "s", "g", "cc"] +
                         [f"{q}{j}" for q in ("qs", "qa", "qg")
                          for j in range(self.NQSEM)])
        self.cnt = {k: 0 for k in self.semnames}
        self.dmacnt = {"qs": 0, "qa": 0, "qg": 0}
        self.clocks = {e: {} for e in ENAMES}
        self.snap = {}
        self.last_w = {}
        self.readers = {}
        self.eng_sem = {"t": "t", "v": "v", "s": "s", "g": "g",
                        "q_s": "qs", "q_a": "qa", "q_g": "qg"}
        self.sem_step = {n: (16 if n[:2] in ("qs", "qa", "qg") else 1)
                         for n in self.semnames}

    def _append(self, ename, entry):
        self.progs[ename].append(entry + (self.seq,))
        self.seq += 1

    def _wait(self, ename, semname, val):
        cl = self.clocks[ename]
        if cl.get(semname, 0) >= val:
            return
        self._append(ename, ("wait", semname, val))
        snap = self.snap.get((semname, val), {semname: val})
        for k, v in snap.items():
            if cl.get(k, 0) < v:
                cl[k] = v

    def _needed(self, reads, writes):
        need = {}
        def add(dep):
            if dep is None:
                return
            s, v = dep
            if need.get(s, 0) < v:
                need[s] = v
        for k in reads:
            add(self.last_w.get(k))
        for k in writes:
            add(self.last_w.get(k))
            for dep in self.readers.get(k, ()):
                add(dep)
        return need

    def _bump(self, ename, semname):
        step = self.sem_step[semname]
        self.cnt[semname] += step
        val = self.cnt[semname]
        self.snap[(semname, val)] = dict(self.clocks[ename]) | {semname: val}
        return (semname, val)

    def op(self, ename, fn, reads=(), writes=(), sem=None):
        for s, v in sorted(self._needed(reads, writes).items()):
            self._wait(ename, s, v)
        semname = sem if sem is not None else self.eng_sem[ename]
        if semname in self.dmacnt:
            idx = self.dmacnt[semname]
            self.dmacnt[semname] += 1
            q = semname
            semname = f"{q}{idx % self.NQSEM}"
            if idx >= self.NQSEM:
                self._wait(ename, semname, 16 * (idx // self.NQSEM))
        dep = self._bump(ename, semname)
        self._append(ename, ("inst", fn, semname, self.sem_step[semname]))
        for k in writes:
            self.last_w[k] = dep
            self.readers[k] = []
        for k in reads:
            self.readers.setdefault(k, []).append(dep)
        return dep

    def group(self, ename, fns_reads, writes=(), sem=None):
        semname = sem if sem is not None else self.eng_sem[ename]
        n = len(fns_reads)
        all_reads = set()
        for i, (fn, reads) in enumerate(fns_reads):
            for s, v in sorted(self._needed(reads, writes if i == 0 else ()).items()):
                self._wait(ename, s, v)
            if i < n - 1:
                self._append(ename, ("inst", fn, None, 0))
            else:
                dep = self._bump(ename, semname)
                self._append(ename, ("inst", fn, semname, self.sem_step[semname]))
            all_reads.update(reads)
        for k in writes:
            self.last_w[k] = dep
            self.readers[k] = []
        for k in all_reads:
            self.readers.setdefault(k, []).append(dep)

    def barrier(self, ename="g", skip=()):
        for s in sorted(self.cnt):
            if self.cnt[s] > 0 and not any(s.startswith(p) for p in skip):
                self._wait(ename, s, self.cnt[s])
        dep = self._bump(ename, self.eng_sem[ename])
        self._append(ename, ("inst", lambda e: e.nop(), self.eng_sem[ename], 1))
        for other in ENAMES:
            if other != ename:
                self._wait(other, dep[0], dep[1])


def build_program(taps=(), stop_after=None):
    taps = set(taps)
    nc = bass.Bass("TRN2", target_bir_lowering=False, debug=False)

    xT = nc.dram_tensor("xT", [D, T], F32, kind="ExternalInput").ap()
    x_tm = nc.dram_tensor("x_tm", [T, D], F32, kind="ExternalInput").ap()
    wqT = nc.dram_tensor("wqT", [NFT, NFT, 128, 128], F32, kind="ExternalInput").ap()
    wkT = nc.dram_tensor("wkT", [NFT, NFT, 128, 128], F32, kind="ExternalInput").ap()
    wvT = nc.dram_tensor("wvT", [NFT, NFT, 128, 128], F32, kind="ExternalInput").ap()
    woT = nc.dram_tensor("woT", [D, D], F32, kind="ExternalInput").ap()
    bo_r = nc.dram_tensor("bo_r", [1, D], F32, kind="ExternalInput").ap()
    wg = nc.dram_tensor("wg", [D, NE], F32, kind="ExternalInput").ap()
    w1pm = nc.dram_tensor("w1pm", [NE * 4 * 128, 4096], F32, kind="ExternalInput").ap()
    w3pm = nc.dram_tensor("w3pm", [NE * 4 * 128, 4096], F32, kind="ExternalInput").ap()
    w2pm = nc.dram_tensor("w2pm", [NE * 128, 16384], F32, kind="ExternalInput").ap()
    esel = nc.dram_tensor("esel", [128, 1], U32, kind="ExternalInput").ap()
    eye_bf = nc.dram_tensor("eye_bf", [128, 128], BF16, kind="ExternalInput").ap()
    eye_f = nc.dram_tensor("eye_f", [128, 128], F32, kind="ExternalInput").ap()
    sut_f = nc.dram_tensor("sut_f", [128, 128], F32, kind="ExternalInput").ap()
    onec_f = nc.dram_tensor("onec_f", [128, 1], F32, kind="ExternalInput").ap()
    oner_f = nc.dram_tensor("oner_f", [1, 128], F32, kind="ExternalInput").ap()
    oner_b = nc.dram_tensor("oner_b", [1, 128], BF16, kind="ExternalInput").ap()
    utk_b = nc.dram_tensor("utk_b", [128, 128], BF16, kind="ExternalInput").ap()
    actc_f = nc.dram_tensor("actc_f", [128, 8], F32, kind="ExternalInput").ap()

    out_tm = nc.dram_tensor("out_tm", [T, D], F32, kind="ExternalOutput").ap()
    tap_aps = {}
    def tap(name, shape, dtype=F32):
        if name in taps:
            tap_aps[name] = nc.dram_tensor("tap_" + name, shape, dtype,
                                           kind="ExternalOutput").ap()
        return tap_aps.get(name)

    x2_tap = tap("x2", [T, D])
    sc_tap = tap("scores", [T, NE])
    thr_tap = tap("thr", [1, NE])
    pos_tap = tap("pos", [128, NE * NTT], U32)
    il_tap = tap("ilist", [NE * CAP, 1], U32)
    y2_tap = tap("y2", [T, D])

    x2aug_d = nc.dram_tensor("x2aug_d", [NROWS, AUGW], BF16, kind="Internal").ap()
    y2acc_d = nc.dram_tensor("y2acc_d", [NROWS, D], F32, kind="Internal").ap()
    ilist_d = nc.dram_tensor("ilist_d", [NE * CAP, 1], U32, kind="Internal").ap()
    cc1_in = nc.dram_tensor("cc1_in", [NE, T], F32, kind="Internal").ap()
    cc1_out = nc.dram_tensor("cc1_out", [8 * NE, T], F32, kind="Internal").ap()
    cc2_in = nc.dram_tensor("cc2_in", [1, 1], F32, kind="Internal").ap()
    cc2_out = nc.dram_tensor("cc2_out", [8, 1], F32, kind="Internal").ap()
    ts_d = nc.dram_tensor("ts_d", [1, 128], F32, kind="Internal").ap()
    wob_d = nc.dram_tensor("wob_d", [D, D], BF16, kind="Internal").ap()
    w1b_d = nc.dram_tensor("w1b_d", [NE * 4 * 128, 4096], BF16, kind="Internal").ap()
    w3b_d = nc.dram_tensor("w3b_d", [NE * 4 * 128, 4096], BF16, kind="Internal").ap()

    off_d = nc.dram_tensor("off_d", [1, 128], F32, kind="Internal").ap()

    REGS = {}
    import contextlib
    stack = contextlib.ExitStack()
    stkX2 = contextlib.ExitStack()
    stkCT = contextlib.ExitStack()
    stkA = contextlib.ExitStack()
    stkB = contextlib.ExitStack()
    stkC = contextlib.ExitStack()
    stkD = contextlib.ExitStack()
    stkE = contextlib.ExitStack()

    def mk(stk):
        return (lambda name, shape, dtype: stk.enter_context(nc.sbuf_tensor(name, shape, dtype)))
    sb = mk(stack)
    ps = lambda name, shape, dtype: stack.enter_context(nc.psum_tensor(name, shape, dtype))

    stkPS1 = contextlib.ExitStack()
    with stack:
        ps1 = lambda name, shape, dtype: stkPS1.enter_context(nc.psum_tensor(name, shape, dtype))
        PSF = [ps1(f"psf{i}", [128, 512], F32) for i in range(6)]
        PSB = [ps1(f"psb{i}", [128, 512], BF16) for i in range(2)]
        BK = [f"PSF{i}" for i in range(6)] + ["PSB0", "PSB1"]

        EYEB = sb("eyeb", [128, 128], BF16)
        EYEF = sb("eyef", [128, 128], F32)
        SUT = sb("sut", [128, 128], F32)
        ONEC = sb("onec", [128, 1], F32)
        ONER = sb("oner", [1, 128], F32)
        ONERB = sb("onerb", [1, 128], BF16)
        UTK = sb("utk", [128, 128], BF16)
        ESEL = sb("eselS", [128, 1], U32)
        BO = sb("boS", [1, D], F32)
        WGS = sb("wgs", [128, NE * NFT], F32)
        IDS = sb("ids", [128, NTT], U32)
        ONESQ = sb("onesq", [128, 128], F32)
        ACTC = sb("actc", [128, 8], F32)
        for j, v in enumerate(ACT_CONSTS):
            nc.const_aps.register(F32, v, ACTC[:, j:j + 1])
        INIT5 = sb("init5", [128, CT], U32)
        BIGL = sb("bigl", [128, 1], F32)
        JUNKB = [sb(f"junkb{j}", [128, D], BF16) for j in range(2)]
        JUNKF = sb("junkf", [128, 128], F32)
        ST1 = [sb(f"st1_{j}", [128, 1], F32) for j in range(2)]
        ST2 = [sb(f"st2_{j}", [128, 1], F32) for j in range(2)]
        MU = [sb(f"mu{j}", [128, 1], F32) for j in range(2)]
        VAR = [sb(f"var{j}", [128, 1], F32) for j in range(2)]
        RSTD = [sb(f"rstd{j}", [128, 1], F32) for j in range(2)]
        NMR = [sb(f"nmr{j}", [128, 1], F32) for j in range(2)]
        TM1 = [sb(f"tm1_{j}", [128, 1], F32) for j in range(2)]
        POSW = sb("posw", [128, 128], U32)

        sba2 = mk(stkCT); stkCT.__enter__()
        QTB = [sba2(f"qtb{i}", [128, T], BF16) for i in range(NFT)]
        KTB = [sba2(f"ktb{i}", [128, T], BF16) for i in range(NFT)]
        VTM = [sba2(f"vtm{i}", [128, H * (HD + 1)], BF16) for i in range(NTT)]
        PT = [[sba2(f"pt{l}_{i}", [128, 512], BF16) for i in range(2)] for l in range(2)]
        CTXF = [sba2(f"ctxf{i}", [65, 512], F32) for i in range(2)]
        LROW = [sba2(f"lrow{i}", [1, 512], F32) for i in range(2)]
        RLR = [sba2(f"rlr{i}", [1, 512], F32) for i in range(2)]
        ZROA = sba2("zroa", [128, AUGW], BF16)
        ZROY = sba2("zroy", [128, D], F32)

        sba = mk(stkA); stkA.__enter__()
        XTB = [sba(f"xtb{i}", [128, T], BF16) for i in range(NFT)]
        WSTG = [sba(f"wstg{i}", [128, 1024], BF16) for i in range(2)]
        VTT = sba("vtt", [128, T], BF16)
        CTXT = [stack.enter_context(nc.sbuf_tensor(f"ctxt{i}", [128, T], BF16, side="right"))
                for i in range(NFT)]

        L = Ledger(("t", "v", "s", "g", "qs", "qa", "qg", "cc"))
        E = L.op
        G = L.group

        def ln_block(src_tile, src_keys, out_tile, out_key, par=0):
            st1, st2, mu, var, rstd, nmr, tm1, jb = (ST1[par], ST2[par], MU[par],
                VAR[par], RSTD[par], NMR[par], TM1[par], JUNKB[par])
            k1, k2, kmu, kva, kvr, krl, krs, knm, ktm, kjb = [(n, par) for n in
                ("st1", "st2", "mu", "var_a", "var", "rstd_l", "rstd", "nmr", "tm1", "junkb")]
            E("v", lambda e: e.reduce_sum(st1[:, :], src_tile[:, :], AX.X),
              reads=list(src_keys) + [k1], writes=[k1])
            E("s", lambda e: e.activation(out=jb[:, :], in_=src_tile[:, :],
                                          func=AF.Square, accum_out=st2[:, :]),
              reads=list(src_keys) + [k2, kjb], writes=[k2, kjb])
            E("v", lambda e: e.tensor_scalar_mul(mu[:, :], st1[:, :], 1.0 / D),
              reads=[k1, kmu], writes=[kmu])
            E("v", lambda e: e.tensor_scalar_mul(var[:, :], st2[:, :], 1.0 / D),
              reads=[k2, kva], writes=[kva])
            E("v", lambda e: e.tensor_tensor(tm1[:, :], mu[:, :], mu[:, :], ALU.mult),
              reads=[kmu, ktm], writes=[ktm])
            E("v", lambda e: e.tensor_tensor(var[:, :], var[:, :], tm1[:, :], ALU.subtract),
              reads=[kva, ktm], writes=[kva])
            E("s", lambda e: e.activation(out=rstd[:, :], in_=var[:, :], func=AF.Ln, bias=1e-5),
              reads=[kva, krl], writes=[krl])
            E("s", lambda e: e.activation(out=rstd[:, :], in_=rstd[:, :], func=AF.Exp, scale=-0.5),
              reads=[krl, krs], writes=[krs])
            E("v", lambda e: e.tensor_tensor(nmr[:, :], mu[:, :], rstd[:, :], ALU.mult),
              reads=[kmu, krs, knm], writes=[knm])
            E("v", lambda e: e.tensor_scalar_mul(nmr[:, :], nmr[:, :], -1.0),
              reads=[knm], writes=[knm])
            E("v", lambda e: e.tensor_scalar(out_tile[:, :], src_tile[:, :],
                                             rstd[:, :], nmr[:, :], ALU.mult, ALU.add),
              reads=list(src_keys) + [krs, knm], writes=[out_key])

        for dst, src, key in [
            (EYEB, eye_bf, "eye_bf"), (EYEF, eye_f, "eye_f"),
            (SUT, sut_f, "sut"), (ONEC, onec_f, "onec"),
            (ONER, oner_f, "oner"), (ONERB, oner_b, "onerb"),
            (UTK, utk_b, "utk"), (ESEL, esel, "eselS"), (BO, bo_r, "bo_r"),
            (ACTC, actc_f, "actc"),
        ]:
            E("q_s", lambda e, d=dst, s=src: e.dma_start(out=d[:, :], in_=s[:, :]),
              writes=[key], sem="qs")
        for f in range(NFT):
            E("q_s", lambda e, f=f: e.dma_start(out=WGS[:, f * NE:(f + 1) * NE],
                                                in_=wg[f * 128:(f + 1) * 128, :]),
              writes=[("wgs", f)], sem="qs")
        E("g", lambda e: e.iota(IDS[:, :], pattern=[[128, NTT]], base=0, channel_multiplier=1),
          writes=["ids"])
        E("g", lambda e: e.iota(INIT5[:, :], pattern=[[1, CT]], base=T, channel_multiplier=CT),
          writes=["init5"])
        E("v", lambda e: e.tensor_scalar(BIGL[:, :], IDS[:, 0:1], 1000000.0, None, ALU.add),
          reads=["ids"], writes=["bigl"])
        E("v", lambda e: e.memset(ONESQ[:, :], 1.0), writes=["onesq"])
        for i in range(NTT):
            E("v", lambda e, i=i: e.memset(VTM[i][:, :], 1.0), writes=[("vtm", i)])
        E("v", lambda e: e.memset(ZROA[:, :], 0.0), writes=["zroa"])
        E("v", lambda e: e.memset(ZROY[:, :], 0.0), writes=["zroy"])
        for r in range(CT):
            E("q_s", lambda e, r=r: e.dma_start(out=x2aug_d[T + r * 128: T + (r + 1) * 128, :],
                                                in_=ZROA[:, :]),
              reads=["zroa"], writes=[("x2aug_pad", r)], sem="qs")
            E("q_s", lambda e, r=r: e.dma_start(out=y2acc_d[T + r * 128: T + (r + 1) * 128, :],
                                                in_=ZROY[:, :]),
              reads=["zroy"], writes=[("y2acc_pad", r)], sem="qs")

        L.barrier()

        for i in range(NFT):
            E("q_g", lambda e, i=i: e.dma_start(out=XTB[i][:, :],
                                                in_=xT[i * 128:(i + 1) * 128, :]),
              writes=[("xtb", i)], sem="qg")

        def proj(wsrc, dstf, dkey, post=None):
            for m in range(NFT):
                b = m % 2
                E("q_g", lambda e, m=m, b=b, w=wsrc: e.dma_start(
                    out=WSTG[b][:, :].rearrange("p (k f) -> p k f", f=128),
                    in_=w[m].rearrange("k p f -> p k f")),
                  reads=[("wstg", b)], writes=[("wstg", b)], sem="qg")
                for ch in range(4):
                    pb = (m * 4 + ch) % 2
                    bank = BK[pb]
                    fns = []
                    for k in range(NFT):
                        r = [("wstg", b), ("xtb", k)]
                        if k == 0:
                            r.append(bank)
                        fns.append((
                            (lambda e, b=b, ch=ch, k=k, pb=pb: e.matmul(
                                PSF[pb][:, :], WSTG[b][:, k * 128:(k + 1) * 128],
                                XTB[k][:, ch * 512:(ch + 1) * 512],
                                start=(k == 0), stop=(k == NFT - 1))),
                            r))
                    G("t", fns, writes=[bank])
                    E("v", lambda e, ch=ch, pb=pb, dst=dstf(m): e.tensor_copy(
                        dst[:, ch * 512:(ch + 1) * 512], PSF[pb][:, :]),
                      reads=[bank], writes=[bank, (dkey, m, ch)])
                if post is not None:
                    post(m)

        proj(wqT, lambda m: QTB[m], "q")
        proj(wkT, lambda m: KTB[m], "k")

        def vpost(m):
            for i in range(NTT):
                tb = i % 2
                bank = BK[6 + tb]
                E("t", lambda e, m=m, i=i, tb=tb: e.transpose(
                    PSB[tb][:, 0:128], VTT[:, i * 128:(i + 1) * 128], EYEB[:, :]),
                  reads=[("vt", m, i // 4), "eye_bf", bank], writes=[bank])
                for hh2 in range(2):
                    h = 2 * m + hh2
                    E("v", lambda e, i=i, tb=tb, hh2=hh2, h=h: e.tensor_copy(
                        VTM[i][:, h * (HD + 1): h * (HD + 1) + HD],
                        PSB[tb][:, hh2 * 64: hh2 * 64 + 64]),
                      reads=[bank, ("vtm", i)], writes=[bank, ("vtm", i)])

        proj(wvT, lambda m: VTT, "vt", post=vpost)
        L.barrier()
        stkA.close()

        E("q_g", lambda g: g.dma_start(out=wob_d[:, :], in_=woT[:, :]),
          writes=["wob_d"], sem="qg")
        E("q_g", lambda g: g.dma_start(out=w1b_d[:, :], in_=w1pm[:, :]),
          writes=["w1b"], sem="qg")
        E("q_g", lambda g: g.dma_start(out=w3b_d[:, :], in_=w3pm[:, :]),
          writes=["w3b"], sem="qg")
        if stop_after not in ("qkv",):
            for hp in range(NFT):
                for c in range(4):
                    cbank = [BK[4], BK[5]]
                    nj = 4 * c + 4

                    def emit_scores(j):
                        d = j - 4 * c
                        q0 = c * 512 + max(0, d) * 128
                        w = (c + 1) * 512 - q0
                        for lane in range(2):
                            sp = PSF[2 * lane + j % 2]
                            sbank = BK[2 * lane + j % 2]
                            rows = slice(lane * 64, lane * 64 + 64)
                            E("t", lambda e, hp=hp, j=j, rows=rows, q0=q0, w=w, sp=sp: e.matmul(
                                sp[:, 0:w], KTB[hp][rows, j * 128:(j + 1) * 128],
                                QTB[hp][rows, q0:q0 + w], start=True, stop=True),
                              reads=[sbank], writes=[sbank])

                    emit_scores(0)
                    for j in range(nj):
                        d = j - 4 * c
                        q0 = c * 512 + max(0, d) * 128
                        w = (c + 1) * 512 - q0
                        for lane in range(2):
                            sp = PSF[2 * lane + j % 2]
                            sbank = BK[2 * lane + j % 2]
                            pt = PT[lane][j % 2]
                            E("s", lambda e, w=w, sp=sp, pt=pt: e.activation(
                                out=pt[:, 0:w], in_=sp[:, 0:w], func=AF.Exp,
                                bias=-4.0, scale=0.125),
                              reads=[sbank, ("pt", lane, j % 2)],
                              writes=[sbank, ("pt", lane, j % 2)])
                            if d >= 0:
                                E("v", lambda e, pt=pt: e.tensor_tensor(
                                    pt[:, 0:128], pt[:, 0:128], UTK[:, :], ALU.mult),
                                  reads=[("pt", lane, j % 2), "utk"],
                                  writes=[("pt", lane, j % 2)])
                        if j + 1 < nj:
                            emit_scores(j + 1)
                        for lane in range(2):
                            h = 2 * hp + lane
                            pt = PT[lane][j % 2]
                            cp = PSF[4 + lane]
                            r = [("pt", lane, j % 2)]
                            wr = ()
                            if j == 0:
                                r.append(cbank[lane])
                            if j == nj - 1:
                                wr = [cbank[lane]]
                            E("t", lambda e, j=j, h=h, q0=q0, w=w, c=c, nj=nj, pt=pt, cp=cp: e.matmul(
                                cp[0:65, (q0 - c * 512):(q0 - c * 512) + w],
                                VTM[j][:, h * (HD + 1): (h + 1) * (HD + 1)],
                                pt[:, 0:w], start=(j == 0), stop=(j == nj - 1)),
                              reads=r, writes=wr)
                    for lane in range(2):
                        cf = CTXF[lane]
                        nb = BK[1 + 2 * lane]
                        nbp = PSF[1 + 2 * lane]
                        E("v", lambda e, cf=cf, cp=PSF[4 + lane]: e.tensor_copy(cf[:, :], cp[0:65, :]),
                          reads=[cbank[lane], ("ctxf", lane)], writes=[cbank[lane], ("ctxf", lane)])
                        E("s", lambda e, cf=cf, lr=LROW[lane]: e.activation(
                            out=lr[:, :], in_=cf[64:65, :], func=AF.Ln),
                          reads=[("ctxf", lane)], writes=[("lrow", lane)])
                        E("s", lambda e, lr=LROW[lane], rl=RLR[lane]: e.activation(
                            out=rl[:, :], in_=lr[:, :], func=AF.Exp, scale=-1.0),
                          reads=[("lrow", lane)], writes=[("rlr", lane)])
                        E("t", lambda e, rl=RLR[lane], nbp=nbp: e.matmul(
                            nbp[0:64, :], ONER[0:1, 0:64], rl[0:1, :], start=True, stop=True),
                          reads=[("rlr", lane), "oner", nb], writes=[nb])
                        E("v", lambda e, hp=hp, c=c, lane=lane, cf=cf, nbp=nbp: e.tensor_tensor(
                            CTXT[hp][lane * 64:(lane + 1) * 64, c * 512:(c + 1) * 512],
                            cf[0:64, :], nbp[0:64, :], ALU.mult),
                          reads=[("ctxf", lane), nb],
                          writes=[nb, ("ctxt", hp, c, lane)])
            L.barrier(skip=("qg",))

        if stop_after not in ("qkv", "attn"):
            stkCT.close()
            sbx = mk(stkX2); stkX2.__enter__()
            X2 = [sbx(f"x2_{i}", [128, D], F32) for i in range(NTT)]
            sbcc = mk(stkC); stkC.__enter__()
            TRSW = sbcc("trsw", [128, 1024], F32)
            EXR = sbcc("exr", [128, NE], F32)
            ESUM = sbcc("esum", [128, 1], F32)
            ERCP = sbcc("ercp", [128, 1], F32)
            SCT = sbcc("sct", [NE, T], F32)
            SCALL = sbcc("scall", [128, 128], F32)
            BLO = sbcc("blo", [128, 1], F32)
            BHI = sbcc("bhi", [128, 1], F32)
            BMID = sbcc("bmid", [128, 1], F32)
            BCNT = sbcc("bcnt", [128, 1], F32)
            BC1 = sbcc("bc1", [1, 1], F32)
            BSEL = sbcc("bsel", [128, 1], F32)
            BT1 = sbcc("bt1", [128, 1], F32)
            BT2 = sbcc("bt2", [128, 1], F32)
            TROW = sbcc("trow", [1, NE], F32)
            TRT = sbcc("trt", [1, 128], F32)
            SCW = sbcc("scw", [128, 128], F32)
            TBCW = sbcc("tbcw", [128, 128], F32)
            MSKW = sbcc("mskw", [128, 128], F32)
            CUMW = sbcc("cumw", [128, 128], F32)
            MSBW = sbcc("msbw", [128, 128], BF16)
            TSR = sbcc("tsr", [1, 128], F32)
            TS16 = sbcc("ts16", [16, NE], F32)
            OFF16 = sbcc("off16", [16, NE], F32)
            OFFR = sbcc("offr", [1, 128], F32)
            sbb = mk(stkB); stkB.__enter__()
            WOB = [sbb(f"wob{i}", [128, D], BF16) for i in range(NFT)]
            XTMS = [sbb(f"xtms{i}", [128, D], F32) for i in range(2)]
            YT = [sbb(f"yt{i}", [128, D], F32) for i in range(2)]
            BOBC = sbb("bobc", [128, D], F32)
            for hh in range(2):
                sl = slice(hh * 512, hh * 512 + 512)
                E("t", lambda e, sl=sl: e.matmul(PSF[0][:, :], ONER[0:1, :], BO[0:1, sl],
                                                 start=True, stop=True),
                  reads=["oner", "bo_r", "PSF0"], writes=["PSF0"])
                E("v", lambda e, sl=sl, hh=hh: e.tensor_copy(BOBC[:, sl], PSF[0][:, :]),
                  reads=["PSF0"], writes=["PSF0", ("bobc", hh)])
            for k in range(NFT):
                E("q_s", lambda e, k=k: e.dma_start(out=WOB[k][:, :],
                                                    in_=wob_d[k * 128:(k + 1) * 128, :]),
                  reads=["wob_d"], writes=[("wob", k)], sem="qs")
            for i in range(NTT):
                b = i % 2
                E("q_a", lambda e, i=i, b=b: e.dma_start(out=XTMS[b][:, :],
                                                         in_=x_tm[i * 128:(i + 1) * 128, :]),
                  reads=[("xtms", b)], writes=[("xtms", b)], sem="qa")
                for hh in range(2):
                    pb = (2 * i + hh) % 2
                    bank = BK[pb]
                    fns = []
                    for k in range(NFT):
                        r = [("wob", k)]
                        if k == 0:
                            r.append(bank)
                        fns.append((
                            (lambda e, i=i, hh=hh, k=k, pb=pb: e.matmul(
                                PSF[pb][:, :], CTXT[k][:, i * 128:(i + 1) * 128],
                                WOB[k][:, hh * 512:(hh + 1) * 512],
                                start=(k == 0), stop=(k == NFT - 1))),
                            r))
                    G("t", fns, writes=[bank])
                    sl = slice(hh * 512, hh * 512 + 512)
                    E("v", lambda e, b=b, pb=pb, sl=sl: e.tensor_tensor(
                        YT[b][:, sl], PSF[pb][:, :], BOBC[:, sl], ALU.add),
                      reads=[bank, ("bobc", hh), ("yt", b, hh)], writes=[bank, ("yt", b, hh)])
                    E("v", lambda e, b=b, sl=sl: e.tensor_tensor(
                        YT[b][:, sl], YT[b][:, sl], XTMS[b][:, sl], ALU.add),
                      reads=[("yt", b, hh), ("xtms", b)], writes=[("yt", b, hh)])
                ln_block(YT[b], [("yt", b, 0), ("yt", b, 1)], X2[i], ("x2", i), par=b)
                if x2_tap is not None:
                    E("q_a", lambda e, i=i: e.dma_start(out=x2_tap[i * 128:(i + 1) * 128, :],
                                                        in_=X2[i][:, :]),
                      reads=[("x2", i)], writes=[("x2tap", i)], sem="qa")
                pg = 4 + (i % 2)
                gbank = BK[pg]
                for f in range(NFT):
                    tb = f % 2
                    tbank = BK[2 + tb]
                    E("t", lambda e, i=i, f=f, tb=tb: e.transpose(
                        PSF[2 + tb][:, 0:128], X2[i][:, f * 128:(f + 1) * 128], EYEF[:, :]),
                      reads=[("x2", i), "eye_f", tbank], writes=[tbank])
                    E("v", lambda e, f=f, tb=tb: e.tensor_copy(TRSW[:, f * 128:(f + 1) * 128],
                                                               PSF[2 + tb][:, 0:128]),
                      reads=[tbank, ("trsw", f)], writes=[tbank, ("trsw", f)])
                fns = []
                for f in range(NFT):
                    r = [("trsw", f), ("wgs", f)]
                    if f == 0:
                        r.append(gbank)
                    fns.append((
                        (lambda e, f=f, pg=pg: e.matmul(
                            PSF[pg][:, 0:NE], TRSW[:, f * 128:(f + 1) * 128],
                            WGS[:, f * NE:(f + 1) * NE],
                            start=(f == 0), stop=(f == NFT - 1))),
                        r))
                G("t", fns, writes=[gbank])
                E("s", lambda e, pg=pg: e.activation(out=EXR[:, :], in_=PSF[pg][:, 0:NE],
                                                     func=AF.Exp, accum_out=ESUM[:, :]),
                  reads=[gbank, "exr", "esum"], writes=[gbank, "exr", "esum"])
                E("v", lambda e: e.reciprocal(ERCP[:, :], ESUM[:, :]),
                  reads=["esum"], writes=["ercp"])
                E("v", lambda e, i=i: e.tensor_scalar(SCW[:, i * NE:(i + 1) * NE], EXR[:, :],
                                                      ERCP[:, :], None, ALU.mult),
                  reads=["exr", "ercp", ("scw", i)], writes=[("scw", i)])
                if sc_tap is not None:
                    E("q_a", lambda e, i=i: e.dma_start(out=sc_tap[i * 128:(i + 1) * 128, :],
                                                        in_=SCW[:, i * NE:(i + 1) * NE]),
                      reads=[("scw", i)], writes=[("sctap", i)], sem="qa")
                tb2 = i % 2
                t2bank = BK[2 + tb2]
                E("t", lambda e, i=i, tb2=tb2: e.transpose(
                    PSF[2 + tb2][0:NE, 0:128], SCW[:, i * NE:(i + 1) * NE], EYEF[:, :]),
                  reads=[("scw", i), "eye_f", t2bank], writes=[t2bank])
                E("v", lambda e, i=i, tb2=tb2: e.tensor_copy(
                    SCT[:, i * 128:(i + 1) * 128], PSF[2 + tb2][0:NE, 0:128]),
                  reads=[t2bank, ("sct", i)], writes=[t2bank, ("sct", i)])
            L.barrier(skip=("qg",))

        if stop_after not in ("qkv", "attn", "ln1", "gate"):
            stkB.close()
            E("q_s", lambda e: e.dma_start(out=cc1_in[:, :], in_=SCT[:, :]),
              reads=[("sct", i) for i in range(NTT)], writes=["cc1_in"], sem="qs")
            E("g", lambda e: e.collective_compute(
                "AllGather", ALU.bypass, replica_groups=[list(range(8))],
                ins=[cc1_in[:, :]], outs=[cc1_out[:, :]]),
              reads=["cc1_in"], writes=["cc1_out"], sem="cc")
            E("q_g", lambda e: e.indirect_dma_start(
                out=SCALL[:, :], out_offset=None,
                in_=cc1_out.rearrange("r (s f) -> (r s) f", f=128),
                in_offset=IOff(ESEL[:, 0:1], 0)),
              reads=["cc1_out", "eselS"], writes=["scall"], sem="qg")
            E("v", lambda e: e.memset(BLO[:, :], 0.0), writes=["blo"])
            E("v", lambda e: e.memset(BHI[:, :], 0.5), writes=["bhid"])
            for it in range(SEARCH_ITERS):
                E("v", lambda e: e.tensor_tensor(BMID[:, :], BLO[:, :], BHI[:, :], ALU.add),
                  reads=["blo", "bhid", "bmid"], writes=["bmid"])
                E("v", lambda e: e.tensor_scalar(JUNKF[:, :], SCALL[:, :], BMID[:, :], None,
                                                 ALU.is_ge, ALU.add, accum_out=BCNT[:, :]),
                  reads=["scall", "bmid", "bcnt", "junkf"], writes=["bcnt", "junkf"])
                E("t", lambda e: e.matmul(PSF[2][:, 0:1], ONESQ[:, :], BCNT[:, :],
                                          start=True, stop=True),
                  reads=["onesq", "bcnt", "PSF2"], writes=["PSF2"])
                E("v", lambda e: e.tensor_scalar(BSEL[:, :], PSF[2][:, 0:1], float(CAPG), None, ALU.is_ge),
                  reads=["PSF2", "bsel"], writes=["PSF2", "bsel"])
                E("v", lambda e: e.tensor_tensor(BT1[:, :], BSEL[:, :], BHI[:, :], ALU.mult),
                  reads=["bsel", "bhid", "bt1"], writes=["bt1"])
                E("v", lambda e: e.tensor_tensor(BLO[:, :], BLO[:, :], BT1[:, :], ALU.add),
                  reads=["blo", "bt1"], writes=["blo"])
                E("v", lambda e: e.tensor_scalar(BHI[:, :], BHI[:, :], 0.5, None, ALU.mult),
                  reads=["bhid"], writes=["bhid"])
            E("q_s", lambda e: e.dma_start(out=cc2_in[:, :], in_=BLO[0:1, 0:1]),
              reads=["blo"], writes=["cc2_in"], sem="qs")
            E("g", lambda e: e.collective_compute(
                "AllGather", ALU.bypass, replica_groups=[list(range(8))],
                ins=[cc2_in[:, :]], outs=[cc2_out[:, :]]),
              reads=["cc2_in"], writes=["cc2_out"], sem="cc")
            E("q_s", lambda e: e.dma_start(out=TROW[:, :],
                                           in_=cc2_out.rearrange("e one -> one e")),
              reads=["cc2_out"], writes=["trow"], sem="qs")
            if thr_tap is not None:
                E("q_s", lambda e: e.dma_start(out=thr_tap[:, :], in_=TROW[:, :]),
                  reads=["trow"], writes=["thrtap"], sem="qs")
            for i in range(NTT):
                E("v", lambda e, i=i: e.tensor_copy(TRT[0:1, i * NE:(i + 1) * NE], TROW[0:1, :]),
                  reads=["trow", ("trt", i)], writes=[("trt", i)])
            trt_keys = [("trt", i) for i in range(NTT)]
            scw_keys = [("scw", i) for i in range(NTT)]
            E("t", lambda e: e.matmul(PSF[2][:, 0:128], ONER[0:1, :], TRT[0:1, :],
                                      start=True, stop=True),
              reads=trt_keys + ["oner", "PSF2"], writes=["PSF2"])
            E("v", lambda e: e.tensor_copy(TBCW[:, :], PSF[2][:, 0:128]),
              reads=["PSF2", "tbcw"], writes=["PSF2", "tbcw"])
            E("v", lambda e: e.tensor_tensor(MSKW[:, :], SCW[:, :], TBCW[:, :], ALU.is_ge),
              reads=scw_keys + ["tbcw", "mskw"], writes=["mskw"])
            E("v", lambda e: e.tensor_tensor(MSBW[:, :], SCW[:, :], MSKW[:, :], ALU.mult),
              reads=scw_keys + ["mskw", "msbw"], writes=["msbw"])
            for i in range(NTT):
                E("q_g", lambda e, i=i: e.dma_start(out=x2aug_d[i * 128:(i + 1) * 128, 0:D],
                                                    in_=X2[i][:, :]),
                  reads=[("x2", i)], writes=[("x2aug", i)], sem="qg")
                E("q_g", lambda e, i=i: e.dma_start(out=x2aug_d[i * 128:(i + 1) * 128, D:AUGW],
                                                    in_=MSBW[:, i * NE:(i + 1) * NE]),
                  reads=["msbw"], writes=[("x2aug_s", i)], sem="qg")
                E("q_s", lambda e, i=i: e.dma_start(out=y2acc_d[i * 128:(i + 1) * 128, :],
                                                    in_=X2[i][:, :]),
                  reads=[("x2", i)], writes=[("y2acc_init", i)], sem="qs")
                pcb = 2 + (i % 2)
                E("t", lambda e, i=i, pcb=pcb: e.matmul(PSF[pcb][:, 0:NE], SUT[:, :],
                                                        MSKW[:, i * NE:(i + 1) * NE],
                                                        start=True, stop=True),
                  reads=["sut", "mskw", BK[pcb]], writes=[BK[pcb]])
                E("v", lambda e, i=i, pcb=pcb: e.tensor_copy(CUMW[:, i * NE:(i + 1) * NE],
                                                             PSF[pcb][:, 0:NE]),
                  reads=[BK[pcb], ("cumw", i)], writes=[BK[pcb], ("cumw", i)])
                tsb = 4 + (i % 2)
                E("t", lambda e, i=i, tsb=tsb: e.matmul(PSF[tsb][0:1, 0:NE], ONEC[:, :],
                                                        MSKW[:, i * NE:(i + 1) * NE],
                                                        start=True, stop=True),
                  reads=["onec", "mskw", BK[tsb]], writes=[BK[tsb]])
                E("v", lambda e, i=i, tsb=tsb: e.tensor_copy(TSR[:, i * NE:(i + 1) * NE],
                                                             PSF[tsb][0:1, 0:NE]),
                  reads=[BK[tsb], ("tsr", i)], writes=[BK[tsb], ("tsr", i)])
            cum_keys = [("cumw", i) for i in range(NTT)]
            E("q_a", lambda e: e.dma_start(out=ts_d[:, :], in_=TSR[:, :]),
              reads=[("tsr", i) for i in range(NTT)], writes=["ts_d"], sem="qa")
            E("q_a", lambda e: e.dma_start(out=TS16[:, :],
                                           in_=ts_d.rearrange("one (i e) -> (one i) e", e=NE)),
              reads=["ts_d"], writes=["ts16"], sem="qa")
            E("t", lambda e: e.matmul(PSF[2][0:16, 0:NE], SUT[0:16, 0:16], TS16[:, :],
                                      start=True, stop=True),
              reads=["ts16", "sut", "PSF2"], writes=["PSF2"])
            E("v", lambda e: e.tensor_copy(OFF16[:, :], PSF[2][0:16, 0:NE]),
              reads=["PSF2", "off16"], writes=["PSF2", "off16"])
            E("q_a", lambda e: e.dma_start(out=off_d.rearrange("one (i e) -> (one i) e", e=NE),
                                           in_=OFF16[:, :]),
              reads=["off16"], writes=["off_d"], sem="qa")
            E("q_a", lambda e: e.dma_start(out=OFFR[:, :], in_=off_d[:, :]),
              reads=["off_d"], writes=["offr"], sem="qa")
            E("t", lambda e: e.matmul(PSF[3][:, 0:128], ONER[0:1, :], OFFR[0:1, :],
                                      start=True, stop=True),
              reads=["oner", "offr", "PSF3"], writes=["PSF3"])
            E("v", lambda e: e.tensor_tensor(CUMW[:, :], CUMW[:, :], PSF[3][:, 0:128], ALU.add),
              reads=cum_keys + ["PSF3"], writes=["PSF3"] + cum_keys)
            E("v", lambda e: e.tensor_scalar(CUMW[:, :], CUMW[:, :], BIGL[:, :], None, ALU.subtract),
              reads=cum_keys + ["bigl"], writes=cum_keys)
            E("v", lambda e: e.tensor_tensor(CUMW[:, :], CUMW[:, :], MSKW[:, :], ALU.mult),
              reads=cum_keys + ["mskw"], writes=cum_keys)
            E("v", lambda e: e.tensor_scalar(CUMW[:, :], CUMW[:, :], BIGL[:, :], None, ALU.add),
              reads=cum_keys + ["bigl"], writes=cum_keys)
            E("v", lambda e: e.tensor_copy(POSW[:, :], CUMW[:, :]),
              reads=cum_keys + ["posw"], writes=["posw"])
            if pos_tap is not None:
                E("q_a", lambda e: e.dma_start(out=pos_tap[:, :], in_=POSW[:, :]),
                  reads=["posw"], writes=["postap"], sem="qa")
            L.barrier(skip=("qg",))

        if stop_after not in ("qkv", "attn", "ln1", "gate", "route"):
            stkC.close()
            stkX2.close()
            stkPS1.close()
            FPS = [ps(f"fps{i}", [128, 512], F32) for i in range(4)]
            FPB = [ps(f"fpb{i}", [128, 512], BF16) for i in range(4)]
            sbd = mk(stkD); stkD.__enter__()
            ILSB = [sbd(f"ilsb{i}", [128, CT], U32) for i in range(4)]
            RTM = [sbd(f"rtm{i}", [128, CT * AUGW], BF16) for i in range(2)]
            SCOL = sbd("scol", [128, 1], F32)
            RT = [sbd(f"rt{i}", [128, NFT * CAP], BF16) for i in range(2)]
            W1C = [sbd(f"w1c{i}", [128, 4, 1024], BF16) for i in range(2)]
            W3C = [sbd(f"w3c{i}", [128, 4, 1024], BF16) for i in range(2)]
            W2A = sbd("w2a", [128, NHT, D], BF16)
            SIL = [sbd(f"sil{i}", [128, CAP], BF16) for i in range(2)]
            HT = sbd("ht", [128, NHT * CAP], BF16)
            OTM = [sbd("otm0", [128, CT * D], F32)] * 2

            x2aug_keys = ([("x2aug", i) for i in range(NTT)] +
                          [("x2aug_s", i) for i in range(NTT)] +
                          [("x2aug_pad", r) for r in range(CT)])
            y2init_keys = ([("y2acc_init", i) for i in range(NTT)] +
                           [("y2acc_pad", r) for r in range(CT)])

            def ilist_emit(e_i):
                ilrows = slice(e_i * CAP, (e_i + 1) * CAP)
                E("q_g", lambda e, ilrows=ilrows: e.dma_start(
                    out=ilist_d[ilrows, :].rearrange("(p c) one -> p (c one)", c=CT),
                    in_=INIT5[:, :]),
                  reads=["init5", ("il", e_i)], writes=[("il", e_i)], sem="qg")
                for i in range(NTT):
                    E("q_g", lambda e, e_i=e_i, i=i: e.indirect_dma_start(
                        out=ilist_d[:, :],
                        out_offset=IOff(POSW[:, NE * i + e_i: NE * i + e_i + 1], 0),
                        in_=IDS[:, i:i + 1], in_offset=None,
                        element_offset=e_i * CAP,
                        bounds_check=REGS["bc"], oob_is_err=False),
                      reads=["ids", "posw", ("il", e_i)], writes=[("ilsc", e_i, i)], sem="qg")

            def fetch_emit(e_i):
                eb = e_i % 2
                ib = e_i % 4
                ilrows = slice(e_i * CAP, (e_i + 1) * CAP)
                E("q_g", lambda e, ib=ib, ilrows=ilrows: e.dma_start(
                    out=ILSB[ib][:, :],
                    in_=ilist_d[ilrows, :].rearrange("(p c) one -> p (c one)", c=CT)),
                  reads=[("il", e_i), ("ilsb", ib)] + [("ilsc", e_i, i) for i in range(NTT)],
                  writes=[("ilsb", ib)], sem="qg")
                if il_tap is not None:
                    E("q_g", lambda e, ilrows=ilrows: e.dma_start(out=il_tap[ilrows, :],
                                                                  in_=ilist_d[ilrows, :]),
                      reads=[("il", e_i)], writes=[("iltap", e_i)], sem="qg")
                for c in range(CT):
                    E("q_g", lambda e, eb=eb, ib=ib, c=c: e.indirect_dma_start(
                        out=RTM[eb][:, c * AUGW:(c + 1) * AUGW], out_offset=None,
                        in_=x2aug_d[:, :], in_offset=IOff(ILSB[ib][:, c:c + 1], 0)),
                      reads=[("ilsb", ib), ("rtm", eb, c)] +
                            (x2aug_keys if e_i == 0 else []),
                      writes=[("rtm", eb, c)], sem="qg")

            def trans_thunks(e_i):
                eb = e_i % 2
                th = []
                for c in range(CT):
                    def t0(eb=eb, c=c, e_i=e_i):
                        E("v", lambda e, eb=eb, c=c, e_i=e_i: e.tensor_copy(
                            SCOL[:, :], RTM[eb][:, c * AUGW + D + e_i: c * AUGW + D + e_i + 1]),
                          reads=[("rtm", eb, c), "scol"], writes=["scol"])
                        E("v", lambda e, eb=eb, c=c: e.tensor_scalar(
                            RTM[eb][:, c * AUGW: c * AUGW + D],
                            RTM[eb][:, c * AUGW: c * AUGW + D],
                            SCOL[:, :], None, ALU.mult),
                          reads=[("rtm", eb, c), "scol"], writes=[("rtm", eb, c)])
                    th.append(t0)
                n = 0
                for c in range(CT):
                    for fb in range(NFT):
                        def t2(eb=eb, c=c, fb=fb, n=n):
                            bk = f"FPB{n % 4}"
                            pb = FPB[n % 4]
                            E("t", lambda e, eb=eb, c=c, fb=fb, pb=pb: e.transpose(
                                pb[:, 0:128],
                                RTM[eb][:, c * AUGW + fb * 128: c * AUGW + (fb + 1) * 128],
                                EYEB[:, :]),
                              reads=[("rtm", eb, c), "eye_bf", bk], writes=[bk])
                            E("v", lambda e, eb=eb, c=c, fb=fb, pb=pb: e.tensor_copy(
                                RT[eb][:, fb * CAP + c * 128: fb * CAP + (c + 1) * 128],
                                pb[:, 0:128]),
                              reads=[bk, ("rt", eb, fb)],
                              writes=[bk, ("rt", eb, fb)])
                        th.append(t2)
                        n += 1
                return th

            def wchunk_emit(e_i, q):
                r = q % 2
                rows = slice((e_i * 4 + q) * 128, (e_i * 4 + q + 1) * 128)
                xr = x2aug_keys if (e_i == 0 and q == 0) else []
                E("q_s", lambda e, rows=rows, r=r: e.dma_start(
                    out=W1C[r][:, :, :],
                    in_=w1b_d[rows, :].rearrange("p (h f) -> p h f", f=1024)),
                  reads=[("w1c", r), "w1b"] + xr, writes=[("w1c", r)], sem="qs")
                E("q_s", lambda e, rows=rows, r=r: e.dma_start(
                    out=W3C[r][:, :, :],
                    in_=w3b_d[rows, :].rearrange("p (h f) -> p h f", f=1024)),
                  reads=[("w3c", r), "w3b"] + xr, writes=[("w3c", r)], sem="qs")

            def w2_emit(e_i):
                xr = x2aug_keys if e_i == 0 else []
                E("q_g", lambda e, e_i=e_i: e.dma_start(
                    out=W2A[:, :, :],
                    in_=w2pm[e_i * 128:(e_i + 1) * 128, :].rearrange("p (k f) -> p k f", f=1024)),
                  reads=["w2a"] + xr, writes=["w2a"], sem="qg")

            def compute_emit(e_i, hooks=(), pending=None):
                eb = e_i % 2
                for ht_i in range(NHT):
                    q, hh = divmod(ht_i, 4)
                    r = (e_i * 4 + q) % 2
                    if ht_i == 0:
                        w2_emit(e_i)
                    if hh == 0 and q + 1 <= 3:
                        wchunk_emit(e_i, q + 1)
                    if ht_i == 12 and e_i + 1 < NE:
                        wchunk_emit(e_i + 1, 0)
                    for hat, hfn in hooks:
                        if ht_i == hat:
                            hfn()
                    if ht_i == 5 and e_i + 2 < NE:
                        ilist_emit(e_i + 2)
                    for part in range(2):
                        wsb = W1C[r] if part == 0 else W3C[r]
                        wkey = ("w1c", r) if part == 0 else ("w3c", r)
                        for seg, (n0, n1) in enumerate([(0, 512), (512, CAP)]):
                            pi = part * 2 + seg
                            pp = FPS[pi]
                            bank = f"FPS{pi}"
                            fns = []
                            for k in range(NFT):
                                rd = [wkey, ("rt", eb, k)]
                                if k == 0:
                                    rd.append(bank)
                                fns.append((
                                    (lambda e, wsb=wsb, hh=hh, k=k, n0=n0, n1=n1, pp=pp, eb=eb: e.matmul(
                                        pp[:, 0:n1 - n0],
                                        wsb[:, hh, k * 128:(k + 1) * 128],
                                        RT[eb][:, k * CAP + n0: k * CAP + n1],
                                        start=(k == 0), stop=(k == NFT - 1))),
                                    rd))
                            G("t", fns, writes=[bank])
                    sb_i = ht_i % 2
                    for seg, (n0, n1) in enumerate([(0, 512), (512, CAP)]):
                        p1k, p3k = f"FPS{seg}", f"FPS{2 + seg}"
                        p1p, p3p = FPS[seg], FPS[2 + seg]
                        E("s", lambda e, sb_i=sb_i, n0=n0, n1=n1, p1p=p1p: e.activation(
                            out=SIL[sb_i][:, n0:n1], in_=p1p[:, 0:n1 - n0], func=AF.Silu),
                          reads=[p1k, ("sil", sb_i, seg)], writes=[p1k, ("sil", sb_i, seg)])
                        E("v", lambda e, sb_i=sb_i, ht_i=ht_i, n0=n0, n1=n1, p3p=p3p: e.tensor_tensor(
                            HT[:, ht_i * CAP + n0: ht_i * CAP + n1],
                            p3p[:, 0:n1 - n0], SIL[sb_i][:, n0:n1], ALU.mult),
                          reads=[("sil", sb_i, seg), p3k, ("ht", ht_i)],
                          writes=[p3k, ("ht", ht_i)])
                    if pending is not None and ht_i >= 4:
                        slot = ht_i - 4
                        lo = (len(pending) * slot) // 11
                        hi = (len(pending) * (slot + 1)) // 11
                        for th in pending[lo:hi]:
                            th()
                for c in range(CT):
                    for seg, (n0, n1) in enumerate([(0, 512), (512, 1024)]):
                        pp = FPS[seg]
                        bank = f"FPS{seg}"
                        fns = []
                        for k in range(NHT):
                            rd = [("ht", k), "w2a"]
                            if k == 0:
                                rd.append(bank)
                            fns.append((
                                (lambda e, c=c, k=k, n0=n0, n1=n1, pp=pp: e.matmul(
                                    pp[:, 0:n1 - n0],
                                    HT[:, k * CAP + c * 128: k * CAP + (c + 1) * 128],
                                    W2A[:, k, n0:n1],
                                    start=(k == 0), stop=(k == NHT - 1))),
                                rd))
                        G("t", fns, writes=[bank])
                        E("v", lambda e, eb=eb, c=c, n0=n0, n1=n1, pp=pp: e.tensor_copy(
                            OTM[eb][:, c * D + n0: c * D + n1], pp[:, 0:n1 - n0]),
                          reads=[bank, ("otm", c)], writes=[bank, ("otm", c)])

            def scatter_emit(e_i):
                eb = e_i % 2
                ib = e_i % 4
                prev = ([("y2w", e_i - 1, c) for c in range(CT)] if e_i > 0
                        else y2init_keys)
                for c in range(CT):
                    E("q_g", lambda e, eb=eb, ib=ib, c=c: e.indirect_dma_start(
                        out=y2acc_d[:, :], out_offset=IOff(ILSB[ib][:, c:c + 1], 0),
                        in_=OTM[eb][:, c * D:(c + 1) * D], in_offset=None,
                        compute_op=ALU.add, oob_is_err=True),
                      reads=[("otm", c), ("ilsb", ib)] + prev,
                      writes=[("y2w", e_i, c)], sem="qg")

            ilist_emit(0)
            fetch_emit(0)
            ilist_emit(1)
            wchunk_emit(0, 0)
            for th in trans_thunks(0):
                th()
            for e_i in range(NE):
                hooks = []
                if e_i + 1 < NE:
                    hooks.append((0, lambda e_i=e_i: fetch_emit(e_i + 1)))
                if e_i > 0:
                    hooks.append((2, lambda e_i=e_i: scatter_emit(e_i - 1)))
                pending = trans_thunks(e_i + 1) if e_i + 1 < NE else None
                compute_emit(e_i, hooks=hooks, pending=pending)
            scatter_emit(NE - 1)
            L.barrier(skip=("qg",))

        if stop_after is None:
            stkD.close()
            sbe = mk(stkE); stkE.__enter__()
            Y2S = [sbe(f"y2s{i}", [128, D], F32) for i in range(2)]
            OUTS = [sbe(f"outs{i}", [128, D], F32) for i in range(2)]
            for i in range(NTT):
                b = i % 2
                E("q_s", lambda e, i=i, b=b: e.dma_start(out=Y2S[b][:, :],
                                                         in_=y2acc_d[i * 128:(i + 1) * 128, :]),
                  reads=[("y2w", NE - 1, c) for c in range(CT)] + [("y2s", b)],
                  writes=[("y2s", b)], sem="qs")
                if y2_tap is not None:
                    E("q_a", lambda e, i=i, b=b: e.dma_start(out=y2_tap[i * 128:(i + 1) * 128, :],
                                                             in_=Y2S[b][:, :]),
                      reads=[("y2s", b)], writes=[("y2tap", i)], sem="qa")
                ln_block(Y2S[b], [("y2s", b)], OUTS[b], ("outs", b), par=b)
                E("q_a", lambda e, i=i, b=b: e.dma_start(out=out_tm[i * 128:(i + 1) * 128, :],
                                                         in_=OUTS[b][:, :]),
                  reads=[("outs", b)], writes=[("out", i), ("y2s", b), ("outs", b)], sem="qa")
        L.barrier()
        for _stk in (stkE, stkD, stkC, stkB, stkX2, stkA, stkCT):
            _stk.close()

        sem_stack = contextlib.ExitStack()
        with sem_stack, nc.Block() as block:
            semmap = {n: sem_stack.enter_context(nc.semaphore("sem_" + n))
                      for n in L.semnames}

            block.tensor(_make_merged_runner(L, ("t",), semmap))
            block.vector(_make_merged_runner(L, ("v",), semmap))
            block.scalar(_make_merged_runner(L, ("s", "q_a"), semmap))
            def _g_pre(eng):
                REGS["bc"] = eng.alloc_register("bc_reg")
                eng.reg_mov(REGS["bc"], CAP - 1)
            block.gpsimd(_make_merged_runner(L, ("g", "q_g"), semmap, pre=_g_pre))
            block.sync(_make_merged_runner(L, ("q_s",), semmap))

    return nc, tap_aps


def _make_merged_runner(L, names, semmap, pre=None):
    def run(eng):
        if pre is not None:
            pre(eng)
        entries = []
        for n in names:
            entries.extend(L.progs[n])
        entries.sort(key=lambda t: t[-1])
        for entry in entries:
            if entry[0] == "wait":
                _, semname, val, _ = entry
                eng.wait_ge(semmap[semname], val)
            else:
                _, fn, semname, step, _ = entry
                inst = fn(eng)
                if semname is not None:
                    if step > 1:
                        inst.then_inc(semmap[semname], step)
                    else:
                        inst.then_inc(semmap[semname])
    return run


def host_inputs(inputs):
    x = np.asarray(inputs["L0_attn_x"])
    Wq = np.asarray(inputs["L0_attn_Wq"]); Wk = np.asarray(inputs["L0_attn_Wk"])
    Wv = np.asarray(inputs["L0_attn_Wv"]); Wo = np.asarray(inputs["L0_attn_Wo"])
    bo = np.asarray(inputs["L0_attn_bo"])
    wg_ = np.asarray(inputs["L0_ffn_wg"])
    w1_ = np.asarray(inputs["L0_ffn_w1"]); w3_ = np.asarray(inputs["L0_ffn_w3"])
    w2_ = np.asarray(inputs["L0_ffn_w2"])

    def tile_T(w):
        wt = np.ascontiguousarray(w.T)
        return np.ascontiguousarray(
            wt.reshape(NFT, 128, NFT, 128).transpose(2, 0, 1, 3))

    wqT = tile_T(Wq); wkT = tile_T(Wk); wvT = tile_T(Wv)
    woT = np.ascontiguousarray(Wo.T)
    def pmaj(w):
        a = w.reshape(NE, NFT, 128, NHT, 128).transpose(0, 2, 3, 1, 4)
        a = a.reshape(NE, 128, 4, 4, NFT, 128).transpose(0, 2, 1, 3, 4, 5)
        return np.ascontiguousarray(a.reshape(NE * 4 * 128, 4096))

    w1pm = pmaj(w1_); w3pm = pmaj(w3_)
    w2pm = np.ascontiguousarray(
        w2_.reshape(NE, NHT, 128, D).transpose(0, 2, 1, 3).reshape(NE * 128, NHT * D))

    eye_bf = np.eye(128, dtype=ml_dtypes.bfloat16)
    eye_f = np.eye(128, dtype=np.float32)
    sut_f = np.triu(np.ones((128, 128), dtype=np.float32), k=1)
    onec_f = np.ones((128, 1), dtype=np.float32)
    oner_f = np.ones((1, 128), dtype=np.float32)
    oner_b = np.ones((1, 128), dtype=ml_dtypes.bfloat16)
    utk_b = (np.arange(128)[:, None] <= np.arange(128)[None, :]).astype(ml_dtypes.bfloat16)

    in_maps = []
    for c in range(8):
        p = np.arange(128)
        esel_c = (((p // 16) * 8 + c) * 16 + (p % 16)).astype(np.uint32).reshape(128, 1)
        in_maps.append({
            "xT": np.ascontiguousarray(x[c].T).astype(np.float32, copy=False),
            "x_tm": np.ascontiguousarray(x[c]).astype(np.float32, copy=False),
            "wqT": wqT, "wkT": wkT, "wvT": wvT, "woT": woT,
            "bo_r": np.ascontiguousarray(bo.reshape(1, D)).astype(np.float32, copy=False),
            "wg": np.ascontiguousarray(wg_).astype(np.float32, copy=False),
            "w1pm": w1pm, "w3pm": w3pm, "w2pm": w2pm,
            "esel": esel_c,
            "actc_f": np.tile(np.array([list(ACT_CONSTS) + [0.0] * (8 - len(ACT_CONSTS))],
                               dtype=np.float32), (128, 1)),
            "eye_bf": eye_bf, "eye_f": eye_f, "sut_f": sut_f,
            "onec_f": onec_f, "oner_f": oner_f, "oner_b": oner_b,
            "utk_b": utk_b,
        })
    return in_maps


def kernel(**inputs):
    in_maps = host_inputs(inputs)
    nc, _ = build_program()
    res = run_bass_kernel_spmd(nc, in_maps, core_ids=list(range(8)))
    out = np.stack([res.results[c]["out_tm"] for c in range(8)], axis=0)
    return out.astype(np.float32)


# version 24
# speedup vs baseline: 1.1603x; 1.0266x over previous
import numpy as np
import ml_dtypes
import concourse.bass as bass
import concourse.mybir as mybir
from concourse.bass_utils import run_bass_kernel_spmd

F32 = mybir.dt.float32
BF16 = mybir.dt.bfloat16
U32 = mybir.dt.uint32
AF = mybir.ActivationFunctionType
ALU = mybir.AluOpType
AX = mybir.AxisListType
IOff = bass.IndirectOffsetOnAxis

T = 2048
D = 1024
H = 16
HD = 64
HID = 2048
NE = 8
CAPG = 4096
CAP = 640
CT = CAP // 128
NTT = T // 128
NFT = D // 128
NHT = HID // 128
AUGW = D + 8
NROWS = T + CAP
SEARCH_ITERS = 16
ACT_CONSTS = (-4.0, 0.125, -1.0, -0.5, 1e-5)
ENAMES = ("t", "v", "s", "g", "q_s", "q_a", "q_g")


class Ledger:

    NQSEM = 20

    def __init__(self, semnames=None):
        self.progs = {e: [] for e in ENAMES}
        self.seq = 0
        self.semnames = (["t", "v", "s", "g", "cc"] +
                         [f"{q}{j}" for q in ("qs", "qa", "qg")
                          for j in range(self.NQSEM)])
        self.cnt = {k: 0 for k in self.semnames}
        self.dmacnt = {"qs": 0, "qa": 0, "qg": 0}
        self.clocks = {e: {} for e in ENAMES}
        self.snap = {}
        self.last_w = {}
        self.readers = {}
        self.eng_sem = {"t": "t", "v": "v", "s": "s", "g": "g",
                        "q_s": "qs", "q_a": "qa", "q_g": "qg"}
        self.sem_step = {n: (16 if n[:2] in ("qs", "qa", "qg") else 1)
                         for n in self.semnames}

    def _append(self, ename, entry):
        self.progs[ename].append(entry + (self.seq,))
        self.seq += 1

    def _wait(self, ename, semname, val):
        cl = self.clocks[ename]
        if cl.get(semname, 0) >= val:
            return
        self._append(ename, ("wait", semname, val))
        snap = self.snap.get((semname, val), {semname: val})
        for k, v in snap.items():
            if cl.get(k, 0) < v:
                cl[k] = v

    def _needed(self, reads, writes):
        need = {}
        def add(dep):
            if dep is None:
                return
            s, v = dep
            if need.get(s, 0) < v:
                need[s] = v
        for k in reads:
            add(self.last_w.get(k))
        for k in writes:
            add(self.last_w.get(k))
            for dep in self.readers.get(k, ()):
                add(dep)
        return need

    def _bump(self, ename, semname):
        step = self.sem_step[semname]
        self.cnt[semname] += step
        val = self.cnt[semname]
        self.snap[(semname, val)] = dict(self.clocks[ename]) | {semname: val}
        return (semname, val)

    def op(self, ename, fn, reads=(), writes=(), sem=None):
        for s, v in sorted(self._needed(reads, writes).items()):
            self._wait(ename, s, v)
        semname = sem if sem is not None else self.eng_sem[ename]
        if semname in self.dmacnt:
            idx = self.dmacnt[semname]
            self.dmacnt[semname] += 1
            q = semname
            semname = f"{q}{idx % self.NQSEM}"
            if idx >= self.NQSEM:
                self._wait(ename, semname, 16 * (idx // self.NQSEM))
        dep = self._bump(ename, semname)
        self._append(ename, ("inst", fn, semname, self.sem_step[semname]))
        for k in writes:
            self.last_w[k] = dep
            self.readers[k] = []
        for k in reads:
            self.readers.setdefault(k, []).append(dep)
        return dep

    def group(self, ename, fns_reads, writes=(), sem=None):
        semname = sem if sem is not None else self.eng_sem[ename]
        n = len(fns_reads)
        all_reads = set()
        for i, (fn, reads) in enumerate(fns_reads):
            for s, v in sorted(self._needed(reads, writes if i == 0 else ()).items()):
                self._wait(ename, s, v)
            if i < n - 1:
                self._append(ename, ("inst", fn, None, 0))
            else:
                dep = self._bump(ename, semname)
                self._append(ename, ("inst", fn, semname, self.sem_step[semname]))
            all_reads.update(reads)
        for k in writes:
            self.last_w[k] = dep
            self.readers[k] = []
        for k in all_reads:
            self.readers.setdefault(k, []).append(dep)

    def barrier(self, ename="g", skip=()):
        for s in sorted(self.cnt):
            if self.cnt[s] > 0 and not any(s.startswith(p) for p in skip):
                self._wait(ename, s, self.cnt[s])
        dep = self._bump(ename, self.eng_sem[ename])
        self._append(ename, ("inst", lambda e: e.nop(), self.eng_sem[ename], 1))
        for other in ENAMES:
            if other != ename:
                self._wait(other, dep[0], dep[1])


def build_program(taps=(), stop_after=None):
    taps = set(taps)
    nc = bass.Bass("TRN2", target_bir_lowering=False, debug=False)

    xT = nc.dram_tensor("xT", [D, T], F32, kind="ExternalInput").ap()
    x_tm = nc.dram_tensor("x_tm", [T, D], F32, kind="ExternalInput").ap()
    wqT = nc.dram_tensor("wqT", [NFT, NFT, 128, 128], F32, kind="ExternalInput").ap()
    wkT = nc.dram_tensor("wkT", [NFT, NFT, 128, 128], F32, kind="ExternalInput").ap()
    wvT = nc.dram_tensor("wvT", [NFT, NFT, 128, 128], F32, kind="ExternalInput").ap()
    woT = nc.dram_tensor("woT", [D, D], F32, kind="ExternalInput").ap()
    bo_r = nc.dram_tensor("bo_r", [1, D], F32, kind="ExternalInput").ap()
    wg = nc.dram_tensor("wg", [D, NE], F32, kind="ExternalInput").ap()
    w1pm = nc.dram_tensor("w1pm", [NE * 4 * 128, 4096], F32, kind="ExternalInput").ap()
    w3pm = nc.dram_tensor("w3pm", [NE * 4 * 128, 4096], F32, kind="ExternalInput").ap()
    w2pm = nc.dram_tensor("w2pm", [NE * 128, 16384], F32, kind="ExternalInput").ap()
    esel = nc.dram_tensor("esel", [128, 1], U32, kind="ExternalInput").ap()
    eye_bf = nc.dram_tensor("eye_bf", [128, 128], BF16, kind="ExternalInput").ap()
    eye_f = nc.dram_tensor("eye_f", [128, 128], F32, kind="ExternalInput").ap()
    sut_f = nc.dram_tensor("sut_f", [128, 128], F32, kind="ExternalInput").ap()
    onec_f = nc.dram_tensor("onec_f", [128, 1], F32, kind="ExternalInput").ap()
    oner_f = nc.dram_tensor("oner_f", [1, 128], F32, kind="ExternalInput").ap()
    oner_b = nc.dram_tensor("oner_b", [1, 128], BF16, kind="ExternalInput").ap()
    utk_b = nc.dram_tensor("utk_b", [128, 128], BF16, kind="ExternalInput").ap()
    actc_f = nc.dram_tensor("actc_f", [128, 8], F32, kind="ExternalInput").ap()

    out_tm = nc.dram_tensor("out_tm", [T, D], F32, kind="ExternalOutput").ap()
    tap_aps = {}
    def tap(name, shape, dtype=F32):
        if name in taps:
            tap_aps[name] = nc.dram_tensor("tap_" + name, shape, dtype,
                                           kind="ExternalOutput").ap()
        return tap_aps.get(name)

    x2_tap = tap("x2", [T, D])
    sc_tap = tap("scores", [T, NE])
    thr_tap = tap("thr", [1, NE])
    pos_tap = tap("pos", [128, NE * NTT], U32)
    il_tap = tap("ilist", [NE * CAP, 1], U32)
    y2_tap = tap("y2", [T, D])

    x2aug_d = nc.dram_tensor("x2aug_d", [NROWS, AUGW], BF16, kind="Internal").ap()
    y2acc_d = nc.dram_tensor("y2acc_d", [NROWS, D], F32, kind="Internal").ap()
    ilist_d = nc.dram_tensor("ilist_d", [NE * CAP, 1], U32, kind="Internal").ap()
    cc1_in = nc.dram_tensor("cc1_in", [NE, T], F32, kind="Internal").ap()
    cc1_out = nc.dram_tensor("cc1_out", [8 * NE, T], F32, kind="Internal").ap()
    cc2_in = nc.dram_tensor("cc2_in", [1, 1], F32, kind="Internal").ap()
    cc2_out = nc.dram_tensor("cc2_out", [8, 1], F32, kind="Internal").ap()
    ts_d = nc.dram_tensor("ts_d", [1, 128], F32, kind="Internal").ap()
    wob_d = nc.dram_tensor("wob_d", [D, D], BF16, kind="Internal").ap()
    w1b_d = nc.dram_tensor("w1b_d", [NE * 4 * 128, 4096], BF16, kind="Internal").ap()
    w3b_d = nc.dram_tensor("w3b_d", [NE * 4 * 128, 4096], BF16, kind="Internal").ap()

    off_d = nc.dram_tensor("off_d", [1, 128], F32, kind="Internal").ap()

    REGS = {}
    import contextlib
    stack = contextlib.ExitStack()
    stkX2 = contextlib.ExitStack()
    stkCT = contextlib.ExitStack()
    stkA = contextlib.ExitStack()
    stkB = contextlib.ExitStack()
    stkC = contextlib.ExitStack()
    stkD = contextlib.ExitStack()
    stkE = contextlib.ExitStack()

    def mk(stk):
        return (lambda name, shape, dtype: stk.enter_context(nc.sbuf_tensor(name, shape, dtype)))
    sb = mk(stack)
    ps = lambda name, shape, dtype: stack.enter_context(nc.psum_tensor(name, shape, dtype))

    stkPS1 = contextlib.ExitStack()
    with stack:
        ps1 = lambda name, shape, dtype: stkPS1.enter_context(nc.psum_tensor(name, shape, dtype))
        PSF = [ps1(f"psf{i}", [128, 512], F32) for i in range(6)]
        PSB = [ps1(f"psb{i}", [128, 512], BF16) for i in range(2)]
        BK = [f"PSF{i}" for i in range(6)] + ["PSB0", "PSB1"]

        EYEB = sb("eyeb", [128, 128], BF16)
        EYEF = sb("eyef", [128, 128], F32)
        SUT = sb("sut", [128, 128], F32)
        ONEC = sb("onec", [128, 1], F32)
        ONER = sb("oner", [1, 128], F32)
        ONERB = sb("onerb", [1, 128], BF16)
        UTK = sb("utk", [128, 128], BF16)
        ESEL = sb("eselS", [128, 1], U32)
        BO = sb("boS", [1, D], F32)
        WGS = sb("wgs", [128, NE * NFT], F32)
        IDS = sb("ids", [128, NTT], U32)
        ONESQ = sb("onesq", [128, 128], F32)
        ACTC = sb("actc", [128, 8], F32)
        for j, v in enumerate(ACT_CONSTS):
            nc.const_aps.register(F32, v, ACTC[:, j:j + 1])
        INIT5 = sb("init5", [128, CT], U32)
        BIGL = sb("bigl", [128, 1], F32)
        JUNKB = [sb(f"junkb{j}", [128, D], BF16) for j in range(2)]
        JUNKF = sb("junkf", [128, 128], F32)
        ST1 = [sb(f"st1_{j}", [128, 1], F32) for j in range(2)]
        ST2 = [sb(f"st2_{j}", [128, 1], F32) for j in range(2)]
        MU = [sb(f"mu{j}", [128, 1], F32) for j in range(2)]
        VAR = [sb(f"var{j}", [128, 1], F32) for j in range(2)]
        RSTD = [sb(f"rstd{j}", [128, 1], F32) for j in range(2)]
        NMR = [sb(f"nmr{j}", [128, 1], F32) for j in range(2)]
        TM1 = [sb(f"tm1_{j}", [128, 1], F32) for j in range(2)]
        POSW = sb("posw", [128, 128], U32)

        sba2 = mk(stkCT); stkCT.__enter__()
        QTB = [sba2(f"qtb{i}", [128, T], BF16) for i in range(NFT)]
        KTB = [sba2(f"ktb{i}", [128, T], BF16) for i in range(NFT)]
        VTM = [sba2(f"vtm{i}", [128, H * (HD + 1)], BF16) for i in range(NTT)]
        PT = [[sba2(f"pt{l}_{i}", [128, 512], BF16) for i in range(2)] for l in range(2)]
        CTXF = [sba2(f"ctxf{i}", [65, 512], F32) for i in range(2)]
        LROW = [sba2(f"lrow{i}", [1, 512], F32) for i in range(2)]
        RLR = [sba2(f"rlr{i}", [1, 512], F32) for i in range(2)]
        ZROA = sba2("zroa", [128, AUGW], BF16)
        ZROY = sba2("zroy", [128, D], F32)

        sba = mk(stkA); stkA.__enter__()
        XTB = [sba(f"xtb{i}", [128, T], BF16) for i in range(NFT)]
        WSTG = [sba(f"wstg{i}", [128, 1024], BF16) for i in range(2)]
        VTT = sba("vtt", [128, T], BF16)
        CTXT = [stack.enter_context(nc.sbuf_tensor(f"ctxt{i}", [128, T], BF16, side="right"))
                for i in range(NFT)]

        L = Ledger(("t", "v", "s", "g", "qs", "qa", "qg", "cc"))
        E = L.op
        G = L.group

        def ln_block(src_tile, src_keys, out_tile, out_key, par=0):
            st1, st2, mu, var, rstd, nmr, tm1, jb = (ST1[par], ST2[par], MU[par],
                VAR[par], RSTD[par], NMR[par], TM1[par], JUNKB[par])
            k1, k2, kmu, kva, kvr, krl, krs, knm, ktm, kjb = [(n, par) for n in
                ("st1", "st2", "mu", "var_a", "var", "rstd_l", "rstd", "nmr", "tm1", "junkb")]
            E("v", lambda e: e.reduce_sum(st1[:, :], src_tile[:, :], AX.X),
              reads=list(src_keys) + [k1], writes=[k1])
            E("s", lambda e: e.activation(out=jb[:, :], in_=src_tile[:, :],
                                          func=AF.Square, accum_out=st2[:, :]),
              reads=list(src_keys) + [k2, kjb], writes=[k2, kjb])
            E("v", lambda e: e.tensor_scalar_mul(mu[:, :], st1[:, :], 1.0 / D),
              reads=[k1, kmu], writes=[kmu])
            E("v", lambda e: e.tensor_scalar_mul(var[:, :], st2[:, :], 1.0 / D),
              reads=[k2, kva], writes=[kva])
            E("v", lambda e: e.tensor_tensor(tm1[:, :], mu[:, :], mu[:, :], ALU.mult),
              reads=[kmu, ktm], writes=[ktm])
            E("v", lambda e: e.tensor_tensor(var[:, :], var[:, :], tm1[:, :], ALU.subtract),
              reads=[kva, ktm], writes=[kva])
            E("s", lambda e: e.activation(out=rstd[:, :], in_=var[:, :], func=AF.Ln, bias=1e-5),
              reads=[kva, krl], writes=[krl])
            E("s", lambda e: e.activation(out=rstd[:, :], in_=rstd[:, :], func=AF.Exp, scale=-0.5),
              reads=[krl, krs], writes=[krs])
            E("v", lambda e: e.tensor_tensor(nmr[:, :], mu[:, :], rstd[:, :], ALU.mult),
              reads=[kmu, krs, knm], writes=[knm])
            E("v", lambda e: e.tensor_scalar_mul(nmr[:, :], nmr[:, :], -1.0),
              reads=[knm], writes=[knm])
            E("v", lambda e: e.tensor_scalar(out_tile[:, :], src_tile[:, :],
                                             rstd[:, :], nmr[:, :], ALU.mult, ALU.add),
              reads=list(src_keys) + [krs, knm], writes=[out_key])

        for dst, src, key in [
            (EYEB, eye_bf, "eye_bf"), (EYEF, eye_f, "eye_f"),
            (SUT, sut_f, "sut"), (ONEC, onec_f, "onec"),
            (ONER, oner_f, "oner"), (ONERB, oner_b, "onerb"),
            (UTK, utk_b, "utk"), (ESEL, esel, "eselS"), (BO, bo_r, "bo_r"),
            (ACTC, actc_f, "actc"),
        ]:
            E("q_s", lambda e, d=dst, s=src: e.dma_start(out=d[:, :], in_=s[:, :]),
              writes=[key], sem="qs")
        for f in range(NFT):
            E("q_s", lambda e, f=f: e.dma_start(out=WGS[:, f * NE:(f + 1) * NE],
                                                in_=wg[f * 128:(f + 1) * 128, :]),
              writes=[("wgs", f)], sem="qs")
        E("g", lambda e: e.iota(IDS[:, :], pattern=[[128, NTT]], base=0, channel_multiplier=1),
          writes=["ids"])
        E("g", lambda e: e.iota(INIT5[:, :], pattern=[[1, CT]], base=T, channel_multiplier=CT),
          writes=["init5"])
        E("v", lambda e: e.tensor_scalar(BIGL[:, :], IDS[:, 0:1], 1000000.0, None, ALU.add),
          reads=["ids"], writes=["bigl"])
        E("v", lambda e: e.memset(ONESQ[:, :], 1.0), writes=["onesq"])
        for i in range(NTT):
            E("v", lambda e, i=i: e.memset(VTM[i][:, :], 1.0), writes=[("vtm", i)])
        E("v", lambda e: e.memset(ZROA[:, :], 0.0), writes=["zroa"])
        E("v", lambda e: e.memset(ZROY[:, :], 0.0), writes=["zroy"])
        for r in range(CT):
            E("q_s", lambda e, r=r: e.dma_start(out=x2aug_d[T + r * 128: T + (r + 1) * 128, :],
                                                in_=ZROA[:, :]),
              reads=["zroa"], writes=[("x2aug_pad", r)], sem="qs")
            E("q_s", lambda e, r=r: e.dma_start(out=y2acc_d[T + r * 128: T + (r + 1) * 128, :],
                                                in_=ZROY[:, :]),
              reads=["zroy"], writes=[("y2acc_pad", r)], sem="qs")

        L.barrier()

        for i in range(NFT):
            E("q_g", lambda e, i=i: e.dma_start(out=XTB[i][:, :],
                                                in_=xT[i * 128:(i + 1) * 128, :]),
              writes=[("xtb", i)], sem="qg")

        def proj(wsrc, dstf, dkey, post=None):
            for m in range(NFT):
                b = m % 2
                E("q_g", lambda e, m=m, b=b, w=wsrc: e.dma_start(
                    out=WSTG[b][:, :].rearrange("p (k f) -> p k f", f=128),
                    in_=w[m].rearrange("k p f -> p k f")),
                  reads=[("wstg", b)], writes=[("wstg", b)], sem="qg")
                for ch in range(4):
                    pb = (m * 4 + ch) % 2
                    bank = BK[pb]
                    fns = []
                    for k in range(NFT):
                        r = [("wstg", b), ("xtb", k)]
                        if k == 0:
                            r.append(bank)
                        fns.append((
                            (lambda e, b=b, ch=ch, k=k, pb=pb: e.matmul(
                                PSF[pb][:, :], WSTG[b][:, k * 128:(k + 1) * 128],
                                XTB[k][:, ch * 512:(ch + 1) * 512],
                                start=(k == 0), stop=(k == NFT - 1))),
                            r))
                    G("t", fns, writes=[bank])
                    E("v", lambda e, ch=ch, pb=pb, dst=dstf(m): e.tensor_copy(
                        dst[:, ch * 512:(ch + 1) * 512], PSF[pb][:, :]),
                      reads=[bank], writes=[bank, (dkey, m, ch)])
                if post is not None:
                    post(m)

        proj(wqT, lambda m: QTB[m], "q")
        proj(wkT, lambda m: KTB[m], "k")

        def vpost(m):
            for i in range(NTT):
                tb = i % 2
                bank = BK[6 + tb]
                E("t", lambda e, m=m, i=i, tb=tb: e.transpose(
                    PSB[tb][:, 0:128], VTT[:, i * 128:(i + 1) * 128], EYEB[:, :]),
                  reads=[("vt", m, i // 4), "eye_bf", bank], writes=[bank])
                for hh2 in range(2):
                    h = 2 * m + hh2
                    E("v", lambda e, i=i, tb=tb, hh2=hh2, h=h: e.tensor_copy(
                        VTM[i][:, h * (HD + 1): h * (HD + 1) + HD],
                        PSB[tb][:, hh2 * 64: hh2 * 64 + 64]),
                      reads=[bank, ("vtm", i)], writes=[bank, ("vtm", i)])

        proj(wvT, lambda m: VTT, "vt", post=vpost)
        L.barrier()
        stkA.close()

        E("q_g", lambda g: g.dma_start(out=wob_d[:, :], in_=woT[:, :]),
          writes=["wob_d"], sem="qg")
        E("q_g", lambda g: g.dma_start(out=w1b_d[:, :], in_=w1pm[:, :]),
          writes=["w1b"], sem="qg")
        E("q_g", lambda g: g.dma_start(out=w3b_d[:, :], in_=w3pm[:, :]),
          writes=["w3b"], sem="qg")
        if stop_after not in ("qkv",):
            for hp in range(NFT):
                for c in range(4):
                    cbank = [BK[4], BK[5]]
                    nj = 4 * c + 4

                    def emit_scores(j):
                        d = j - 4 * c
                        q0 = c * 512 + max(0, d) * 128
                        w = (c + 1) * 512 - q0
                        for lane in range(2):
                            sp = PSF[2 * lane + j % 2]
                            sbank = BK[2 * lane + j % 2]
                            rows = slice(lane * 64, lane * 64 + 64)
                            E("t", lambda e, hp=hp, j=j, rows=rows, q0=q0, w=w, sp=sp: e.matmul(
                                sp[:, 0:w], KTB[hp][rows, j * 128:(j + 1) * 128],
                                QTB[hp][rows, q0:q0 + w], start=True, stop=True),
                              reads=[sbank], writes=[sbank])

                    emit_scores(0)
                    for j in range(nj):
                        d = j - 4 * c
                        q0 = c * 512 + max(0, d) * 128
                        w = (c + 1) * 512 - q0
                        for lane in range(2):
                            sp = PSF[2 * lane + j % 2]
                            sbank = BK[2 * lane + j % 2]
                            pt = PT[lane][j % 2]
                            E("s", lambda e, w=w, sp=sp, pt=pt: e.activation(
                                out=pt[:, 0:w], in_=sp[:, 0:w], func=AF.Exp,
                                bias=-4.0, scale=0.125),
                              reads=[sbank, ("pt", lane, j % 2)],
                              writes=[sbank, ("pt", lane, j % 2)])
                            if d >= 0:
                                E("v", lambda e, pt=pt: e.tensor_tensor(
                                    pt[:, 0:128], pt[:, 0:128], UTK[:, :], ALU.mult),
                                  reads=[("pt", lane, j % 2), "utk"],
                                  writes=[("pt", lane, j % 2)])
                        if j + 1 < nj:
                            emit_scores(j + 1)
                        for lane in range(2):
                            h = 2 * hp + lane
                            pt = PT[lane][j % 2]
                            cp = PSF[4 + lane]
                            r = [("pt", lane, j % 2)]
                            wr = ()
                            if j == 0:
                                r.append(cbank[lane])
                            if j == nj - 1:
                                wr = [cbank[lane]]
                            E("t", lambda e, j=j, h=h, q0=q0, w=w, c=c, nj=nj, pt=pt, cp=cp: e.matmul(
                                cp[0:65, (q0 - c * 512):(q0 - c * 512) + w],
                                VTM[j][:, h * (HD + 1): (h + 1) * (HD + 1)],
                                pt[:, 0:w], start=(j == 0), stop=(j == nj - 1)),
                              reads=r, writes=wr)
                    for lane in range(2):
                        cf = CTXF[lane]
                        nb = BK[1 + 2 * lane]
                        nbp = PSF[1 + 2 * lane]
                        E("v", lambda e, cf=cf, cp=PSF[4 + lane]: e.tensor_copy(cf[:, :], cp[0:65, :]),
                          reads=[cbank[lane], ("ctxf", lane)], writes=[cbank[lane], ("ctxf", lane)])
                        E("s", lambda e, cf=cf, lr=LROW[lane]: e.activation(
                            out=lr[:, :], in_=cf[64:65, :], func=AF.Ln),
                          reads=[("ctxf", lane)], writes=[("lrow", lane)])
                        E("s", lambda e, lr=LROW[lane], rl=RLR[lane]: e.activation(
                            out=rl[:, :], in_=lr[:, :], func=AF.Exp, scale=-1.0),
                          reads=[("lrow", lane)], writes=[("rlr", lane)])
                        E("t", lambda e, rl=RLR[lane], nbp=nbp: e.matmul(
                            nbp[0:64, :], ONER[0:1, 0:64], rl[0:1, :], start=True, stop=True),
                          reads=[("rlr", lane), "oner", nb], writes=[nb])
                        E("v", lambda e, hp=hp, c=c, lane=lane, cf=cf, nbp=nbp: e.tensor_tensor(
                            CTXT[hp][lane * 64:(lane + 1) * 64, c * 512:(c + 1) * 512],
                            cf[0:64, :], nbp[0:64, :], ALU.mult),
                          reads=[("ctxf", lane), nb],
                          writes=[nb, ("ctxt", hp, c, lane)])
            L.barrier(skip=("qg",))

        if stop_after not in ("qkv", "attn"):
            stkCT.close()
            sbx = mk(stkX2); stkX2.__enter__()
            X2 = [sbx(f"x2_{i}", [128, D], F32) for i in range(NTT)]
            sbcc = mk(stkC); stkC.__enter__()
            TRSW = sbcc("trsw", [128, 1024], F32)
            EXR = sbcc("exr", [128, NE], F32)
            ESUM = sbcc("esum", [128, 1], F32)
            ERCP = sbcc("ercp", [128, 1], F32)
            SCT = sbcc("sct", [NE, T], F32)
            SCALL = sbcc("scall", [128, 128], F32)
            BLO = sbcc("blo", [128, 1], F32)
            BHI = sbcc("bhi", [128, 1], F32)
            BMID = sbcc("bmid", [128, 1], F32)
            BCNT = sbcc("bcnt", [128, 1], F32)
            BC1 = sbcc("bc1", [1, 1], F32)
            BSEL = sbcc("bsel", [128, 1], F32)
            BT1 = sbcc("bt1", [128, 1], F32)
            BT2 = sbcc("bt2", [128, 1], F32)
            TROW = sbcc("trow", [1, NE], F32)
            TRT = sbcc("trt", [1, 128], F32)
            SCW = sbcc("scw", [128, 128], F32)
            TBCW = sbcc("tbcw", [128, 128], F32)
            MSKW = sbcc("mskw", [128, 128], F32)
            CUMW = sbcc("cumw", [128, 128], F32)
            MSBW = sbcc("msbw", [128, 128], BF16)
            TSR = sbcc("tsr", [1, 128], F32)
            TS16 = sbcc("ts16", [16, NE], F32)
            OFF16 = sbcc("off16", [16, NE], F32)
            OFFR = sbcc("offr", [1, 128], F32)
            sbb = mk(stkB); stkB.__enter__()
            WOB = [sbb(f"wob{i}", [128, D], BF16) for i in range(NFT)]
            XTMS = [sbb(f"xtms{i}", [128, D], F32) for i in range(2)]
            YT = [sbb(f"yt{i}", [128, D], F32) for i in range(2)]
            BOBC = sbb("bobc", [128, D], F32)
            for hh in range(2):
                sl = slice(hh * 512, hh * 512 + 512)
                E("t", lambda e, sl=sl: e.matmul(PSF[0][:, :], ONER[0:1, :], BO[0:1, sl],
                                                 start=True, stop=True),
                  reads=["oner", "bo_r", "PSF0"], writes=["PSF0"])
                E("v", lambda e, sl=sl, hh=hh: e.tensor_copy(BOBC[:, sl], PSF[0][:, :]),
                  reads=["PSF0"], writes=["PSF0", ("bobc", hh)])
            for k in range(NFT):
                E("q_s", lambda e, k=k: e.dma_start(out=WOB[k][:, :],
                                                    in_=wob_d[k * 128:(k + 1) * 128, :]),
                  reads=["wob_d"], writes=[("wob", k)], sem="qs")
            def p3_partA(i):
                b = i % 2
                E("q_a", lambda e, i=i, b=b: e.dma_start(out=XTMS[b][:, :],
                                                         in_=x_tm[i * 128:(i + 1) * 128, :]),
                  reads=[("xtms", b)], writes=[("xtms", b)], sem="qa")
                for hh in range(2):
                    pb = (2 * i + hh) % 2
                    bank = BK[pb]
                    fns = []
                    for k in range(NFT):
                        r = [("wob", k)]
                        if k == 0:
                            r.append(bank)
                        fns.append((
                            (lambda e, i=i, hh=hh, k=k, pb=pb: e.matmul(
                                PSF[pb][:, :], CTXT[k][:, i * 128:(i + 1) * 128],
                                WOB[k][:, hh * 512:(hh + 1) * 512],
                                start=(k == 0), stop=(k == NFT - 1))),
                            r))
                    G("t", fns, writes=[bank])
                    sl = slice(hh * 512, hh * 512 + 512)
                    E("v", lambda e, b=b, pb=pb, sl=sl: e.tensor_tensor(
                        YT[b][:, sl], PSF[pb][:, :], BOBC[:, sl], ALU.add),
                      reads=[bank, ("bobc", hh), ("yt", b, hh)], writes=[bank, ("yt", b, hh)])
                    E("v", lambda e, b=b, sl=sl: e.tensor_tensor(
                        YT[b][:, sl], YT[b][:, sl], XTMS[b][:, sl], ALU.add),
                      reads=[("yt", b, hh), ("xtms", b)], writes=[("yt", b, hh)])
                ln_block(YT[b], [("yt", b, 0), ("yt", b, 1)], X2[i], ("x2", i), par=b)
                if x2_tap is not None:
                    E("q_a", lambda e, i=i: e.dma_start(out=x2_tap[i * 128:(i + 1) * 128, :],
                                                        in_=X2[i][:, :]),
                      reads=[("x2", i)], writes=[("x2tap", i)], sem="qa")
            def p3_partB(i):
                pg = 4 + (i % 2)
                gbank = BK[pg]
                for f in range(NFT):
                    tb = f % 2
                    tbank = BK[2 + tb]
                    E("t", lambda e, i=i, f=f, tb=tb: e.transpose(
                        PSF[2 + tb][:, 0:128], X2[i][:, f * 128:(f + 1) * 128], EYEF[:, :]),
                      reads=[("x2", i), "eye_f", tbank], writes=[tbank])
                    E("v", lambda e, f=f, tb=tb: e.tensor_copy(TRSW[:, f * 128:(f + 1) * 128],
                                                               PSF[2 + tb][:, 0:128]),
                      reads=[tbank, ("trsw", f)], writes=[tbank, ("trsw", f)])
                fns = []
                for f in range(NFT):
                    r = [("trsw", f), ("wgs", f)]
                    if f == 0:
                        r.append(gbank)
                    fns.append((
                        (lambda e, f=f, pg=pg: e.matmul(
                            PSF[pg][:, 0:NE], TRSW[:, f * 128:(f + 1) * 128],
                            WGS[:, f * NE:(f + 1) * NE],
                            start=(f == 0), stop=(f == NFT - 1))),
                        r))
                G("t", fns, writes=[gbank])
                E("s", lambda e, pg=pg: e.activation(out=EXR[:, :], in_=PSF[pg][:, 0:NE],
                                                     func=AF.Exp, accum_out=ESUM[:, :]),
                  reads=[gbank, "exr", "esum"], writes=[gbank, "exr", "esum"])
                E("v", lambda e: e.reciprocal(ERCP[:, :], ESUM[:, :]),
                  reads=["esum"], writes=["ercp"])
                E("v", lambda e, i=i: e.tensor_scalar(SCW[:, i * NE:(i + 1) * NE], EXR[:, :],
                                                      ERCP[:, :], None, ALU.mult),
                  reads=["exr", "ercp", ("scw", i)], writes=[("scw", i)])
                if sc_tap is not None:
                    E("q_a", lambda e, i=i: e.dma_start(out=sc_tap[i * 128:(i + 1) * 128, :],
                                                        in_=SCW[:, i * NE:(i + 1) * NE]),
                      reads=[("scw", i)], writes=[("sctap", i)], sem="qa")
                tb2 = i % 2
                t2bank = BK[2 + tb2]
                E("t", lambda e, i=i, tb2=tb2: e.transpose(
                    PSF[2 + tb2][0:NE, 0:128], SCW[:, i * NE:(i + 1) * NE], EYEF[:, :]),
                  reads=[("scw", i), "eye_f", t2bank], writes=[t2bank])
                E("v", lambda e, i=i, tb2=tb2: e.tensor_copy(
                    SCT[:, i * 128:(i + 1) * 128], PSF[2 + tb2][0:NE, 0:128]),
                  reads=[t2bank, ("sct", i)], writes=[t2bank, ("sct", i)])
            for i in range(NTT):
                p3_partA(i)
                if i > 0:
                    p3_partB(i - 1)
            p3_partB(NTT - 1)
            L.barrier(skip=("qg",))

        if stop_after not in ("qkv", "attn", "ln1", "gate"):
            stkB.close()
            E("q_s", lambda e: e.dma_start(out=cc1_in[:, :], in_=SCT[:, :]),
              reads=[("sct", i) for i in range(NTT)], writes=["cc1_in"], sem="qs")
            E("g", lambda e: e.collective_compute(
                "AllGather", ALU.bypass, replica_groups=[list(range(8))],
                ins=[cc1_in[:, :]], outs=[cc1_out[:, :]]),
              reads=["cc1_in"], writes=["cc1_out"], sem="cc")
            E("q_g", lambda e: e.indirect_dma_start(
                out=SCALL[:, :], out_offset=None,
                in_=cc1_out.rearrange("r (s f) -> (r s) f", f=128),
                in_offset=IOff(ESEL[:, 0:1], 0)),
              reads=["cc1_out", "eselS"], writes=["scall"], sem="qg")
            E("v", lambda e: e.memset(BLO[:, :], 0.0), writes=["blo"])
            E("v", lambda e: e.memset(BHI[:, :], 0.5), writes=["bhid"])
            for it in range(SEARCH_ITERS):
                E("v", lambda e: e.tensor_tensor(BMID[:, :], BLO[:, :], BHI[:, :], ALU.add),
                  reads=["blo", "bhid", "bmid"], writes=["bmid"])
                E("v", lambda e: e.tensor_scalar(JUNKF[:, :], SCALL[:, :], BMID[:, :], None,
                                                 ALU.is_ge, ALU.add, accum_out=BCNT[:, :]),
                  reads=["scall", "bmid", "bcnt", "junkf"], writes=["bcnt", "junkf"])
                E("t", lambda e: e.matmul(PSF[2][:, 0:1], ONESQ[:, :], BCNT[:, :],
                                          start=True, stop=True),
                  reads=["onesq", "bcnt", "PSF2"], writes=["PSF2"])
                E("v", lambda e: e.tensor_scalar(BSEL[:, :], PSF[2][:, 0:1], float(CAPG), None, ALU.is_ge),
                  reads=["PSF2", "bsel"], writes=["PSF2", "bsel"])
                E("v", lambda e: e.tensor_tensor(BT1[:, :], BSEL[:, :], BHI[:, :], ALU.mult),
                  reads=["bsel", "bhid", "bt1"], writes=["bt1"])
                E("v", lambda e: e.tensor_tensor(BLO[:, :], BLO[:, :], BT1[:, :], ALU.add),
                  reads=["blo", "bt1"], writes=["blo"])
                E("v", lambda e: e.tensor_scalar(BHI[:, :], BHI[:, :], 0.5, None, ALU.mult),
                  reads=["bhid"], writes=["bhid"])
            E("q_s", lambda e: e.dma_start(out=cc2_in[:, :], in_=BLO[0:1, 0:1]),
              reads=["blo"], writes=["cc2_in"], sem="qs")
            E("g", lambda e: e.collective_compute(
                "AllGather", ALU.bypass, replica_groups=[list(range(8))],
                ins=[cc2_in[:, :]], outs=[cc2_out[:, :]]),
              reads=["cc2_in"], writes=["cc2_out"], sem="cc")
            E("q_s", lambda e: e.dma_start(out=TROW[:, :],
                                           in_=cc2_out.rearrange("e one -> one e")),
              reads=["cc2_out"], writes=["trow"], sem="qs")
            if thr_tap is not None:
                E("q_s", lambda e: e.dma_start(out=thr_tap[:, :], in_=TROW[:, :]),
                  reads=["trow"], writes=["thrtap"], sem="qs")
            for i in range(NTT):
                E("v", lambda e, i=i: e.tensor_copy(TRT[0:1, i * NE:(i + 1) * NE], TROW[0:1, :]),
                  reads=["trow", ("trt", i)], writes=[("trt", i)])
            trt_keys = [("trt", i) for i in range(NTT)]
            scw_keys = [("scw", i) for i in range(NTT)]
            E("t", lambda e: e.matmul(PSF[2][:, 0:128], ONER[0:1, :], TRT[0:1, :],
                                      start=True, stop=True),
              reads=trt_keys + ["oner", "PSF2"], writes=["PSF2"])
            E("v", lambda e: e.tensor_copy(TBCW[:, :], PSF[2][:, 0:128]),
              reads=["PSF2", "tbcw"], writes=["PSF2", "tbcw"])
            E("v", lambda e: e.tensor_tensor(MSKW[:, :], SCW[:, :], TBCW[:, :], ALU.is_ge),
              reads=scw_keys + ["tbcw", "mskw"], writes=["mskw"])
            E("v", lambda e: e.tensor_tensor(MSBW[:, :], SCW[:, :], MSKW[:, :], ALU.mult),
              reads=scw_keys + ["mskw", "msbw"], writes=["msbw"])
            for i in range(NTT):
                E("q_g", lambda e, i=i: e.dma_start(out=x2aug_d[i * 128:(i + 1) * 128, 0:D],
                                                    in_=X2[i][:, :]),
                  reads=[("x2", i)], writes=[("x2aug", i)], sem="qg")
                E("q_g", lambda e, i=i: e.dma_start(out=x2aug_d[i * 128:(i + 1) * 128, D:AUGW],
                                                    in_=MSBW[:, i * NE:(i + 1) * NE]),
                  reads=["msbw"], writes=[("x2aug_s", i)], sem="qg")
                E("q_s", lambda e, i=i: e.dma_start(out=y2acc_d[i * 128:(i + 1) * 128, :],
                                                    in_=X2[i][:, :]),
                  reads=[("x2", i)], writes=[("y2acc_init", i)], sem="qs")
                pcb = 2 + (i % 2)
                E("t", lambda e, i=i, pcb=pcb: e.matmul(PSF[pcb][:, 0:NE], SUT[:, :],
                                                        MSKW[:, i * NE:(i + 1) * NE],
                                                        start=True, stop=True),
                  reads=["sut", "mskw", BK[pcb]], writes=[BK[pcb]])
                E("v", lambda e, i=i, pcb=pcb: e.tensor_copy(CUMW[:, i * NE:(i + 1) * NE],
                                                             PSF[pcb][:, 0:NE]),
                  reads=[BK[pcb], ("cumw", i)], writes=[BK[pcb], ("cumw", i)])
                tsb = 4 + (i % 2)
                E("t", lambda e, i=i, tsb=tsb: e.matmul(PSF[tsb][0:1, 0:NE], ONEC[:, :],
                                                        MSKW[:, i * NE:(i + 1) * NE],
                                                        start=True, stop=True),
                  reads=["onec", "mskw", BK[tsb]], writes=[BK[tsb]])
                E("v", lambda e, i=i, tsb=tsb: e.tensor_copy(TSR[:, i * NE:(i + 1) * NE],
                                                             PSF[tsb][0:1, 0:NE]),
                  reads=[BK[tsb], ("tsr", i)], writes=[BK[tsb], ("tsr", i)])
            cum_keys = [("cumw", i) for i in range(NTT)]
            E("q_a", lambda e: e.dma_start(out=ts_d[:, :], in_=TSR[:, :]),
              reads=[("tsr", i) for i in range(NTT)], writes=["ts_d"], sem="qa")
            E("q_a", lambda e: e.dma_start(out=TS16[:, :],
                                           in_=ts_d.rearrange("one (i e) -> (one i) e", e=NE)),
              reads=["ts_d"], writes=["ts16"], sem="qa")
            E("t", lambda e: e.matmul(PSF[2][0:16, 0:NE], SUT[0:16, 0:16], TS16[:, :],
                                      start=True, stop=True),
              reads=["ts16", "sut", "PSF2"], writes=["PSF2"])
            E("v", lambda e: e.tensor_copy(OFF16[:, :], PSF[2][0:16, 0:NE]),
              reads=["PSF2", "off16"], writes=["PSF2", "off16"])
            E("q_a", lambda e: e.dma_start(out=off_d.rearrange("one (i e) -> (one i) e", e=NE),
                                           in_=OFF16[:, :]),
              reads=["off16"], writes=["off_d"], sem="qa")
            E("q_a", lambda e: e.dma_start(out=OFFR[:, :], in_=off_d[:, :]),
              reads=["off_d"], writes=["offr"], sem="qa")
            E("t", lambda e: e.matmul(PSF[3][:, 0:128], ONER[0:1, :], OFFR[0:1, :],
                                      start=True, stop=True),
              reads=["oner", "offr", "PSF3"], writes=["PSF3"])
            E("v", lambda e: e.tensor_tensor(CUMW[:, :], CUMW[:, :], PSF[3][:, 0:128], ALU.add),
              reads=cum_keys + ["PSF3"], writes=["PSF3"] + cum_keys)
            E("v", lambda e: e.tensor_scalar(CUMW[:, :], CUMW[:, :], BIGL[:, :], None, ALU.subtract),
              reads=cum_keys + ["bigl"], writes=cum_keys)
            E("v", lambda e: e.tensor_tensor(CUMW[:, :], CUMW[:, :], MSKW[:, :], ALU.mult),
              reads=cum_keys + ["mskw"], writes=cum_keys)
            E("v", lambda e: e.tensor_scalar(CUMW[:, :], CUMW[:, :], BIGL[:, :], None, ALU.add),
              reads=cum_keys + ["bigl"], writes=cum_keys)
            E("v", lambda e: e.tensor_copy(POSW[:, :], CUMW[:, :]),
              reads=cum_keys + ["posw"], writes=["posw"])
            if pos_tap is not None:
                E("q_a", lambda e: e.dma_start(out=pos_tap[:, :], in_=POSW[:, :]),
                  reads=["posw"], writes=["postap"], sem="qa")
            L.barrier(skip=("qg",))

        if stop_after not in ("qkv", "attn", "ln1", "gate", "route"):
            stkC.close()
            stkX2.close()
            stkPS1.close()
            FPS = [ps(f"fps{i}", [128, 512], F32) for i in range(4)]
            FPB = [ps(f"fpb{i}", [128, 512], BF16) for i in range(4)]
            sbd = mk(stkD); stkD.__enter__()
            ILSB = [sbd(f"ilsb{i}", [128, CT], U32) for i in range(4)]
            RTM = [sbd(f"rtm{i}", [128, CT * AUGW], BF16) for i in range(2)]
            SCOL = sbd("scol", [128, 1], F32)
            RT = [sbd(f"rt{i}", [128, NFT * CAP], BF16) for i in range(2)]
            W1C = [sbd(f"w1c{i}", [128, 4, 1024], BF16) for i in range(2)]
            W3C = [sbd(f"w3c{i}", [128, 4, 1024], BF16) for i in range(2)]
            W2A = sbd("w2a", [128, NHT, D], BF16)
            SIL = [sbd(f"sil{i}", [128, CAP], BF16) for i in range(2)]
            HT = sbd("ht", [128, NHT * CAP], BF16)
            OTM = [sbd("otm0", [128, CT * D], F32)] * 2

            x2aug_keys = ([("x2aug", i) for i in range(NTT)] +
                          [("x2aug_s", i) for i in range(NTT)] +
                          [("x2aug_pad", r) for r in range(CT)])
            y2init_keys = ([("y2acc_init", i) for i in range(NTT)] +
                           [("y2acc_pad", r) for r in range(CT)])

            def ilist_emit(e_i):
                ilrows = slice(e_i * CAP, (e_i + 1) * CAP)
                E("q_g", lambda e, ilrows=ilrows: e.dma_start(
                    out=ilist_d[ilrows, :].rearrange("(p c) one -> p (c one)", c=CT),
                    in_=INIT5[:, :]),
                  reads=["init5", ("il", e_i)], writes=[("il", e_i)], sem="qg")
                for i in range(NTT):
                    E("q_g", lambda e, e_i=e_i, i=i: e.indirect_dma_start(
                        out=ilist_d[:, :],
                        out_offset=IOff(POSW[:, NE * i + e_i: NE * i + e_i + 1], 0),
                        in_=IDS[:, i:i + 1], in_offset=None,
                        element_offset=e_i * CAP,
                        bounds_check=REGS["bc"], oob_is_err=False),
                      reads=["ids", "posw", ("il", e_i)], writes=[("ilsc", e_i, i)], sem="qg")

            def fetch_emit(e_i):
                eb = e_i % 2
                ib = e_i % 4
                ilrows = slice(e_i * CAP, (e_i + 1) * CAP)
                E("q_g", lambda e, ib=ib, ilrows=ilrows: e.dma_start(
                    out=ILSB[ib][:, :],
                    in_=ilist_d[ilrows, :].rearrange("(p c) one -> p (c one)", c=CT)),
                  reads=[("il", e_i), ("ilsb", ib)] + [("ilsc", e_i, i) for i in range(NTT)],
                  writes=[("ilsb", ib)], sem="qg")
                if il_tap is not None:
                    E("q_g", lambda e, ilrows=ilrows: e.dma_start(out=il_tap[ilrows, :],
                                                                  in_=ilist_d[ilrows, :]),
                      reads=[("il", e_i)], writes=[("iltap", e_i)], sem="qg")
                for c in range(CT):
                    E("q_g", lambda e, eb=eb, ib=ib, c=c: e.indirect_dma_start(
                        out=RTM[eb][:, c * AUGW:(c + 1) * AUGW], out_offset=None,
                        in_=x2aug_d[:, :], in_offset=IOff(ILSB[ib][:, c:c + 1], 0)),
                      reads=[("ilsb", ib), ("rtm", eb, c)] +
                            (x2aug_keys if e_i == 0 else []),
                      writes=[("rtm", eb, c)], sem="qg")

            def trans_thunks(e_i):
                eb = e_i % 2
                th = []
                for c in range(CT):
                    def t0(eb=eb, c=c, e_i=e_i):
                        E("v", lambda e, eb=eb, c=c, e_i=e_i: e.tensor_copy(
                            SCOL[:, :], RTM[eb][:, c * AUGW + D + e_i: c * AUGW + D + e_i + 1]),
                          reads=[("rtm", eb, c), "scol"], writes=["scol"])
                        E("v", lambda e, eb=eb, c=c: e.tensor_scalar(
                            RTM[eb][:, c * AUGW: c * AUGW + D],
                            RTM[eb][:, c * AUGW: c * AUGW + D],
                            SCOL[:, :], None, ALU.mult),
                          reads=[("rtm", eb, c), "scol"], writes=[("rtm", eb, c)])
                    th.append(t0)
                n = 0
                for c in range(CT):
                    for fb in range(NFT):
                        def t2(eb=eb, c=c, fb=fb, n=n):
                            bk = f"FPB{n % 4}"
                            pb = FPB[n % 4]
                            E("t", lambda e, eb=eb, c=c, fb=fb, pb=pb: e.transpose(
                                pb[:, 0:128],
                                RTM[eb][:, c * AUGW + fb * 128: c * AUGW + (fb + 1) * 128],
                                EYEB[:, :]),
                              reads=[("rtm", eb, c), "eye_bf", bk], writes=[bk])
                            E("v", lambda e, eb=eb, c=c, fb=fb, pb=pb: e.tensor_copy(
                                RT[eb][:, fb * CAP + c * 128: fb * CAP + (c + 1) * 128],
                                pb[:, 0:128]),
                              reads=[bk, ("rt", eb, fb)],
                              writes=[bk, ("rt", eb, fb)])
                        th.append(t2)
                        n += 1
                return th

            def wchunk_emit(e_i, q):
                r = q % 2
                rows = slice((e_i * 4 + q) * 128, (e_i * 4 + q + 1) * 128)
                xr = x2aug_keys if (e_i == 0 and q == 0) else []
                E("q_s", lambda e, rows=rows, r=r: e.dma_start(
                    out=W1C[r][:, :, :],
                    in_=w1b_d[rows, :].rearrange("p (h f) -> p h f", f=1024)),
                  reads=[("w1c", r), "w1b"] + xr, writes=[("w1c", r)], sem="qs")
                E("q_s", lambda e, rows=rows, r=r: e.dma_start(
                    out=W3C[r][:, :, :],
                    in_=w3b_d[rows, :].rearrange("p (h f) -> p h f", f=1024)),
                  reads=[("w3c", r), "w3b"] + xr, writes=[("w3c", r)], sem="qs")

            def w2_emit(e_i):
                xr = x2aug_keys if e_i == 0 else []
                E("q_g", lambda e, e_i=e_i: e.dma_start(
                    out=W2A[:, :, :],
                    in_=w2pm[e_i * 128:(e_i + 1) * 128, :].rearrange("p (k f) -> p k f", f=1024)),
                  reads=["w2a"] + xr, writes=["w2a"], sem="qg")

            def compute_emit(e_i, hooks=(), pending=None):
                eb = e_i % 2
                for ht_i in range(NHT):
                    q, hh = divmod(ht_i, 4)
                    r = (e_i * 4 + q) % 2
                    if ht_i == 0:
                        w2_emit(e_i)
                    if hh == 0 and q + 1 <= 3:
                        wchunk_emit(e_i, q + 1)
                    if ht_i == 12 and e_i + 1 < NE:
                        wchunk_emit(e_i + 1, 0)
                    for hat, hfn in hooks:
                        if ht_i == hat:
                            hfn()
                    if ht_i == 5 and e_i + 2 < NE:
                        ilist_emit(e_i + 2)
                    for part in range(2):
                        wsb = W1C[r] if part == 0 else W3C[r]
                        wkey = ("w1c", r) if part == 0 else ("w3c", r)
                        for seg, (n0, n1) in enumerate([(0, 512), (512, CAP)]):
                            pi = part * 2 + seg
                            pp = FPS[pi]
                            bank = f"FPS{pi}"
                            fns = []
                            for k in range(NFT):
                                rd = [wkey, ("rt", eb, k)]
                                if k == 0:
                                    rd.append(bank)
                                fns.append((
                                    (lambda e, wsb=wsb, hh=hh, k=k, n0=n0, n1=n1, pp=pp, eb=eb: e.matmul(
                                        pp[:, 0:n1 - n0],
                                        wsb[:, hh, k * 128:(k + 1) * 128],
                                        RT[eb][:, k * CAP + n0: k * CAP + n1],
                                        start=(k == 0), stop=(k == NFT - 1))),
                                    rd))
                            G("t", fns, writes=[bank])
                    sb_i = ht_i % 2
                    for seg, (n0, n1) in enumerate([(0, 512), (512, CAP)]):
                        p1k, p3k = f"FPS{seg}", f"FPS{2 + seg}"
                        p1p, p3p = FPS[seg], FPS[2 + seg]
                        E("s", lambda e, sb_i=sb_i, n0=n0, n1=n1, p1p=p1p: e.activation(
                            out=SIL[sb_i][:, n0:n1], in_=p1p[:, 0:n1 - n0], func=AF.Silu),
                          reads=[p1k, ("sil", sb_i, seg)], writes=[p1k, ("sil", sb_i, seg)])
                        E("v", lambda e, sb_i=sb_i, ht_i=ht_i, n0=n0, n1=n1, p3p=p3p: e.tensor_tensor(
                            HT[:, ht_i * CAP + n0: ht_i * CAP + n1],
                            p3p[:, 0:n1 - n0], SIL[sb_i][:, n0:n1], ALU.mult),
                          reads=[("sil", sb_i, seg), p3k, ("ht", ht_i)],
                          writes=[p3k, ("ht", ht_i)])
                    if pending is not None and ht_i >= 4:
                        slot = ht_i - 4
                        lo = (len(pending) * slot) // 11
                        hi = (len(pending) * (slot + 1)) // 11
                        for th in pending[lo:hi]:
                            th()
                for c in range(CT):
                    for seg, (n0, n1) in enumerate([(0, 512), (512, 1024)]):
                        pp = FPS[seg]
                        bank = f"FPS{seg}"
                        fns = []
                        for k in range(NHT):
                            rd = [("ht", k), "w2a"]
                            if k == 0:
                                rd.append(bank)
                            fns.append((
                                (lambda e, c=c, k=k, n0=n0, n1=n1, pp=pp: e.matmul(
                                    pp[:, 0:n1 - n0],
                                    HT[:, k * CAP + c * 128: k * CAP + (c + 1) * 128],
                                    W2A[:, k, n0:n1],
                                    start=(k == 0), stop=(k == NHT - 1))),
                                rd))
                        G("t", fns, writes=[bank])
                        E("v", lambda e, eb=eb, c=c, n0=n0, n1=n1, pp=pp: e.tensor_copy(
                            OTM[eb][:, c * D + n0: c * D + n1], pp[:, 0:n1 - n0]),
                          reads=[bank, ("otm", c)], writes=[bank, ("otm", c)])

            def scatter_emit(e_i):
                eb = e_i % 2
                ib = e_i % 4
                prev = ([("y2w", e_i - 1, c) for c in range(CT)] if e_i > 0
                        else y2init_keys)
                for c in range(CT):
                    E("q_g", lambda e, eb=eb, ib=ib, c=c: e.indirect_dma_start(
                        out=y2acc_d[:, :], out_offset=IOff(ILSB[ib][:, c:c + 1], 0),
                        in_=OTM[eb][:, c * D:(c + 1) * D], in_offset=None,
                        compute_op=ALU.add, oob_is_err=True),
                      reads=[("otm", c), ("ilsb", ib)] + prev,
                      writes=[("y2w", e_i, c)], sem="qg")

            ilist_emit(0)
            fetch_emit(0)
            ilist_emit(1)
            wchunk_emit(0, 0)
            for th in trans_thunks(0):
                th()
            for e_i in range(NE):
                hooks = []
                if e_i + 1 < NE:
                    hooks.append((0, lambda e_i=e_i: fetch_emit(e_i + 1)))
                if e_i > 0:
                    hooks.append((2, lambda e_i=e_i: scatter_emit(e_i - 1)))
                pending = trans_thunks(e_i + 1) if e_i + 1 < NE else None
                compute_emit(e_i, hooks=hooks, pending=pending)
            scatter_emit(NE - 1)
            L.barrier(skip=("qg",))

        if stop_after is None:
            stkD.close()
            sbe = mk(stkE); stkE.__enter__()
            Y2S = [sbe(f"y2s{i}", [128, D], F32) for i in range(2)]
            OUTS = [sbe(f"outs{i}", [128, D], F32) for i in range(2)]
            for i in range(NTT):
                b = i % 2
                E("q_s", lambda e, i=i, b=b: e.dma_start(out=Y2S[b][:, :],
                                                         in_=y2acc_d[i * 128:(i + 1) * 128, :]),
                  reads=[("y2w", NE - 1, c) for c in range(CT)] + [("y2s", b)],
                  writes=[("y2s", b)], sem="qs")
                if y2_tap is not None:
                    E("q_a", lambda e, i=i, b=b: e.dma_start(out=y2_tap[i * 128:(i + 1) * 128, :],
                                                             in_=Y2S[b][:, :]),
                      reads=[("y2s", b)], writes=[("y2tap", i)], sem="qa")
                ln_block(Y2S[b], [("y2s", b)], OUTS[b], ("outs", b), par=b)
                E("q_a", lambda e, i=i, b=b: e.dma_start(out=out_tm[i * 128:(i + 1) * 128, :],
                                                         in_=OUTS[b][:, :]),
                  reads=[("outs", b)], writes=[("out", i), ("y2s", b), ("outs", b)], sem="qa")
        L.barrier()
        for _stk in (stkE, stkD, stkC, stkB, stkX2, stkA, stkCT):
            _stk.close()

        sem_stack = contextlib.ExitStack()
        with sem_stack, nc.Block() as block:
            semmap = {n: sem_stack.enter_context(nc.semaphore("sem_" + n))
                      for n in L.semnames}

            block.tensor(_make_merged_runner(L, ("t",), semmap))
            block.vector(_make_merged_runner(L, ("v",), semmap))
            block.scalar(_make_merged_runner(L, ("s", "q_a"), semmap))
            def _g_pre(eng):
                REGS["bc"] = eng.alloc_register("bc_reg")
                eng.reg_mov(REGS["bc"], CAP - 1)
            block.gpsimd(_make_merged_runner(L, ("g", "q_g"), semmap, pre=_g_pre))
            block.sync(_make_merged_runner(L, ("q_s",), semmap))

    return nc, tap_aps


def _make_merged_runner(L, names, semmap, pre=None):
    def run(eng):
        if pre is not None:
            pre(eng)
        entries = []
        for n in names:
            entries.extend(L.progs[n])
        entries.sort(key=lambda t: t[-1])
        for entry in entries:
            if entry[0] == "wait":
                _, semname, val, _ = entry
                eng.wait_ge(semmap[semname], val)
            else:
                _, fn, semname, step, _ = entry
                inst = fn(eng)
                if semname is not None:
                    if step > 1:
                        inst.then_inc(semmap[semname], step)
                    else:
                        inst.then_inc(semmap[semname])
    return run


def host_inputs(inputs):
    x = np.asarray(inputs["L0_attn_x"])
    Wq = np.asarray(inputs["L0_attn_Wq"]); Wk = np.asarray(inputs["L0_attn_Wk"])
    Wv = np.asarray(inputs["L0_attn_Wv"]); Wo = np.asarray(inputs["L0_attn_Wo"])
    bo = np.asarray(inputs["L0_attn_bo"])
    wg_ = np.asarray(inputs["L0_ffn_wg"])
    w1_ = np.asarray(inputs["L0_ffn_w1"]); w3_ = np.asarray(inputs["L0_ffn_w3"])
    w2_ = np.asarray(inputs["L0_ffn_w2"])

    def tile_T(w):
        wt = np.ascontiguousarray(w.T)
        return np.ascontiguousarray(
            wt.reshape(NFT, 128, NFT, 128).transpose(2, 0, 1, 3))

    wqT = tile_T(Wq); wkT = tile_T(Wk); wvT = tile_T(Wv)
    woT = np.ascontiguousarray(Wo.T)
    def pmaj(w):
        a = w.reshape(NE, NFT, 128, NHT, 128).transpose(0, 2, 3, 1, 4)
        a = a.reshape(NE, 128, 4, 4, NFT, 128).transpose(0, 2, 1, 3, 4, 5)
        return np.ascontiguousarray(a.reshape(NE * 4 * 128, 4096))

    w1pm = pmaj(w1_); w3pm = pmaj(w3_)
    w2pm = np.ascontiguousarray(
        w2_.reshape(NE, NHT, 128, D).transpose(0, 2, 1, 3).reshape(NE * 128, NHT * D))

    eye_bf = np.eye(128, dtype=ml_dtypes.bfloat16)
    eye_f = np.eye(128, dtype=np.float32)
    sut_f = np.triu(np.ones((128, 128), dtype=np.float32), k=1)
    onec_f = np.ones((128, 1), dtype=np.float32)
    oner_f = np.ones((1, 128), dtype=np.float32)
    oner_b = np.ones((1, 128), dtype=ml_dtypes.bfloat16)
    utk_b = (np.arange(128)[:, None] <= np.arange(128)[None, :]).astype(ml_dtypes.bfloat16)

    in_maps = []
    for c in range(8):
        p = np.arange(128)
        esel_c = (((p // 16) * 8 + c) * 16 + (p % 16)).astype(np.uint32).reshape(128, 1)
        in_maps.append({
            "xT": np.ascontiguousarray(x[c].T).astype(np.float32, copy=False),
            "x_tm": np.ascontiguousarray(x[c]).astype(np.float32, copy=False),
            "wqT": wqT, "wkT": wkT, "wvT": wvT, "woT": woT,
            "bo_r": np.ascontiguousarray(bo.reshape(1, D)).astype(np.float32, copy=False),
            "wg": np.ascontiguousarray(wg_).astype(np.float32, copy=False),
            "w1pm": w1pm, "w3pm": w3pm, "w2pm": w2pm,
            "esel": esel_c,
            "actc_f": np.tile(np.array([list(ACT_CONSTS) + [0.0] * (8 - len(ACT_CONSTS))],
                               dtype=np.float32), (128, 1)),
            "eye_bf": eye_bf, "eye_f": eye_f, "sut_f": sut_f,
            "onec_f": onec_f, "oner_f": oner_f, "oner_b": oner_b,
            "utk_b": utk_b,
        })
    return in_maps


def kernel(**inputs):
    in_maps = host_inputs(inputs)
    nc, _ = build_program()
    res = run_bass_kernel_spmd(nc, in_maps, core_ids=list(range(8)))
    out = np.stack([res.results[c]["out_tm"] for c in range(8)], axis=0)
    return out.astype(np.float32)


# version 25
# speedup vs baseline: 1.1744x; 1.0121x over previous
import numpy as np
import ml_dtypes
import concourse.bass as bass
import concourse.mybir as mybir
from concourse.bass_utils import run_bass_kernel_spmd

F32 = mybir.dt.float32
BF16 = mybir.dt.bfloat16
U32 = mybir.dt.uint32
AF = mybir.ActivationFunctionType
ALU = mybir.AluOpType
AX = mybir.AxisListType
IOff = bass.IndirectOffsetOnAxis

T = 2048
D = 1024
H = 16
HD = 64
HID = 2048
NE = 8
CAPG = 4096
CAP = 640
CT = CAP // 128
NTT = T // 128
NFT = D // 128
NHT = HID // 128
AUGW = D + 8
NROWS = T + CAP
SEARCH_ITERS = 16
ACT_CONSTS = (-4.0, 0.125, -1.0, -0.5, 1e-5)
ENAMES = ("t", "v", "s", "g", "q_s", "q_a", "q_g")


class Ledger:

    NQSEM = 20

    def __init__(self, semnames=None):
        self.progs = {e: [] for e in ENAMES}
        self.seq = 0
        self.semnames = (["t", "v", "s", "g", "cc"] +
                         [f"{q}{j}" for q in ("qs", "qa", "qg")
                          for j in range(self.NQSEM)])
        self.cnt = {k: 0 for k in self.semnames}
        self.dmacnt = {"qs": 0, "qa": 0, "qg": 0}
        self.clocks = {e: {} for e in ENAMES}
        self.snap = {}
        self.last_w = {}
        self.readers = {}
        self.eng_sem = {"t": "t", "v": "v", "s": "s", "g": "g",
                        "q_s": "qs", "q_a": "qa", "q_g": "qg"}
        self.sem_step = {n: (16 if n[:2] in ("qs", "qa", "qg") else 1)
                         for n in self.semnames}

    def _append(self, ename, entry):
        self.progs[ename].append(entry + (self.seq,))
        self.seq += 1

    def _wait(self, ename, semname, val):
        cl = self.clocks[ename]
        if cl.get(semname, 0) >= val:
            return
        self._append(ename, ("wait", semname, val))
        snap = self.snap.get((semname, val), {semname: val})
        for k, v in snap.items():
            if cl.get(k, 0) < v:
                cl[k] = v

    def _needed(self, reads, writes):
        need = {}
        def add(dep):
            if dep is None:
                return
            s, v = dep
            if need.get(s, 0) < v:
                need[s] = v
        for k in reads:
            add(self.last_w.get(k))
        for k in writes:
            add(self.last_w.get(k))
            for dep in self.readers.get(k, ()):
                add(dep)
        return need

    def _bump(self, ename, semname):
        step = self.sem_step[semname]
        self.cnt[semname] += step
        val = self.cnt[semname]
        self.snap[(semname, val)] = dict(self.clocks[ename]) | {semname: val}
        return (semname, val)

    def op(self, ename, fn, reads=(), writes=(), sem=None):
        for s, v in sorted(self._needed(reads, writes).items()):
            self._wait(ename, s, v)
        semname = sem if sem is not None else self.eng_sem[ename]
        if semname in self.dmacnt:
            idx = self.dmacnt[semname]
            self.dmacnt[semname] += 1
            q = semname
            semname = f"{q}{idx % self.NQSEM}"
            if idx >= self.NQSEM:
                self._wait(ename, semname, 16 * (idx // self.NQSEM))
        dep = self._bump(ename, semname)
        self._append(ename, ("inst", fn, semname, self.sem_step[semname]))
        for k in writes:
            self.last_w[k] = dep
            self.readers[k] = []
        for k in reads:
            self.readers.setdefault(k, []).append(dep)
        return dep

    def group(self, ename, fns_reads, writes=(), sem=None):
        semname = sem if sem is not None else self.eng_sem[ename]
        n = len(fns_reads)
        all_reads = set()
        for i, (fn, reads) in enumerate(fns_reads):
            for s, v in sorted(self._needed(reads, writes if i == 0 else ()).items()):
                self._wait(ename, s, v)
            if i < n - 1:
                self._append(ename, ("inst", fn, None, 0))
            else:
                dep = self._bump(ename, semname)
                self._append(ename, ("inst", fn, semname, self.sem_step[semname]))
            all_reads.update(reads)
        for k in writes:
            self.last_w[k] = dep
            self.readers[k] = []
        for k in all_reads:
            self.readers.setdefault(k, []).append(dep)

    def barrier(self, ename="g", skip=()):
        for s in sorted(self.cnt):
            if self.cnt[s] > 0 and not any(s.startswith(p) for p in skip):
                self._wait(ename, s, self.cnt[s])
        dep = self._bump(ename, self.eng_sem[ename])
        self._append(ename, ("inst", lambda e: e.nop(), self.eng_sem[ename], 1))
        for other in ENAMES:
            if other != ename:
                self._wait(other, dep[0], dep[1])


def build_program(taps=(), stop_after=None):
    taps = set(taps)
    nc = bass.Bass("TRN2", target_bir_lowering=False, debug=False)

    xT = nc.dram_tensor("xT", [D, T], F32, kind="ExternalInput").ap()
    x_tm = nc.dram_tensor("x_tm", [T, D], F32, kind="ExternalInput").ap()
    wqT = nc.dram_tensor("wqT", [NFT, NFT, 128, 128], F32, kind="ExternalInput").ap()
    wkT = nc.dram_tensor("wkT", [NFT, NFT, 128, 128], F32, kind="ExternalInput").ap()
    wvT = nc.dram_tensor("wvT", [NFT, NFT, 128, 128], F32, kind="ExternalInput").ap()
    woT = nc.dram_tensor("woT", [D, D], F32, kind="ExternalInput").ap()
    bo_r = nc.dram_tensor("bo_r", [1, D], F32, kind="ExternalInput").ap()
    wg = nc.dram_tensor("wg", [D, NE], F32, kind="ExternalInput").ap()
    w1pm = nc.dram_tensor("w1pm", [NE * 4 * 128, 4096], F32, kind="ExternalInput").ap()
    w3pm = nc.dram_tensor("w3pm", [NE * 4 * 128, 4096], F32, kind="ExternalInput").ap()
    w2pm = nc.dram_tensor("w2pm", [NE * 128, 16384], F32, kind="ExternalInput").ap()
    esel = nc.dram_tensor("esel", [128, 1], U32, kind="ExternalInput").ap()
    eye_bf = nc.dram_tensor("eye_bf", [128, 128], BF16, kind="ExternalInput").ap()
    eye_f = nc.dram_tensor("eye_f", [128, 128], F32, kind="ExternalInput").ap()
    sut_f = nc.dram_tensor("sut_f", [128, 128], F32, kind="ExternalInput").ap()
    onec_f = nc.dram_tensor("onec_f", [128, 1], F32, kind="ExternalInput").ap()
    oner_f = nc.dram_tensor("oner_f", [1, 128], F32, kind="ExternalInput").ap()
    oner_b = nc.dram_tensor("oner_b", [1, 128], BF16, kind="ExternalInput").ap()
    utk_b = nc.dram_tensor("utk_b", [128, 128], BF16, kind="ExternalInput").ap()
    actc_f = nc.dram_tensor("actc_f", [128, 8], F32, kind="ExternalInput").ap()

    out_tm = nc.dram_tensor("out_tm", [T, D], F32, kind="ExternalOutput").ap()
    tap_aps = {}
    def tap(name, shape, dtype=F32):
        if name in taps:
            tap_aps[name] = nc.dram_tensor("tap_" + name, shape, dtype,
                                           kind="ExternalOutput").ap()
        return tap_aps.get(name)

    x2_tap = tap("x2", [T, D])
    sc_tap = tap("scores", [T, NE])
    thr_tap = tap("thr", [1, NE])
    pos_tap = tap("pos", [128, NE * NTT], U32)
    il_tap = tap("ilist", [NE * CAP, 1], U32)
    y2_tap = tap("y2", [T, D])

    x2aug_d = nc.dram_tensor("x2aug_d", [NROWS, AUGW], BF16, kind="Internal").ap()
    y2acc_d = nc.dram_tensor("y2acc_d", [NROWS, D], F32, kind="Internal").ap()
    ilist_d = nc.dram_tensor("ilist_d", [NE * CAP, 1], U32, kind="Internal").ap()
    cc1_in = nc.dram_tensor("cc1_in", [NE, T], F32, kind="Internal").ap()
    cc1_out = nc.dram_tensor("cc1_out", [8 * NE, T], F32, kind="Internal").ap()
    cc2_in = nc.dram_tensor("cc2_in", [1, 1], F32, kind="Internal").ap()
    cc2_out = nc.dram_tensor("cc2_out", [8, 1], F32, kind="Internal").ap()
    ts_d = nc.dram_tensor("ts_d", [1, 128], F32, kind="Internal").ap()
    wob_d = nc.dram_tensor("wob_d", [D, D], BF16, kind="Internal").ap()
    w1b_d = nc.dram_tensor("w1b_d", [NE * 4 * 128, 4096], BF16, kind="Internal").ap()
    w3b_d = nc.dram_tensor("w3b_d", [NE * 4 * 128, 4096], BF16, kind="Internal").ap()

    off_d = nc.dram_tensor("off_d", [1, 128], F32, kind="Internal").ap()

    REGS = {}
    import contextlib
    stack = contextlib.ExitStack()
    stkX2 = contextlib.ExitStack()
    stkCT = contextlib.ExitStack()
    stkA = contextlib.ExitStack()
    stkB = contextlib.ExitStack()
    stkC = contextlib.ExitStack()
    stkD = contextlib.ExitStack()
    stkE = contextlib.ExitStack()

    def mk(stk):
        return (lambda name, shape, dtype: stk.enter_context(nc.sbuf_tensor(name, shape, dtype)))
    sb = mk(stack)
    ps = lambda name, shape, dtype: stack.enter_context(nc.psum_tensor(name, shape, dtype))

    stkPS1 = contextlib.ExitStack()
    with stack:
        ps1 = lambda name, shape, dtype: stkPS1.enter_context(nc.psum_tensor(name, shape, dtype))
        PSF = [ps1(f"psf{i}", [128, 512], F32) for i in range(6)]
        PSB = [ps1(f"psb{i}", [128, 512], BF16) for i in range(2)]
        BK = [f"PSF{i}" for i in range(6)] + ["PSB0", "PSB1"]

        EYEB = sb("eyeb", [128, 128], BF16)
        EYEF = sb("eyef", [128, 128], F32)
        SUT = sb("sut", [128, 128], F32)
        ONEC = sb("onec", [128, 1], F32)
        ONER = sb("oner", [1, 128], F32)
        ONERB = sb("onerb", [1, 128], BF16)
        UTK = sb("utk", [128, 128], BF16)
        ESEL = sb("eselS", [128, 1], U32)
        BO = sb("boS", [1, D], F32)
        WGS = sb("wgs", [128, NE * NFT], F32)
        IDS = sb("ids", [128, NTT], U32)
        ONESQ = sb("onesq", [128, 128], F32)
        ACTC = sb("actc", [128, 8], F32)
        for j, v in enumerate(ACT_CONSTS):
            nc.const_aps.register(F32, v, ACTC[:, j:j + 1])
        INIT5 = sb("init5", [128, CT], U32)
        BIGL = sb("bigl", [128, 1], F32)
        JUNKB = [sb(f"junkb{j}", [128, D], BF16) for j in range(2)]
        JUNKF = sb("junkf", [128, 128], F32)
        ST1 = [sb(f"st1_{j}", [128, 1], F32) for j in range(2)]
        ST2 = [sb(f"st2_{j}", [128, 1], F32) for j in range(2)]
        MU = [sb(f"mu{j}", [128, 1], F32) for j in range(2)]
        VAR = [sb(f"var{j}", [128, 1], F32) for j in range(2)]
        RSTD = [sb(f"rstd{j}", [128, 1], F32) for j in range(2)]
        NMR = [sb(f"nmr{j}", [128, 1], F32) for j in range(2)]
        TM1 = [sb(f"tm1_{j}", [128, 1], F32) for j in range(2)]
        POSW = sb("posw", [128, 128], U32)

        sba2 = mk(stkCT); stkCT.__enter__()
        QTB = [sba2(f"qtb{i}", [128, T], BF16) for i in range(NFT)]
        KTB = [sba2(f"ktb{i}", [128, T], BF16) for i in range(NFT)]
        VTM = [sba2(f"vtm{i}", [128, H * (HD + 1)], BF16) for i in range(NTT)]
        PT = [[sba2(f"pt{l}_{i}", [128, 512], BF16) for i in range(2)] for l in range(2)]
        CTXF = [sba2(f"ctxf{i}", [65, 512], F32) for i in range(2)]
        LROW = [sba2(f"lrow{i}", [1, 512], F32) for i in range(2)]
        RLR = [sba2(f"rlr{i}", [1, 512], F32) for i in range(2)]
        ZROA = sba2("zroa", [128, AUGW], BF16)
        ZROY = sba2("zroy", [128, D], F32)

        sba = mk(stkA); stkA.__enter__()
        XTB = [sba(f"xtb{i}", [128, T], BF16) for i in range(NFT)]
        WSTG = [sba(f"wstg{i}", [128, 1024], BF16) for i in range(2)]
        VTT = sba("vtt", [128, T], BF16)
        CTXT = [stack.enter_context(nc.sbuf_tensor(f"ctxt{i}", [128, T], BF16, side="right"))
                for i in range(NFT)]

        L = Ledger(("t", "v", "s", "g", "qs", "qa", "qg", "cc"))
        E = L.op
        G = L.group

        def ln_block(src_tile, src_keys, out_tile, out_key, par=0):
            st1, st2, mu, var, rstd, nmr, tm1, jb = (ST1[par], ST2[par], MU[par],
                VAR[par], RSTD[par], NMR[par], TM1[par], JUNKB[par])
            k1, k2, kmu, kva, kvr, krl, krs, knm, ktm, kjb = [(n, par) for n in
                ("st1", "st2", "mu", "var_a", "var", "rstd_l", "rstd", "nmr", "tm1", "junkb")]
            E("v", lambda e: e.reduce_sum(st1[:, :], src_tile[:, :], AX.X),
              reads=list(src_keys) + [k1], writes=[k1])
            E("s", lambda e: e.activation(out=jb[:, :], in_=src_tile[:, :],
                                          func=AF.Square, accum_out=st2[:, :]),
              reads=list(src_keys) + [k2, kjb], writes=[k2, kjb])
            E("v", lambda e: e.tensor_scalar_mul(mu[:, :], st1[:, :], 1.0 / D),
              reads=[k1, kmu], writes=[kmu])
            E("v", lambda e: e.tensor_scalar_mul(var[:, :], st2[:, :], 1.0 / D),
              reads=[k2, kva], writes=[kva])
            E("v", lambda e: e.tensor_tensor(tm1[:, :], mu[:, :], mu[:, :], ALU.mult),
              reads=[kmu, ktm], writes=[ktm])
            E("v", lambda e: e.tensor_tensor(var[:, :], var[:, :], tm1[:, :], ALU.subtract),
              reads=[kva, ktm], writes=[kva])
            E("s", lambda e: e.activation(out=rstd[:, :], in_=var[:, :], func=AF.Ln, bias=1e-5),
              reads=[kva, krl], writes=[krl])
            E("s", lambda e: e.activation(out=rstd[:, :], in_=rstd[:, :], func=AF.Exp, scale=-0.5),
              reads=[krl, krs], writes=[krs])
            E("v", lambda e: e.tensor_tensor(nmr[:, :], mu[:, :], rstd[:, :], ALU.mult),
              reads=[kmu, krs, knm], writes=[knm])
            E("v", lambda e: e.tensor_scalar_mul(nmr[:, :], nmr[:, :], -1.0),
              reads=[knm], writes=[knm])
            E("v", lambda e: e.tensor_scalar(out_tile[:, :], src_tile[:, :],
                                             rstd[:, :], nmr[:, :], ALU.mult, ALU.add),
              reads=list(src_keys) + [krs, knm], writes=[out_key])

        for dst, src, key in [
            (EYEB, eye_bf, "eye_bf"), (EYEF, eye_f, "eye_f"),
            (SUT, sut_f, "sut"), (ONEC, onec_f, "onec"),
            (ONER, oner_f, "oner"), (ONERB, oner_b, "onerb"),
            (UTK, utk_b, "utk"), (ESEL, esel, "eselS"), (BO, bo_r, "bo_r"),
            (ACTC, actc_f, "actc"),
        ]:
            E("q_s", lambda e, d=dst, s=src: e.dma_start(out=d[:, :], in_=s[:, :]),
              writes=[key], sem="qs")
        for f in range(NFT):
            E("q_s", lambda e, f=f: e.dma_start(out=WGS[:, f * NE:(f + 1) * NE],
                                                in_=wg[f * 128:(f + 1) * 128, :]),
              writes=[("wgs", f)], sem="qs")
        E("g", lambda e: e.iota(IDS[:, :], pattern=[[128, NTT]], base=0, channel_multiplier=1),
          writes=["ids"])
        E("g", lambda e: e.iota(INIT5[:, :], pattern=[[1, CT]], base=T, channel_multiplier=CT),
          writes=["init5"])
        E("v", lambda e: e.tensor_scalar(BIGL[:, :], IDS[:, 0:1], 1000000.0, None, ALU.add),
          reads=["ids"], writes=["bigl"])
        E("v", lambda e: e.memset(ONESQ[:, :], 1.0), writes=["onesq"])
        for i in range(NTT):
            E("v", lambda e, i=i: e.memset(VTM[i][:, :], 1.0), writes=[("vtm", i)])
        E("v", lambda e: e.memset(ZROA[:, :], 0.0), writes=["zroa"])
        E("v", lambda e: e.memset(ZROY[:, :], 0.0), writes=["zroy"])
        for r in range(CT):
            E("q_s", lambda e, r=r: e.dma_start(out=x2aug_d[T + r * 128: T + (r + 1) * 128, :],
                                                in_=ZROA[:, :]),
              reads=["zroa"], writes=[("x2aug_pad", r)], sem="qs")
            E("q_s", lambda e, r=r: e.dma_start(out=y2acc_d[T + r * 128: T + (r + 1) * 128, :],
                                                in_=ZROY[:, :]),
              reads=["zroy"], writes=[("y2acc_pad", r)], sem="qs")

        L.barrier()

        for i in range(NFT):
            E("q_g", lambda e, i=i: e.dma_start(out=XTB[i][:, :],
                                                in_=xT[i * 128:(i + 1) * 128, :]),
              writes=[("xtb", i)], sem="qg")

        def proj(wsrc, dstf, dkey, post=None):
            for m in range(NFT):
                b = m % 2
                E("q_g", lambda e, m=m, b=b, w=wsrc: e.dma_start(
                    out=WSTG[b][:, :].rearrange("p (k f) -> p k f", f=128),
                    in_=w[m].rearrange("k p f -> p k f")),
                  reads=[("wstg", b)], writes=[("wstg", b)], sem="qg")
                for ch in range(4):
                    pb = (m * 4 + ch) % 2
                    bank = BK[pb]
                    fns = []
                    for k in range(NFT):
                        r = [("wstg", b), ("xtb", k)]
                        if k == 0:
                            r.append(bank)
                        fns.append((
                            (lambda e, b=b, ch=ch, k=k, pb=pb: e.matmul(
                                PSF[pb][:, :], WSTG[b][:, k * 128:(k + 1) * 128],
                                XTB[k][:, ch * 512:(ch + 1) * 512],
                                start=(k == 0), stop=(k == NFT - 1))),
                            r))
                    G("t", fns, writes=[bank])
                    E("v", lambda e, ch=ch, pb=pb, dst=dstf(m): e.tensor_copy(
                        dst[:, ch * 512:(ch + 1) * 512], PSF[pb][:, :]),
                      reads=[bank], writes=[bank, (dkey, m, ch)])
                if post is not None:
                    post(m)

        proj(wqT, lambda m: QTB[m], "q")
        proj(wkT, lambda m: KTB[m], "k")

        def vpost(m):
            for i in range(NTT):
                tb = i % 2
                bank = BK[6 + tb]
                E("t", lambda e, m=m, i=i, tb=tb: e.transpose(
                    PSB[tb][:, 0:128], VTT[:, i * 128:(i + 1) * 128], EYEB[:, :]),
                  reads=[("vt", m, i // 4), "eye_bf", bank], writes=[bank])
                for hh2 in range(2):
                    h = 2 * m + hh2
                    E("v", lambda e, i=i, tb=tb, hh2=hh2, h=h: e.tensor_copy(
                        VTM[i][:, h * (HD + 1): h * (HD + 1) + HD],
                        PSB[tb][:, hh2 * 64: hh2 * 64 + 64]),
                      reads=[bank, ("vtm", i)], writes=[bank, ("vtm", i)])

        proj(wvT, lambda m: VTT, "vt", post=vpost)
        L.barrier()
        stkA.close()

        E("q_g", lambda g: g.dma_start(out=wob_d[:, :], in_=woT[:, :]),
          writes=["wob_d"], sem="qg")
        E("q_g", lambda g: g.dma_start(out=w1b_d[:, :], in_=w1pm[:, :]),
          writes=["w1b"], sem="qg")
        E("q_g", lambda g: g.dma_start(out=w3b_d[:, :], in_=w3pm[:, :]),
          writes=["w3b"], sem="qg")
        if stop_after not in ("qkv",):
            for hp in range(NFT):
                for c in range(4):
                    cbank = [BK[4], BK[5]]
                    nj = 4 * c + 4

                    def emit_scores(j):
                        d = j - 4 * c
                        q0 = c * 512 + max(0, d) * 128
                        w = (c + 1) * 512 - q0
                        for lane in range(2):
                            sp = PSF[2 * lane + j % 2]
                            sbank = BK[2 * lane + j % 2]
                            rows = slice(lane * 64, lane * 64 + 64)
                            E("t", lambda e, hp=hp, j=j, rows=rows, q0=q0, w=w, sp=sp: e.matmul(
                                sp[:, 0:w], KTB[hp][rows, j * 128:(j + 1) * 128],
                                QTB[hp][rows, q0:q0 + w], start=True, stop=True),
                              reads=[sbank], writes=[sbank])

                    emit_scores(0)
                    for j in range(nj):
                        d = j - 4 * c
                        q0 = c * 512 + max(0, d) * 128
                        w = (c + 1) * 512 - q0
                        for lane in range(2):
                            sp = PSF[2 * lane + j % 2]
                            sbank = BK[2 * lane + j % 2]
                            pt = PT[lane][j % 2]
                            E("s", lambda e, w=w, sp=sp, pt=pt: e.activation(
                                out=pt[:, 0:w], in_=sp[:, 0:w], func=AF.Exp,
                                bias=-4.0, scale=0.125),
                              reads=[sbank, ("pt", lane, j % 2)],
                              writes=[sbank, ("pt", lane, j % 2)])
                            if d >= 0:
                                E("v", lambda e, pt=pt: e.tensor_tensor(
                                    pt[:, 0:128], pt[:, 0:128], UTK[:, :], ALU.mult),
                                  reads=[("pt", lane, j % 2), "utk"],
                                  writes=[("pt", lane, j % 2)])
                        if j + 1 < nj:
                            emit_scores(j + 1)
                        for lane in range(2):
                            h = 2 * hp + lane
                            pt = PT[lane][j % 2]
                            cp = PSF[4 + lane]
                            r = [("pt", lane, j % 2)]
                            wr = ()
                            if j == 0:
                                r.append(cbank[lane])
                            if j == nj - 1:
                                wr = [cbank[lane]]
                            E("t", lambda e, j=j, h=h, q0=q0, w=w, c=c, nj=nj, pt=pt, cp=cp: e.matmul(
                                cp[0:65, (q0 - c * 512):(q0 - c * 512) + w],
                                VTM[j][:, h * (HD + 1): (h + 1) * (HD + 1)],
                                pt[:, 0:w], start=(j == 0), stop=(j == nj - 1)),
                              reads=r, writes=wr)
                    for lane in range(2):
                        cf = CTXF[lane]
                        nb = BK[1 + 2 * lane]
                        nbp = PSF[1 + 2 * lane]
                        E("v", lambda e, cf=cf, cp=PSF[4 + lane]: e.tensor_copy(cf[:, :], cp[0:65, :]),
                          reads=[cbank[lane], ("ctxf", lane)], writes=[cbank[lane], ("ctxf", lane)])
                        E("s", lambda e, cf=cf, lr=LROW[lane]: e.activation(
                            out=lr[:, :], in_=cf[64:65, :], func=AF.Ln),
                          reads=[("ctxf", lane)], writes=[("lrow", lane)])
                        E("s", lambda e, lr=LROW[lane], rl=RLR[lane]: e.activation(
                            out=rl[:, :], in_=lr[:, :], func=AF.Exp, scale=-1.0),
                          reads=[("lrow", lane)], writes=[("rlr", lane)])
                        E("t", lambda e, rl=RLR[lane], nbp=nbp: e.matmul(
                            nbp[0:64, :], ONER[0:1, 0:64], rl[0:1, :], start=True, stop=True),
                          reads=[("rlr", lane), "oner", nb], writes=[nb])
                        E("v", lambda e, hp=hp, c=c, lane=lane, cf=cf, nbp=nbp: e.tensor_tensor(
                            CTXT[hp][lane * 64:(lane + 1) * 64, c * 512:(c + 1) * 512],
                            cf[0:64, :], nbp[0:64, :], ALU.mult),
                          reads=[("ctxf", lane), nb],
                          writes=[nb, ("ctxt", hp, c, lane)])
            L.barrier(skip=("qg",))

        if stop_after not in ("qkv", "attn"):
            stkCT.close()
            sbx = mk(stkX2); stkX2.__enter__()
            X2 = [sbx(f"x2_{i}", [128, D], F32) for i in range(NTT)]
            sbcc = mk(stkC); stkC.__enter__()
            TRSW = sbcc("trsw", [128, 1024], F32)
            EXR = sbcc("exr", [128, NE], F32)
            ESUM = sbcc("esum", [128, 1], F32)
            ERCP = sbcc("ercp", [128, 1], F32)
            SCT = sbcc("sct", [NE, T], F32)
            SCALL = sbcc("scall", [128, 128], F32)
            BLO = sbcc("blo", [128, 1], F32)
            BHI = sbcc("bhi", [128, 1], F32)
            BMID = sbcc("bmid", [128, 1], F32)
            BCNT = sbcc("bcnt", [128, 1], F32)
            BC1 = sbcc("bc1", [1, 1], F32)
            BSEL = sbcc("bsel", [128, 1], F32)
            BT1 = sbcc("bt1", [128, 1], F32)
            BT2 = sbcc("bt2", [128, 1], F32)
            TROW = sbcc("trow", [1, NE], F32)
            TRT = sbcc("trt", [1, 128], F32)
            SCW = sbcc("scw", [128, 128], F32)
            TBCW = sbcc("tbcw", [128, 128], F32)
            MSKW = sbcc("mskw", [128, 128], F32)
            CUMW = sbcc("cumw", [128, 128], F32)
            MSBW = sbcc("msbw", [128, 128], BF16)
            TSR = sbcc("tsr", [1, 128], F32)
            TS16 = sbcc("ts16", [16, NE], F32)
            OFF16 = sbcc("off16", [16, NE], F32)
            OFFR = sbcc("offr", [1, 128], F32)
            sbb = mk(stkB); stkB.__enter__()
            WOB = [sbb(f"wob{i}", [128, D], BF16) for i in range(NFT)]
            XTMS = [sbb(f"xtms{i}", [128, D], F32) for i in range(2)]
            YT = [sbb(f"yt{i}", [128, D], F32) for i in range(2)]
            BOBC = sbb("bobc", [128, D], F32)
            for hh in range(2):
                sl = slice(hh * 512, hh * 512 + 512)
                E("t", lambda e, sl=sl: e.matmul(PSF[0][:, :], ONER[0:1, :], BO[0:1, sl],
                                                 start=True, stop=True),
                  reads=["oner", "bo_r", "PSF0"], writes=["PSF0"])
                E("v", lambda e, sl=sl, hh=hh: e.tensor_copy(BOBC[:, sl], PSF[0][:, :]),
                  reads=["PSF0"], writes=["PSF0", ("bobc", hh)])
            for k in range(NFT):
                E("q_s", lambda e, k=k: e.dma_start(out=WOB[k][:, :],
                                                    in_=wob_d[k * 128:(k + 1) * 128, :]),
                  reads=["wob_d"], writes=[("wob", k)], sem="qs")
            def p3_partA(i):
                b = i % 2
                E("q_a", lambda e, i=i, b=b: e.dma_start(out=XTMS[b][:, :],
                                                         in_=x_tm[i * 128:(i + 1) * 128, :]),
                  reads=[("xtms", b)], writes=[("xtms", b)], sem="qa")
                for hh in range(2):
                    pb = (2 * i + hh) % 2
                    bank = BK[pb]
                    fns = []
                    for k in range(NFT):
                        r = [("wob", k)]
                        if k == 0:
                            r.append(bank)
                        fns.append((
                            (lambda e, i=i, hh=hh, k=k, pb=pb: e.matmul(
                                PSF[pb][:, :], CTXT[k][:, i * 128:(i + 1) * 128],
                                WOB[k][:, hh * 512:(hh + 1) * 512],
                                start=(k == 0), stop=(k == NFT - 1))),
                            r))
                    G("t", fns, writes=[bank])
                    sl = slice(hh * 512, hh * 512 + 512)
                    E("v", lambda e, b=b, pb=pb, sl=sl: e.tensor_tensor(
                        YT[b][:, sl], PSF[pb][:, :], BOBC[:, sl], ALU.add),
                      reads=[bank, ("bobc", hh), ("yt", b, hh)], writes=[bank, ("yt", b, hh)])
                    E("v", lambda e, b=b, sl=sl: e.tensor_tensor(
                        YT[b][:, sl], YT[b][:, sl], XTMS[b][:, sl], ALU.add),
                      reads=[("yt", b, hh), ("xtms", b)], writes=[("yt", b, hh)])
                ln_block(YT[b], [("yt", b, 0), ("yt", b, 1)], X2[i], ("x2", i), par=b)
                if x2_tap is not None:
                    E("q_a", lambda e, i=i: e.dma_start(out=x2_tap[i * 128:(i + 1) * 128, :],
                                                        in_=X2[i][:, :]),
                      reads=[("x2", i)], writes=[("x2tap", i)], sem="qa")
            def p3_partB(i):
                pg = 4 + (i % 2)
                gbank = BK[pg]
                for f in range(NFT):
                    tb = f % 2
                    tbank = BK[2 + tb]
                    E("t", lambda e, i=i, f=f, tb=tb: e.transpose(
                        PSF[2 + tb][:, 0:128], X2[i][:, f * 128:(f + 1) * 128], EYEF[:, :]),
                      reads=[("x2", i), "eye_f", tbank], writes=[tbank])
                    E("s", lambda e, f=f, tb=tb: e.activation(
                        out=TRSW[:, f * 128:(f + 1) * 128], in_=PSF[2 + tb][:, 0:128],
                        func=AF.Copy, bias=0.0),
                      reads=[tbank, ("trsw", f)], writes=[tbank, ("trsw", f)])
                fns = []
                for f in range(NFT):
                    r = [("trsw", f), ("wgs", f)]
                    if f == 0:
                        r.append(gbank)
                    fns.append((
                        (lambda e, f=f, pg=pg: e.matmul(
                            PSF[pg][:, 0:NE], TRSW[:, f * 128:(f + 1) * 128],
                            WGS[:, f * NE:(f + 1) * NE],
                            start=(f == 0), stop=(f == NFT - 1))),
                        r))
                G("t", fns, writes=[gbank])
                E("s", lambda e, pg=pg: e.activation(out=EXR[:, :], in_=PSF[pg][:, 0:NE],
                                                     func=AF.Exp, accum_out=ESUM[:, :]),
                  reads=[gbank, "exr", "esum"], writes=[gbank, "exr", "esum"])
                E("v", lambda e: e.reciprocal(ERCP[:, :], ESUM[:, :]),
                  reads=["esum"], writes=["ercp"])
                E("v", lambda e, i=i: e.tensor_scalar(SCW[:, i * NE:(i + 1) * NE], EXR[:, :],
                                                      ERCP[:, :], None, ALU.mult),
                  reads=["exr", "ercp", ("scw", i)], writes=[("scw", i)])
                if sc_tap is not None:
                    E("q_a", lambda e, i=i: e.dma_start(out=sc_tap[i * 128:(i + 1) * 128, :],
                                                        in_=SCW[:, i * NE:(i + 1) * NE]),
                      reads=[("scw", i)], writes=[("sctap", i)], sem="qa")
                tb2 = i % 2
                t2bank = BK[2 + tb2]
                E("t", lambda e, i=i, tb2=tb2: e.transpose(
                    PSF[2 + tb2][0:NE, 0:128], SCW[:, i * NE:(i + 1) * NE], EYEF[:, :]),
                  reads=[("scw", i), "eye_f", t2bank], writes=[t2bank])
                E("s", lambda e, i=i, tb2=tb2: e.activation(
                    out=SCT[:, i * 128:(i + 1) * 128], in_=PSF[2 + tb2][0:NE, 0:128],
                    func=AF.Copy, bias=0.0),
                  reads=[t2bank, ("sct", i)], writes=[t2bank, ("sct", i)])
            for i in range(NTT):
                p3_partA(i)
                if i > 0:
                    p3_partB(i - 1)
            p3_partB(NTT - 1)
            L.barrier(skip=("qg",))

        if stop_after not in ("qkv", "attn", "ln1", "gate"):
            stkB.close()
            E("q_s", lambda e: e.dma_start(out=cc1_in[:, :], in_=SCT[:, :]),
              reads=[("sct", i) for i in range(NTT)], writes=["cc1_in"], sem="qs")
            E("g", lambda e: e.collective_compute(
                "AllGather", ALU.bypass, replica_groups=[list(range(8))],
                ins=[cc1_in[:, :]], outs=[cc1_out[:, :]]),
              reads=["cc1_in"], writes=["cc1_out"], sem="cc")
            E("q_g", lambda e: e.indirect_dma_start(
                out=SCALL[:, :], out_offset=None,
                in_=cc1_out.rearrange("r (s f) -> (r s) f", f=128),
                in_offset=IOff(ESEL[:, 0:1], 0)),
              reads=["cc1_out", "eselS"], writes=["scall"], sem="qg")
            E("v", lambda e: e.memset(BLO[:, :], 0.0), writes=["blo"])
            E("v", lambda e: e.memset(BHI[:, :], 0.5), writes=["bhid"])
            for it in range(SEARCH_ITERS):
                E("v", lambda e: e.tensor_tensor(BMID[:, :], BLO[:, :], BHI[:, :], ALU.add),
                  reads=["blo", "bhid", "bmid"], writes=["bmid"])
                E("v", lambda e: e.tensor_scalar(JUNKF[:, :], SCALL[:, :], BMID[:, :], None,
                                                 ALU.is_ge, ALU.add, accum_out=BCNT[:, :]),
                  reads=["scall", "bmid", "bcnt", "junkf"], writes=["bcnt", "junkf"])
                E("t", lambda e: e.matmul(PSF[2][:, 0:1], ONESQ[:, :], BCNT[:, :],
                                          start=True, stop=True),
                  reads=["onesq", "bcnt", "PSF2"], writes=["PSF2"])
                E("v", lambda e: e.tensor_scalar(BSEL[:, :], PSF[2][:, 0:1], float(CAPG), None, ALU.is_ge),
                  reads=["PSF2", "bsel"], writes=["PSF2", "bsel"])
                E("v", lambda e: e.tensor_tensor(BT1[:, :], BSEL[:, :], BHI[:, :], ALU.mult),
                  reads=["bsel", "bhid", "bt1"], writes=["bt1"])
                E("v", lambda e: e.tensor_tensor(BLO[:, :], BLO[:, :], BT1[:, :], ALU.add),
                  reads=["blo", "bt1"], writes=["blo"])
                E("v", lambda e: e.tensor_scalar(BHI[:, :], BHI[:, :], 0.5, None, ALU.mult),
                  reads=["bhid"], writes=["bhid"])
            E("q_s", lambda e: e.dma_start(out=cc2_in[:, :], in_=BLO[0:1, 0:1]),
              reads=["blo"], writes=["cc2_in"], sem="qs")
            E("g", lambda e: e.collective_compute(
                "AllGather", ALU.bypass, replica_groups=[list(range(8))],
                ins=[cc2_in[:, :]], outs=[cc2_out[:, :]]),
              reads=["cc2_in"], writes=["cc2_out"], sem="cc")
            E("q_s", lambda e: e.dma_start(out=TROW[:, :],
                                           in_=cc2_out.rearrange("e one -> one e")),
              reads=["cc2_out"], writes=["trow"], sem="qs")
            if thr_tap is not None:
                E("q_s", lambda e: e.dma_start(out=thr_tap[:, :], in_=TROW[:, :]),
                  reads=["trow"], writes=["thrtap"], sem="qs")
            for i in range(NTT):
                E("v", lambda e, i=i: e.tensor_copy(TRT[0:1, i * NE:(i + 1) * NE], TROW[0:1, :]),
                  reads=["trow", ("trt", i)], writes=[("trt", i)])
            trt_keys = [("trt", i) for i in range(NTT)]
            scw_keys = [("scw", i) for i in range(NTT)]
            E("t", lambda e: e.matmul(PSF[2][:, 0:128], ONER[0:1, :], TRT[0:1, :],
                                      start=True, stop=True),
              reads=trt_keys + ["oner", "PSF2"], writes=["PSF2"])
            E("v", lambda e: e.tensor_copy(TBCW[:, :], PSF[2][:, 0:128]),
              reads=["PSF2", "tbcw"], writes=["PSF2", "tbcw"])
            E("v", lambda e: e.tensor_tensor(MSKW[:, :], SCW[:, :], TBCW[:, :], ALU.is_ge),
              reads=scw_keys + ["tbcw", "mskw"], writes=["mskw"])
            E("v", lambda e: e.tensor_tensor(MSBW[:, :], SCW[:, :], MSKW[:, :], ALU.mult),
              reads=scw_keys + ["mskw", "msbw"], writes=["msbw"])
            for i in range(NTT):
                E("q_g", lambda e, i=i: e.dma_start(out=x2aug_d[i * 128:(i + 1) * 128, 0:D],
                                                    in_=X2[i][:, :]),
                  reads=[("x2", i)], writes=[("x2aug", i)], sem="qg")
                E("q_g", lambda e, i=i: e.dma_start(out=x2aug_d[i * 128:(i + 1) * 128, D:AUGW],
                                                    in_=MSBW[:, i * NE:(i + 1) * NE]),
                  reads=["msbw"], writes=[("x2aug_s", i)], sem="qg")
                E("q_s", lambda e, i=i: e.dma_start(out=y2acc_d[i * 128:(i + 1) * 128, :],
                                                    in_=X2[i][:, :]),
                  reads=[("x2", i)], writes=[("y2acc_init", i)], sem="qs")
                pcb = 2 + (i % 2)
                E("t", lambda e, i=i, pcb=pcb: e.matmul(PSF[pcb][:, 0:NE], SUT[:, :],
                                                        MSKW[:, i * NE:(i + 1) * NE],
                                                        start=True, stop=True),
                  reads=["sut", "mskw", BK[pcb]], writes=[BK[pcb]])
                E("v", lambda e, i=i, pcb=pcb: e.tensor_copy(CUMW[:, i * NE:(i + 1) * NE],
                                                             PSF[pcb][:, 0:NE]),
                  reads=[BK[pcb], ("cumw", i)], writes=[BK[pcb], ("cumw", i)])
                tsb = 4 + (i % 2)
                E("t", lambda e, i=i, tsb=tsb: e.matmul(PSF[tsb][0:1, 0:NE], ONEC[:, :],
                                                        MSKW[:, i * NE:(i + 1) * NE],
                                                        start=True, stop=True),
                  reads=["onec", "mskw", BK[tsb]], writes=[BK[tsb]])
                E("v", lambda e, i=i, tsb=tsb: e.tensor_copy(TSR[:, i * NE:(i + 1) * NE],
                                                             PSF[tsb][0:1, 0:NE]),
                  reads=[BK[tsb], ("tsr", i)], writes=[BK[tsb], ("tsr", i)])
            cum_keys = [("cumw", i) for i in range(NTT)]
            E("q_a", lambda e: e.dma_start(out=ts_d[:, :], in_=TSR[:, :]),
              reads=[("tsr", i) for i in range(NTT)], writes=["ts_d"], sem="qa")
            E("q_a", lambda e: e.dma_start(out=TS16[:, :],
                                           in_=ts_d.rearrange("one (i e) -> (one i) e", e=NE)),
              reads=["ts_d"], writes=["ts16"], sem="qa")
            E("t", lambda e: e.matmul(PSF[2][0:16, 0:NE], SUT[0:16, 0:16], TS16[:, :],
                                      start=True, stop=True),
              reads=["ts16", "sut", "PSF2"], writes=["PSF2"])
            E("v", lambda e: e.tensor_copy(OFF16[:, :], PSF[2][0:16, 0:NE]),
              reads=["PSF2", "off16"], writes=["PSF2", "off16"])
            E("q_a", lambda e: e.dma_start(out=off_d.rearrange("one (i e) -> (one i) e", e=NE),
                                           in_=OFF16[:, :]),
              reads=["off16"], writes=["off_d"], sem="qa")
            E("q_a", lambda e: e.dma_start(out=OFFR[:, :], in_=off_d[:, :]),
              reads=["off_d"], writes=["offr"], sem="qa")
            E("t", lambda e: e.matmul(PSF[3][:, 0:128], ONER[0:1, :], OFFR[0:1, :],
                                      start=True, stop=True),
              reads=["oner", "offr", "PSF3"], writes=["PSF3"])
            E("v", lambda e: e.tensor_tensor(CUMW[:, :], CUMW[:, :], PSF[3][:, 0:128], ALU.add),
              reads=cum_keys + ["PSF3"], writes=["PSF3"] + cum_keys)
            E("v", lambda e: e.tensor_scalar(CUMW[:, :], CUMW[:, :], BIGL[:, :], None, ALU.subtract),
              reads=cum_keys + ["bigl"], writes=cum_keys)
            E("v", lambda e: e.tensor_tensor(CUMW[:, :], CUMW[:, :], MSKW[:, :], ALU.mult),
              reads=cum_keys + ["mskw"], writes=cum_keys)
            E("v", lambda e: e.tensor_scalar(CUMW[:, :], CUMW[:, :], BIGL[:, :], None, ALU.add),
              reads=cum_keys + ["bigl"], writes=cum_keys)
            E("v", lambda e: e.tensor_copy(POSW[:, :], CUMW[:, :]),
              reads=cum_keys + ["posw"], writes=["posw"])
            if pos_tap is not None:
                E("q_a", lambda e: e.dma_start(out=pos_tap[:, :], in_=POSW[:, :]),
                  reads=["posw"], writes=["postap"], sem="qa")
            L.barrier(skip=("qg",))

        if stop_after not in ("qkv", "attn", "ln1", "gate", "route"):
            stkC.close()
            stkX2.close()
            stkPS1.close()
            FPS = [ps(f"fps{i}", [128, 512], F32) for i in range(4)]
            FPB = [ps(f"fpb{i}", [128, 512], BF16) for i in range(4)]
            sbd = mk(stkD); stkD.__enter__()
            ILSB = [sbd(f"ilsb{i}", [128, CT], U32) for i in range(4)]
            RTM = [sbd(f"rtm{i}", [128, CT * AUGW], BF16) for i in range(2)]
            SCOL = sbd("scol", [128, 1], F32)
            RT = [sbd(f"rt{i}", [128, NFT * CAP], BF16) for i in range(2)]
            W1C = [sbd(f"w1c{i}", [128, 4, 1024], BF16) for i in range(2)]
            W3C = [sbd(f"w3c{i}", [128, 4, 1024], BF16) for i in range(2)]
            W2A = sbd("w2a", [128, NHT, D], BF16)
            SIL = [sbd(f"sil{i}", [128, CAP], BF16) for i in range(2)]
            HT = sbd("ht", [128, NHT * CAP], BF16)
            OTM = [sbd("otm0", [128, CT * D], F32)] * 2

            x2aug_keys = ([("x2aug", i) for i in range(NTT)] +
                          [("x2aug_s", i) for i in range(NTT)] +
                          [("x2aug_pad", r) for r in range(CT)])
            y2init_keys = ([("y2acc_init", i) for i in range(NTT)] +
                           [("y2acc_pad", r) for r in range(CT)])

            def ilist_emit(e_i):
                ilrows = slice(e_i * CAP, (e_i + 1) * CAP)
                E("q_g", lambda e, ilrows=ilrows: e.dma_start(
                    out=ilist_d[ilrows, :].rearrange("(p c) one -> p (c one)", c=CT),
                    in_=INIT5[:, :]),
                  reads=["init5", ("il", e_i)], writes=[("il", e_i)], sem="qg")
                for i in range(NTT):
                    E("q_g", lambda e, e_i=e_i, i=i: e.indirect_dma_start(
                        out=ilist_d[:, :],
                        out_offset=IOff(POSW[:, NE * i + e_i: NE * i + e_i + 1], 0),
                        in_=IDS[:, i:i + 1], in_offset=None,
                        element_offset=e_i * CAP,
                        bounds_check=REGS["bc"], oob_is_err=False),
                      reads=["ids", "posw", ("il", e_i)], writes=[("ilsc", e_i, i)], sem="qg")

            def fetch_emit(e_i):
                eb = e_i % 2
                ib = e_i % 4
                ilrows = slice(e_i * CAP, (e_i + 1) * CAP)
                E("q_g", lambda e, ib=ib, ilrows=ilrows: e.dma_start(
                    out=ILSB[ib][:, :],
                    in_=ilist_d[ilrows, :].rearrange("(p c) one -> p (c one)", c=CT)),
                  reads=[("il", e_i), ("ilsb", ib)] + [("ilsc", e_i, i) for i in range(NTT)],
                  writes=[("ilsb", ib)], sem="qg")
                if il_tap is not None:
                    E("q_g", lambda e, ilrows=ilrows: e.dma_start(out=il_tap[ilrows, :],
                                                                  in_=ilist_d[ilrows, :]),
                      reads=[("il", e_i)], writes=[("iltap", e_i)], sem="qg")
                for c in range(CT):
                    E("q_g", lambda e, eb=eb, ib=ib, c=c: e.indirect_dma_start(
                        out=RTM[eb][:, c * AUGW:(c + 1) * AUGW], out_offset=None,
                        in_=x2aug_d[:, :], in_offset=IOff(ILSB[ib][:, c:c + 1], 0)),
                      reads=[("ilsb", ib), ("rtm", eb, c)] +
                            (x2aug_keys if e_i == 0 else []),
                      writes=[("rtm", eb, c)], sem="qg")

            def trans_thunks(e_i):
                eb = e_i % 2
                th = []
                for c in range(CT):
                    def t0(eb=eb, c=c, e_i=e_i):
                        E("v", lambda e, eb=eb, c=c, e_i=e_i: e.tensor_copy(
                            SCOL[:, :], RTM[eb][:, c * AUGW + D + e_i: c * AUGW + D + e_i + 1]),
                          reads=[("rtm", eb, c), "scol"], writes=["scol"])
                        E("v", lambda e, eb=eb, c=c: e.tensor_scalar(
                            RTM[eb][:, c * AUGW: c * AUGW + D],
                            RTM[eb][:, c * AUGW: c * AUGW + D],
                            SCOL[:, :], None, ALU.mult),
                          reads=[("rtm", eb, c), "scol"], writes=[("rtm", eb, c)])
                    th.append(t0)
                n = 0
                for c in range(CT):
                    for fb in range(NFT):
                        def t2(eb=eb, c=c, fb=fb, n=n):
                            bk = f"FPB{n % 4}"
                            pb = FPB[n % 4]
                            E("t", lambda e, eb=eb, c=c, fb=fb, pb=pb: e.transpose(
                                pb[:, 0:128],
                                RTM[eb][:, c * AUGW + fb * 128: c * AUGW + (fb + 1) * 128],
                                EYEB[:, :]),
                              reads=[("rtm", eb, c), "eye_bf", bk], writes=[bk])
                            E("v", lambda e, eb=eb, c=c, fb=fb, pb=pb: e.tensor_copy(
                                RT[eb][:, fb * CAP + c * 128: fb * CAP + (c + 1) * 128],
                                pb[:, 0:128]),
                              reads=[bk, ("rt", eb, fb)],
                              writes=[bk, ("rt", eb, fb)])
                        th.append(t2)
                        n += 1
                return th

            def wchunk_emit(e_i, q):
                r = q % 2
                rows = slice((e_i * 4 + q) * 128, (e_i * 4 + q + 1) * 128)
                xr = x2aug_keys if (e_i == 0 and q == 0) else []
                E("q_s", lambda e, rows=rows, r=r: e.dma_start(
                    out=W1C[r][:, :, :],
                    in_=w1b_d[rows, :].rearrange("p (h f) -> p h f", f=1024)),
                  reads=[("w1c", r), "w1b"] + xr, writes=[("w1c", r)], sem="qs")
                E("q_s", lambda e, rows=rows, r=r: e.dma_start(
                    out=W3C[r][:, :, :],
                    in_=w3b_d[rows, :].rearrange("p (h f) -> p h f", f=1024)),
                  reads=[("w3c", r), "w3b"] + xr, writes=[("w3c", r)], sem="qs")

            def w2_emit(e_i):
                xr = x2aug_keys if e_i == 0 else []
                E("q_g", lambda e, e_i=e_i: e.dma_start(
                    out=W2A[:, :, :],
                    in_=w2pm[e_i * 128:(e_i + 1) * 128, :].rearrange("p (k f) -> p k f", f=1024)),
                  reads=["w2a"] + xr, writes=["w2a"], sem="qg")

            def compute_emit(e_i, hooks=(), pending=None):
                eb = e_i % 2
                for ht_i in range(NHT):
                    q, hh = divmod(ht_i, 4)
                    r = (e_i * 4 + q) % 2
                    if ht_i == 0:
                        w2_emit(e_i)
                    if hh == 0 and q + 1 <= 3:
                        wchunk_emit(e_i, q + 1)
                    if ht_i == 12 and e_i + 1 < NE:
                        wchunk_emit(e_i + 1, 0)
                    for hat, hfn in hooks:
                        if ht_i == hat:
                            hfn()
                    if ht_i == 5 and e_i + 2 < NE:
                        ilist_emit(e_i + 2)
                    for part in range(2):
                        wsb = W1C[r] if part == 0 else W3C[r]
                        wkey = ("w1c", r) if part == 0 else ("w3c", r)
                        for seg, (n0, n1) in enumerate([(0, 512), (512, CAP)]):
                            pi = part * 2 + seg
                            pp = FPS[pi]
                            bank = f"FPS{pi}"
                            fns = []
                            for k in range(NFT):
                                rd = [wkey, ("rt", eb, k)]
                                if k == 0:
                                    rd.append(bank)
                                fns.append((
                                    (lambda e, wsb=wsb, hh=hh, k=k, n0=n0, n1=n1, pp=pp, eb=eb: e.matmul(
                                        pp[:, 0:n1 - n0],
                                        wsb[:, hh, k * 128:(k + 1) * 128],
                                        RT[eb][:, k * CAP + n0: k * CAP + n1],
                                        start=(k == 0), stop=(k == NFT - 1))),
                                    rd))
                            G("t", fns, writes=[bank])
                    sb_i = ht_i % 2
                    for seg, (n0, n1) in enumerate([(0, 512), (512, CAP)]):
                        p1k, p3k = f"FPS{seg}", f"FPS{2 + seg}"
                        p1p, p3p = FPS[seg], FPS[2 + seg]
                        E("s", lambda e, sb_i=sb_i, n0=n0, n1=n1, p1p=p1p: e.activation(
                            out=SIL[sb_i][:, n0:n1], in_=p1p[:, 0:n1 - n0], func=AF.Silu),
                          reads=[p1k, ("sil", sb_i, seg)], writes=[p1k, ("sil", sb_i, seg)])
                        E("v", lambda e, sb_i=sb_i, ht_i=ht_i, n0=n0, n1=n1, p3p=p3p: e.tensor_tensor(
                            HT[:, ht_i * CAP + n0: ht_i * CAP + n1],
                            p3p[:, 0:n1 - n0], SIL[sb_i][:, n0:n1], ALU.mult),
                          reads=[("sil", sb_i, seg), p3k, ("ht", ht_i)],
                          writes=[p3k, ("ht", ht_i)])
                    if pending is not None and ht_i >= 4:
                        slot = ht_i - 4
                        lo = (len(pending) * slot) // 11
                        hi = (len(pending) * (slot + 1)) // 11
                        for th in pending[lo:hi]:
                            th()
                for c in range(CT):
                    for seg, (n0, n1) in enumerate([(0, 512), (512, 1024)]):
                        pp = FPS[seg]
                        bank = f"FPS{seg}"
                        fns = []
                        for k in range(NHT):
                            rd = [("ht", k), "w2a"]
                            if k == 0:
                                rd.append(bank)
                            fns.append((
                                (lambda e, c=c, k=k, n0=n0, n1=n1, pp=pp: e.matmul(
                                    pp[:, 0:n1 - n0],
                                    HT[:, k * CAP + c * 128: k * CAP + (c + 1) * 128],
                                    W2A[:, k, n0:n1],
                                    start=(k == 0), stop=(k == NHT - 1))),
                                rd))
                        G("t", fns, writes=[bank])
                        E("v", lambda e, eb=eb, c=c, n0=n0, n1=n1, pp=pp: e.tensor_copy(
                            OTM[eb][:, c * D + n0: c * D + n1], pp[:, 0:n1 - n0]),
                          reads=[bank, ("otm", c)], writes=[bank, ("otm", c)])

            def scatter_emit(e_i):
                eb = e_i % 2
                ib = e_i % 4
                prev = ([("y2w", e_i - 1, c) for c in range(CT)] if e_i > 0
                        else y2init_keys)
                for c in range(CT):
                    E("q_g", lambda e, eb=eb, ib=ib, c=c: e.indirect_dma_start(
                        out=y2acc_d[:, :], out_offset=IOff(ILSB[ib][:, c:c + 1], 0),
                        in_=OTM[eb][:, c * D:(c + 1) * D], in_offset=None,
                        compute_op=ALU.add, oob_is_err=True),
                      reads=[("otm", c), ("ilsb", ib)] + prev,
                      writes=[("y2w", e_i, c)], sem="qg")

            ilist_emit(0)
            fetch_emit(0)
            ilist_emit(1)
            wchunk_emit(0, 0)
            for th in trans_thunks(0):
                th()
            for e_i in range(NE):
                hooks = []
                if e_i + 1 < NE:
                    hooks.append((0, lambda e_i=e_i: fetch_emit(e_i + 1)))
                if e_i > 0:
                    hooks.append((2, lambda e_i=e_i: scatter_emit(e_i - 1)))
                pending = trans_thunks(e_i + 1) if e_i + 1 < NE else None
                compute_emit(e_i, hooks=hooks, pending=pending)
            scatter_emit(NE - 1)
            L.barrier(skip=("qg",))

        if stop_after is None:
            stkD.close()
            sbe = mk(stkE); stkE.__enter__()
            Y2S = [sbe(f"y2s{i}", [128, D], F32) for i in range(2)]
            OUTS = [sbe(f"outs{i}", [128, D], F32) for i in range(2)]
            for i in range(NTT):
                b = i % 2
                E("q_s", lambda e, i=i, b=b: e.dma_start(out=Y2S[b][:, :],
                                                         in_=y2acc_d[i * 128:(i + 1) * 128, :]),
                  reads=[("y2w", NE - 1, c) for c in range(CT)] + [("y2s", b)],
                  writes=[("y2s", b)], sem="qs")
                if y2_tap is not None:
                    E("q_a", lambda e, i=i, b=b: e.dma_start(out=y2_tap[i * 128:(i + 1) * 128, :],
                                                             in_=Y2S[b][:, :]),
                      reads=[("y2s", b)], writes=[("y2tap", i)], sem="qa")
                ln_block(Y2S[b], [("y2s", b)], OUTS[b], ("outs", b), par=b)
                E("q_a", lambda e, i=i, b=b: e.dma_start(out=out_tm[i * 128:(i + 1) * 128, :],
                                                         in_=OUTS[b][:, :]),
                  reads=[("outs", b)], writes=[("out", i), ("y2s", b), ("outs", b)], sem="qa")
        L.barrier()
        for _stk in (stkE, stkD, stkC, stkB, stkX2, stkA, stkCT):
            _stk.close()

        sem_stack = contextlib.ExitStack()
        with sem_stack, nc.Block() as block:
            semmap = {n: sem_stack.enter_context(nc.semaphore("sem_" + n))
                      for n in L.semnames}

            block.tensor(_make_merged_runner(L, ("t",), semmap))
            block.vector(_make_merged_runner(L, ("v",), semmap))
            block.scalar(_make_merged_runner(L, ("s", "q_a"), semmap))
            def _g_pre(eng):
                REGS["bc"] = eng.alloc_register("bc_reg")
                eng.reg_mov(REGS["bc"], CAP - 1)
            block.gpsimd(_make_merged_runner(L, ("g", "q_g"), semmap, pre=_g_pre))
            block.sync(_make_merged_runner(L, ("q_s",), semmap))

    return nc, tap_aps


def _make_merged_runner(L, names, semmap, pre=None):
    def run(eng):
        if pre is not None:
            pre(eng)
        entries = []
        for n in names:
            entries.extend(L.progs[n])
        entries.sort(key=lambda t: t[-1])
        for entry in entries:
            if entry[0] == "wait":
                _, semname, val, _ = entry
                eng.wait_ge(semmap[semname], val)
            else:
                _, fn, semname, step, _ = entry
                inst = fn(eng)
                if semname is not None:
                    if step > 1:
                        inst.then_inc(semmap[semname], step)
                    else:
                        inst.then_inc(semmap[semname])
    return run


def host_inputs(inputs):
    x = np.asarray(inputs["L0_attn_x"])
    Wq = np.asarray(inputs["L0_attn_Wq"]); Wk = np.asarray(inputs["L0_attn_Wk"])
    Wv = np.asarray(inputs["L0_attn_Wv"]); Wo = np.asarray(inputs["L0_attn_Wo"])
    bo = np.asarray(inputs["L0_attn_bo"])
    wg_ = np.asarray(inputs["L0_ffn_wg"])
    w1_ = np.asarray(inputs["L0_ffn_w1"]); w3_ = np.asarray(inputs["L0_ffn_w3"])
    w2_ = np.asarray(inputs["L0_ffn_w2"])

    def tile_T(w):
        wt = np.ascontiguousarray(w.T)
        return np.ascontiguousarray(
            wt.reshape(NFT, 128, NFT, 128).transpose(2, 0, 1, 3))

    wqT = tile_T(Wq); wkT = tile_T(Wk); wvT = tile_T(Wv)
    woT = np.ascontiguousarray(Wo.T)
    def pmaj(w):
        a = w.reshape(NE, NFT, 128, NHT, 128).transpose(0, 2, 3, 1, 4)
        a = a.reshape(NE, 128, 4, 4, NFT, 128).transpose(0, 2, 1, 3, 4, 5)
        return np.ascontiguousarray(a.reshape(NE * 4 * 128, 4096))

    w1pm = pmaj(w1_); w3pm = pmaj(w3_)
    w2pm = np.ascontiguousarray(
        w2_.reshape(NE, NHT, 128, D).transpose(0, 2, 1, 3).reshape(NE * 128, NHT * D))

    eye_bf = np.eye(128, dtype=ml_dtypes.bfloat16)
    eye_f = np.eye(128, dtype=np.float32)
    sut_f = np.triu(np.ones((128, 128), dtype=np.float32), k=1)
    onec_f = np.ones((128, 1), dtype=np.float32)
    oner_f = np.ones((1, 128), dtype=np.float32)
    oner_b = np.ones((1, 128), dtype=ml_dtypes.bfloat16)
    utk_b = (np.arange(128)[:, None] <= np.arange(128)[None, :]).astype(ml_dtypes.bfloat16)

    in_maps = []
    for c in range(8):
        p = np.arange(128)
        esel_c = (((p // 16) * 8 + c) * 16 + (p % 16)).astype(np.uint32).reshape(128, 1)
        in_maps.append({
            "xT": np.ascontiguousarray(x[c].T).astype(np.float32, copy=False),
            "x_tm": np.ascontiguousarray(x[c]).astype(np.float32, copy=False),
            "wqT": wqT, "wkT": wkT, "wvT": wvT, "woT": woT,
            "bo_r": np.ascontiguousarray(bo.reshape(1, D)).astype(np.float32, copy=False),
            "wg": np.ascontiguousarray(wg_).astype(np.float32, copy=False),
            "w1pm": w1pm, "w3pm": w3pm, "w2pm": w2pm,
            "esel": esel_c,
            "actc_f": np.tile(np.array([list(ACT_CONSTS) + [0.0] * (8 - len(ACT_CONSTS))],
                               dtype=np.float32), (128, 1)),
            "eye_bf": eye_bf, "eye_f": eye_f, "sut_f": sut_f,
            "onec_f": onec_f, "oner_f": oner_f, "oner_b": oner_b,
            "utk_b": utk_b,
        })
    return in_maps


def kernel(**inputs):
    in_maps = host_inputs(inputs)
    nc, _ = build_program()
    res = run_bass_kernel_spmd(nc, in_maps, core_ids=list(range(8)))
    out = np.stack([res.results[c]["out_tm"] for c in range(8)], axis=0)
    return out.astype(np.float32)
